# version 01
# baseline (speedup 1.0000x reference)
import math
import os
from contextlib import ExitStack
_NTCB = int(os.environ.get("K_NTCB", "16"))
_P1MODE = os.environ.get("K_P1MODE", "full")

import numpy as np
import concourse.bass as bass
import concourse.mybir as mybir
from concourse.bass import IndirectOffsetOnAxis
from concourse.bass_utils import run_bass_kernel_spmd

F32 = mybir.dt.float32
BF16 = mybir.dt.bfloat16
U32 = mybir.dt.uint32
AF = mybir.ActivationFunctionType
ALU = mybir.AluOpType
AX = mybir.AxisListType
ds = bass.ds

B_, S, DM = 2, 4096, 4096
H, HKV, HD = 32, 8, 128
HPC, KVPC = 8, 2
TQ = 1024
EPS_QK = 1e-6
EPS_LN = 1e-5
RMS_EPS = 1.1920929e-07
NEG = -1e30
NT = S // 128
NSUP = S // 512


def _alibi_slopes(n_heads):
    start = 2 ** (-(2 ** -(math.log2(n_heads) - 3)))
    return np.array([start * start ** i for i in range(n_heads)], dtype=np.float32)


class Sem:
    def __init__(self, ctx, nc, name):
        self.h = ctx.enter_context(nc.semaphore(name))
        self.val = 0

    def inc(self, instr, n=1):
        instr.then_inc(self.h, n)
        self.val += n
        return instr


class PingSem:

    def __init__(self, ctx, nc, name, n):
        self.s = [ctx.enter_context(nc.semaphore(f"{name}{i}")) for i in range(n)]
        self.n = n

    def inc(self, instr, k):
        instr.then_inc(self.s[k % self.n], 16)
        return instr

    def wait(self, eng, k):
        eng.wait_ge(self.s[k % self.n], 16 * (k // self.n + 1))


def build(phases=("p1", "p2", "p3", "p4"), debug_outs=()):
    nc = bass.Bass("TRN2", target_bir_lowering=False)

    xT = nc.dram_tensor("xT", [DM, S], F32, kind="ExternalInput").ap()
    xrows = nc.dram_tensor("xrows", [TQ, DM], F32, kind="ExternalInput").ap()
    wqkvT = nc.dram_tensor("wqkvT", [DM, 1536], F32, kind="ExternalInput").ap()
    wqn = nc.dram_tensor("wqn", [HD, 1], F32, kind="ExternalInput").ap()
    wkn = nc.dram_tensor("wkn", [HD, 1], F32, kind="ExternalInput").ap()
    slopes = nc.dram_tensor("slopes", [128, HPC], F32, kind="ExternalInput").ap()
    woT = nc.dram_tensor("woT", [DM, DM], F32, kind="ExternalInput").ap()
    wcatT = nc.dram_tensor("wcatT", [DM, 24], F32, kind="ExternalInput").ap()
    basetile = nc.dram_tensor("basetile", [128, 24], F32, kind="ExternalInput").ap()
    gatet = nc.dram_tensor("gatet", [128, 1], F32, kind="ExternalInput").ap()
    wbranch = nc.dram_tensor("wbranch", [1024, 1024], F32, kind="ExternalInput").ap()
    eye16 = nc.dram_tensor("eye16", [128, 16], F32, kind="ExternalInput").ap()
    rowsel = nc.dram_tensor("rowsel", [128, 4, 32], U32, kind="ExternalInput").ap()

    out = nc.dram_tensor("out", [TQ, DM], F32, kind="ExternalOutput").ap()
    dbg = {}
    for name, shape in (("qfm", [HPC * HD, S]), ("kfm", [KVPC * HD, S]),
                        ("v", [S, KVPC * 129]), ("attn", [S, HPC * HD]),
                        ("s1", [TQ, DM]), ("h1", [TQ, DM]), ("zb", [TQ, 24]),
                        ("hres", [TQ, 16]), ("consts", [128, 16])):
        if name in debug_outs:
            dbg[name] = nc.dram_tensor("dbg_" + name, shape, F32, kind="ExternalOutput").ap()

    wqkv_bf = nc.dram_tensor("wqkv_bf", [DM, 1536], BF16, kind="Internal").ap()
    woT_bf = nc.dram_tensor("woT_bf", [DM, DM], BF16, kind="Internal").ap()
    wb_bf = nc.dram_tensor("wb_bf", [1024, 1024], BF16, kind="Internal").ap()
    cin = nc.dram_tensor("cin", [S, HPC * HD], BF16, kind="Internal").ap()
    cout = nc.dram_tensor("cout", [4, S, HPC * HD], BF16, kind="Internal").ap()

    ctx = ExitStack()
    with ctx:
        sbt = lambda name, shape, dt=F32: ctx.enter_context(nc.sbuf_tensor(name, shape, dt))
        identB = sbt("identB", [128, 128], BF16)
        identF = sbt("identF", [128, 128], F32)
        cq = sbt("cq", [128, 1])
        slp = sbt("slp", [128, HPC])
        strips = sbt("strips", [128, HPC, NT])
        offs = sbt("offs", [128, HPC, 64])
        maskS = sbt("maskS", [128, 4, 512], BF16)
        maskT = sbt("maskT", [128, 4, 512], BF16)
        wc_sb = sbt("wc_sb", [128, 32, 24], BF16)
        base_s = sbt("base_s", [128, 24])
        base10 = sbt("base10", [128, 16])
        eye_s = sbt("eye_s", [128, 16])
        gsig = sbt("gsig", [128, 4])
        epss = sbt("epss", [128, 4])
        rsel = ctx.enter_context(nc.sbuf_tensor("rsel", [128, 4, 32], U32))
        smalls = sbt("smalls", [128, 4])

        s_init = Sem(ctx, nc, "s_init")
        s_cst = Sem(ctx, nc, "s_cst")
        s_dbg = Sem(ctx, nc, "s_dbg")
        n_dbg = [0]

        with nc.Block() as blk:
            s_gp = Sem(ctx, nc, "s_gp")

            def g_body(g):
                nms = 0
                g.memset(identB[:, :], 1.0).then_inc(s_gp.h, 1); nms += 1
                g.memset(identF[:, :], 1.0).then_inc(s_gp.h, 1); nms += 1
                for i in range(4):
                    g.memset(maskS[:, i, :], 0.0).then_inc(s_gp.h, 1); nms += 1
                    g.memset(maskT[:, i, :], 0.0).then_inc(s_gp.h, 1); nms += 1
                g.memset(epss[:, 0:1], EPS_QK)
                g.memset(epss[:, 1:2], EPS_LN)
                g.memset(epss[:, 2:3], RMS_EPS)
                s_init.inc(g.iota(strips[:, 0, :], [[128, NT]], base=0,
                                  channel_multiplier=1,
                                  allow_small_or_imprecise_dtypes=True))
                s_init.inc(g.iota(offs[:, 0, :], [[64, 64]], base=63,
                                  channel_multiplier=0,
                                  allow_small_or_imprecise_dtypes=True))
                g.wait_ge(s_gp.h, nms)
                s_gp.val += nms
                g.affine_select(identB[:, :], identB[:, :], [[1, 128]], ALU.is_equal,
                                0.0, base=0, channel_multiplier=-1)
                g.affine_select(identF[:, :], identF[:, :], [[1, 128]], ALU.is_equal,
                                0.0, base=0, channel_multiplier=-1)
                for i in range(4):
                    g.affine_select(maskS[:, i, :], maskS[:, i, :], [[-1, 512]],
                                    ALU.is_ge, NEG, base=128 * i, channel_multiplier=1)
                    g.affine_select(maskT[:, i, :], maskT[:, i, :], [[1, 512]],
                                    ALU.is_ge, NEG, base=-128 * i, channel_multiplier=-1)
            blk.gpsimd(g_body)

            s_ld0 = Sem(ctx, nc, "s_ld0")
            s_ldg = Sem(ctx, nc, "s_ldg")

            def s_body(s):
                s_ld0.inc(s.dma_start(slp[:, :], slopes), 16)
                s_ld0.inc(s.dma_start(smalls[:, 0:1], wqn), 16)
                s_ld0.inc(s.dma_start(smalls[:, 1:2], wkn), 16)
                s_ld0.inc(s.dma_start(base_s[:, :], basetile), 16)
                s_ld0.inc(s.dma_start(eye_s[:, :], eye16), 16)
                s_ld0.inc(s.dma_start(gsig[:, 0:1], gatet), 16)
                s_ld0.inc(s.dma_start(rsel[:, :, :], rowsel), 16)
            blk.sync(s_body)

            def gc_body(g):
                s_ldg.inc(g.dma_start(wc_sb[:, :, :],
                                      wcatT.rearrange("(c p) o -> p c o", p=128)), 16)
            blk.gpsimd(gc_body)

            def sc0_body(sc):
                sc.wait_ge(s_ld0.h, s_ld0.val)
                s_init.inc(sc.activation(gsig[:, 2:3], gsig[:, 0:1], AF.Exp, scale=-1.0))
            blk.scalar(sc0_body)

            def v0_body(v):
                v.wait_ge(s_ld0.h, s_ld0.val)
                v.wait_ge(s_init.h, s_init.val)
                v.tensor_scalar_add(gsig[:, 2:3], gsig[:, 2:3], 1.0)
                v.drain()
                v.reciprocal(gsig[:, 0:1], gsig[:, 2:3])
                v.tensor_mul(cq[:, :], smalls[:, 0:1], smalls[:, 1:2])
                v.drain()
                v.tensor_scalar(gsig[:, 1:2], gsig[:, 0:1], -1.0, 2.0, ALU.mult, ALU.add)
                v.tensor_scalar_mul(cq[:, :], cq[:, :], 1.0 / math.sqrt(HD))
                v.drain()
                for h in range(HPC - 1, -1, -1):
                    v.tensor_scalar_mul(strips[:, h, :], strips[:, 0, :], slp[:, h:h + 1])
                    v.tensor_scalar_mul(offs[:, h, :], offs[:, 0, :], slp[:, h:h + 1])
                s_init.inc(v.tensor_scalar_mul(base10[:, :], base_s[:, 0:16], 1.0 / 0.1))
            blk.vector(v0_body)

        with ExitStack() as p4s:
            e4 = p4s.enter_context
            blk = e4(nc.Block())
            cb0 = e4(nc.sbuf_tensor("castb0", [128, 2048], BF16))
            cb1 = e4(nc.sbuf_tensor("castb1", [128, 2048], BF16))
            cbs = [cb0, cb1]
            jobs = [(wqkvT, wqkv_bf, DM, 1536), (woT, woT_bf, DM, DM),
                    (wbranch, wb_bf, 1024, 1024)]
            s_ca = PingSem(ctx, nc, "s_ca", 2)
            s_csp = PingSem(ctx, nc, "s_csp", 2)
            ncast = [0]

            def gcast(g):
                k = 0
                for src, dst, rows, cols in jobs:
                    src3 = src.rearrange("(n p) c -> n p c", p=128)
                    for n in range(rows // 128):
                        for c0 in range(0, cols, 2048):
                            cw = min(2048, cols - c0)
                            if k >= 2:
                                s_csp.wait(g, k - 2)
                            s_ca.inc(g.dma_start(cbs[k % 2][:, 0:cw],
                                                 src3[n, :, c0:c0 + cw]), k)
                            k += 1
                ncast[0] = k
            blk.gpsimd(gcast)

            def scast(s):
                k = 0
                for src, dst, rows, cols in jobs:
                    dst3 = dst.rearrange("(n p) c -> n p c", p=128)
                    for n in range(rows // 128):
                        for c0 in range(0, cols, 2048):
                            cw = min(2048, cols - c0)
                            s_ca.wait(s, k)
                            s_csp.inc(s.dma_start(dst3[n, :, c0:c0 + cw],
                                                  cbs[k % 2][:, 0:cw]), k)
                            k += 1
            blk.sync(scast)

        with nc.sbuf_tensor("qfm", [128, HPC, S], BF16) as qfm, \
                nc.sbuf_tensor("kfm", [128, KVPC, S], BF16) as kfm, \
                nc.sbuf_tensor("vp", [128, KVPC, NT, 129], BF16) as vp:

            if "p1" in phases:
                with nc.Block() as blk, \
                        nc.sbuf_tensor("xts", [128, 2, 32, 256], BF16) as xts, \
                        nc.sbuf_tensor("wbb", [128, 2, 32, 256], BF16) as wbb, \
                        nc.sbuf_tensor("qsc", [128, 256], BF16) as qsc, \
                        nc.sbuf_tensor("qtm", [128, 16, 256], BF16) as qtm, \
                        nc.sbuf_tensor("ssb", [128, 8], F32) as ssb, \
                        nc.psum_tensor("pa0", [128, 256], F32) as pa0, \
                        nc.psum_tensor("pa1", [128, 256], F32) as pa1, \
                        nc.psum_tensor("pa2", [128, 256], F32) as pa2, \
                        nc.psum_tensor("pa3", [128, 256], F32) as pa3, \
                        nc.psum_tensor("ptp0", [128, 128], BF16) as ptp0, \
                        nc.psum_tensor("ptp1", [128, 128], BF16) as ptp1:
                    paccs = [pa0, pa1, pa2, pa3]
                    ptps = [ptp0, ptp1]
                    s_x = PingSem(ctx, nc, "s_x", 2)
                    s_w = PingSem(ctx, nc, "s_w", 2)
                    s_mm = Sem(ctx, nc, "s_mm")
                    s_sq = Sem(ctx, nc, "s_sq")
                    s_nrm = Sem(ctx, nc, "s_nrm")
                    s_tp = Sem(ctx, nc, "s_tp")
                    s_ev = Sem(ctx, nc, "s_ev")

                    xT3 = xT.rearrange("(c p) t -> p c t", p=128)
                    wq3 = wqkv_bf.rearrange("(c p) o -> p c o", p=128)

                    def tile_info(n):
                        tc, r = divmod(n, 12)
                        oh, t128 = divmod(r, 2)
                        return tc, oh, t128

                    ntp_before = [0] * 194
                    for n in range(192):
                        _, oh, _ = tile_info(n)
                        ntp_before[n + 1] = ntp_before[n] + (0 if oh == 5 else 2)

                    def gx(g):
                        g.wait_ge(s_csp.s[0], 16 * 52)
                        g.wait_ge(s_csp.s[1], 16 * 52)
                        g.memset(vp[:, 0, :, 128:129], 1.0)
                        g.memset(vp[:, 1, :, 128:129], 1.0)
                        for tc in range(_NTCB):
                            if tc >= 2 and _P1MODE not in ("mm",):
                                g.wait_ge(s_nrm.h, (tc - 1) * 12)
                            s_x.inc(g.dma_start(xts[:, tc % 2, :, :],
                                                xT3[:, :, 256 * tc:256 * tc + 256]), tc)
                    blk.gpsimd(gx)

                    def sw(s):
                        s.wait_ge(s_csp.s[0], 16 * 52)
                        s.wait_ge(s_csp.s[1], 16 * 52)
                        k = 0
                        for tc in range(_NTCB):
                            for oh in range(6):
                                if k >= 2:
                                    s.wait_ge(s_mm.h, max(0, 2 * k - 2))
                                c0 = 256 * (tc * 0 + oh)
                                s_w.inc(s.dma_start(wbb[:, k % 2, :, :],
                                                    wq3[:, :, c0:c0 + 256]), k)
                                k += 1
                    blk.sync(sw)

                    def emit_transposes(t, tcb):
                        for r in range(12):
                            n = tcb * 12 + r
                            _, oh, _ = tile_info(n)
                            if oh == 5:
                                continue
                            for hh in range(2):
                                m = ntp_before[n] + hh
                                if m >= 2:
                                    t.wait_ge(s_ev.h, m - 1)
                                t.wait_ge(s_nrm.h, n + 1)
                                s_tp.inc(t.matmul(ptps[m % 2][:, :],
                                         qtm[:, n % 16, 128 * hh:128 * hh + 128],
                                         identB[:, :], is_transpose=True,
                                         start=True, stop=True))

                    def emit_evacs(sc, tcb):
                        for r in range(12):
                            n = tcb * 12 + r
                            tc2, oh, t128 = tile_info(n)
                            if oh == 5:
                                continue
                            gtok = tc2 * 2 + t128
                            for hh in range(2):
                                m = ntp_before[n] + hh
                                sc.wait_ge(s_tp.h, m + 1)
                                if oh < 4:
                                    head = oh * 2 + hh
                                    ei = sc.activation(qfm[:, head, 128 * gtok:128 * gtok + 128],
                                                       ptps[m % 2][:, :], AF.Copy,
                                                       scale=cq[:, :])
                                else:
                                    ei = sc.activation(kfm[:, hh, 128 * gtok:128 * gtok + 128],
                                                       ptps[m % 2][:, :], AF.Copy)
                                s_ev.inc(ei)

                    for tcb in range(_NTCB):
                        def t_body(t, tcb=tcb):
                            if tcb > 0 and _P1MODE in ("full", "notp_dbg"):
                                emit_transposes(t, tcb - 1)
                            s_x.wait(t, tcb)
                            for r in range(12):
                                n = tcb * 12 + r
                                oh, t128 = divmod(r, 2)
                                s_w.wait(t, tcb * 6 + oh)
                                if n >= 4:
                                    t.wait_ge(s_nrm.h, n - 3)
                                acc = paccs[n % 4]
                                for d in range(32):
                                    mi = t.matmul(acc[:, :],
                                                  xts[:, tcb % 2, d, 128 * t128:128 * t128 + 128],
                                                  wbb[:, (tcb * 6 + oh) % 2, d, :],
                                                  start=(d == 0), stop=(d == 31))
                                s_mm.inc(mi)
                        blk.tensor(t_body)

                        def sc_body(sc, tcb=tcb):
                            if tcb > 0 and _P1MODE == "full":
                                emit_evacs(sc, tcb - 1)
                            if _P1MODE == "mm":
                                for r in range(12):
                                    sc.wait_ge(s_mm.h, tcb * 12 + r + 1)
                                    s_sq.inc(sc.nop())
                                return
                            for r in range(12):
                                n = tcb * 12 + r
                                _, oh, _ = tile_info(n)
                                acc = paccs[n % 4]
                                sc.wait_ge(s_mm.h, n + 1)
                                if n >= 4:
                                    sc.wait_ge(s_nrm.h, n - 3)
                                if oh != 5:
                                    sl = 2 * (n % 4)
                                    if _P1MODE != "mmln":
                                        for hh in range(2):
                                            sc.activation(qsc[:, 128 * hh:128 * hh + 128],
                                                          acc[:, 128 * hh:128 * hh + 128],
                                                          AF.Square,
                                                          accum_out=ssb[:, sl + hh:sl + hh + 1])
                                    if _P1MODE != "mmsqonly":
                                        sc.drain()
                                        sc.activation(ssb[:, sl:sl + 2], ssb[:, sl:sl + 2],
                                                      AF.Ln, scale=1.0 / HD, bias=epss[:, 0:1])
                                        sc.drain()
                                        li = sc.activation(ssb[:, sl:sl + 2], ssb[:, sl:sl + 2],
                                                           AF.Exp, scale=-0.5)
                                        s_sq.inc(li)
                                    else:
                                        sc.drain()
                                        s_sq.inc(sc.nop())
                                else:
                                    sc.drain()
                                    s_sq.inc(sc.nop())
                        blk.scalar(sc_body)

                        def v_body(v, tcb=tcb):
                            if _P1MODE in ("mm", "mmsq"):
                                for r in range(12):
                                    v.wait_ge(s_sq.h, tcb * 12 + r + 1)
                                    s_nrm.inc(v.nop())
                                return
                            for r in range(12):
                                n = tcb * 12 + r
                                tc2, oh, t128 = tile_info(n)
                                acc = paccs[n % 4]
                                v.wait_ge(s_sq.h, n + 1)
                                gtok = tc2 * 2 + t128
                                if oh != 5:
                                    if n >= 16 and _P1MODE == "full":
                                        v.wait_ge(s_tp.h, ntp_before[n - 15])
                                    sl = 2 * (n % 4)
                                    for hh in range(2):
                                        vi = v.tensor_scalar_mul(qtm[:, n % 16, 128 * hh:128 * hh + 128],
                                                                 acc[:, 128 * hh:128 * hh + 128],
                                                                 ssb[:, sl + hh:sl + hh + 1])
                                else:
                                    for hh in range(2):
                                        vi = v.tensor_copy(vp[:, hh, gtok, 0:128],
                                                           acc[:, 128 * hh:128 * hh + 128])
                                s_nrm.inc(vi)
                        blk.vector(v_body)

                    if _P1MODE == "full":
                        blk.tensor(lambda t: emit_transposes(t, _NTCB - 1))
                        blk.scalar(lambda sc: emit_evacs(sc, _NTCB - 1))

                if dbg.keys() & {"qfm", "kfm", "v", "consts"}:
                    with nc.Block() as blk:
                        def dqkv(g):
                            if "qfm" in dbg:
                                q2 = dbg["qfm"].rearrange("(h p) t -> p h t", p=128)
                                s_dbg.inc(g.dma_start(q2, qfm[:, :, :]), 16)
                                n_dbg[0] += 1
                            if "kfm" in dbg:
                                k2 = dbg["kfm"].rearrange("(h p) t -> p h t", p=128)
                                s_dbg.inc(g.dma_start(k2, kfm[:, :, :]), 16)
                                n_dbg[0] += 1
                            if "consts" in dbg:
                                s_dbg.inc(g.dma_start(dbg["consts"][:, 0:4], smalls[:, :]), 16)
                                s_dbg.inc(g.dma_start(dbg["consts"][:, 4:5], cq[:, :], allow_slow_non_contiguous=True), 16)
                                s_dbg.inc(g.dma_start(dbg["consts"][:, 5:9], gsig[:, :]), 16)
                                s_dbg.inc(g.dma_start(dbg["consts"][:, 9:13], epss[:, :]), 16)
                            if "v" in dbg:
                                v2 = dbg["v"].rearrange("(n p) (g c) -> p n g c", p=128, g=KVPC)
                                for gg in range(KVPC):
                                    s_dbg.inc(g.dma_start(
                                        v2[:, :, gg, :], vp[:, gg, :, :]), 16)
                                n_dbg[0] += 2
                            g.wait_ge(s_dbg.h, s_dbg.val)
                        blk.gpsimd(dqkv)

            if "p2" in phases:
                with nc.sbuf_tensor("aout", [128, NT, HPC, 128], BF16) as aout:
                    with ExitStack() as p2s:
                        e2 = p2s.enter_context
                        blk = e2(nc.Block())
                        ptb = e2(nc.sbuf_tensor("ptb", [128, 2, 512], BF16))
                        statb = e2(nc.sbuf_tensor("statb", [128, 32, 64], F32))
                        negm = e2(nc.sbuf_tensor("negm", [128, NSUP, HPC, 4], F32))
                        rcp = e2(nc.sbuf_tensor("rcp", [128, 4], F32))
                        pst0 = e2(nc.psum_tensor("pst0", [128, 512], F32))
                        pst1 = e2(nc.psum_tensor("pst1", [128, 512], F32))
                        pss0 = e2(nc.psum_tensor("pss0", [128, 512], F32))
                        pss1 = e2(nc.psum_tensor("pss1", [128, 512], F32))
                        ppv0 = e2(nc.psum_tensor("ppv0", [128, 129], F32))
                        ppv1 = e2(nc.psum_tensor("ppv1", [128, 129], F32))
                        ppv2 = e2(nc.psum_tensor("ppv2", [128, 129], F32))
                        ppv3 = e2(nc.psum_tensor("ppv3", [128, 129], F32))
                        psts = [pst0, pst1]
                        psss = [pss0, pss1]
                        ppvs = [ppv0, ppv1, ppv2, ppv3]
                        s_sa = Sem(ctx, nc, "s_sa")
                        s_sv = Sem(ctx, nc, "s_sv")
                        s_nm = Sem(ctx, nc, "s_nm")
                        s_st = Sem(ctx, nc, "s_st")
                        s_pt = Sem(ctx, nc, "s_pt")
                        s_pv = Sem(ctx, nc, "s_pv")
                        s_fin = Sem(ctx, nc, "s_fin")

                        na = [0]
                        nb = [0]
                        nfin = [0]

                        for s in range(NSUP):
                            def tA(t, s=s):
                                for h in range(HPC):
                                    gk = h // 4
                                    for i in range(4):
                                        q0 = 512 * s + 128 * i
                                        for c in range(s + 1):
                                            n = na[0]
                                            bank = psss[n % 2]
                                            if n >= 2:
                                                t.wait_ge(s_sv.h, n - 1)
                                            diag = (c == s)
                                            mi = t.matmul(bank[:, :], qfm[:, h, q0:q0 + 128],
                                                          kfm[:, gk, 512 * c:512 * c + 512],
                                                          start=True, stop=not diag)
                                            if diag:
                                                mi = t.matmul(bank[:, :], identB[:, :],
                                                              maskS[:, i, :], start=False,
                                                              stop=True)
                                            s_sa.inc(mi)
                                            na[0] += 1
                            blk.tensor(tA)

                            def vA(v, s=s):
                                nloc = na[0] - 32 * (s + 1)
                                for h in range(HPC):
                                    for i in range(4):
                                        for c in range(s + 1):
                                            bank = psss[nloc % 2]
                                            v.wait_ge(s_sa.h, nloc + 1)
                                            s_sv.inc(v.tensor_reduce(
                                                statb[:, 4 * h + i, 8 * c:8 * c + 8],
                                                bank[:, :].rearrange("p (a b) -> p a b", b=64),
                                                AX.X, ALU.max))
                                            nloc += 1
                                v.drain()
                                for h in range(HPC):
                                    for i in range(4):
                                        v.tensor_add(statb[:, 4 * h + i, 0:8 * (s + 1)],
                                                     statb[:, 4 * h + i, 0:8 * (s + 1)],
                                                     offs[:, h, 0:8 * (s + 1)])
                                v.drain()
                                for h in range(HPC):
                                    for i in range(4):
                                        ni = v.tensor_reduce(negm[:, s, h, i:i + 1],
                                                             statb[:, 4 * h + i, 0:8 * (s + 1)],
                                                             AX.X, ALU.max, negate=True)
                                    s_nm.inc(ni)
                            blk.vector(vA)

                            nch = 4 * (s + 1)
                            for h in range(HPC):
                                gk = h // 4
                                nb0 = nb[0]

                                def tB(t, s=s, h=h, gk=gk, nb0=nb0, nch=nch,
                                       fin0=nfin[0]):
                                    t.wait_ge(s_nm.h, s * 8 + h + 1)
                                    if fin0 >= 1:
                                        t.wait_ge(s_fin.h, fin0)
                                    for c in range(nch):
                                        n = nb0 + c
                                        bank = psts[n % 2]
                                        if n >= 2:
                                            t.wait_ge(s_pt.h, n - 1)
                                        for i in range(4):
                                            t.matmul(bank[:, 128 * i:128 * i + 128],
                                                     negm[:, s, h, i:i + 1].broadcast_to([128, 128]),
                                                     identF[:, :], is_transpose=True,
                                                     start=(i == 0), stop=False)
                                        diag = (c >= nch - 4)
                                        mi = t.matmul(bank[:, :],
                                                      kfm[:, gk, 128 * c:128 * c + 128],
                                                      qfm[:, h, 512 * s:512 * s + 512],
                                                      start=False, stop=not diag)
                                        if diag:
                                            mi = t.matmul(bank[:, :], identB[:, :],
                                                          maskT[:, c - (nch - 4), :],
                                                          start=False, stop=True)
                                        s_st.inc(mi)
                                        t.wait_ge(s_pt.h, n + 1)
                                        for i in range(4):
                                            pvi = t.matmul(ppvs[i][:, :],
                                                           ptb[:, n % 2, 128 * i:128 * i + 128],
                                                           vp[:, gk, c, :],
                                                           start=(c == 0),
                                                           stop=(c == nch - 1))
                                        s_pv.inc(pvi)
                                blk.tensor(tB)

                                def sB(sc, s=s, h=h, nb0=nb0, nch=nch):
                                    for c in range(nch):
                                        n = nb0 + c
                                        bank = psts[n % 2]
                                        sc.wait_ge(s_st.h, n + 1)
                                        if n >= 2:
                                            sc.wait_ge(s_pv.h, n - 1)
                                        s_pt.inc(sc.activation(
                                            ptb[:, n % 2, :], bank[:, :],
                                            AF.Exp, bias=strips[:, h, c:c + 1]))
                                blk.scalar(sB)

                                def vB(v, s=s, h=h, nb0=nb0, nch=nch):
                                    v.wait_ge(s_pv.h, nb0 + nch)
                                    for i in range(4):
                                        v.reciprocal(rcp[:, i:i + 1],
                                                     ppvs[i][:, 128:129])
                                    v.drain()
                                    for i in range(4):
                                        fi = v.tensor_scalar_mul(aout[:, 4 * s + i, h, :],
                                                                 ppvs[i][:, 0:128],
                                                                 rcp[:, i:i + 1])
                                    s_fin.inc(fi)
                                blk.vector(vB)
                                nfin[0] += 1
                                nb[0] += nch

                    with nc.Block() as blk:
                        s_ci = Sem(ctx, nc, "s_ci")

                        def gex(g):
                            cin4 = cin.rearrange("(n p) (h d) -> p n h d", p=128, h=HPC)
                            s_ci.inc(g.dma_start(cin4, aout[:, :, :, :]), 16)
                            g.wait_ge(s_ci.h, 16)
                            s_ci.inc(g.collective_compute(
                                "AllGather", ALU.bypass,
                                replica_groups=[[0, 1, 2, 3], [4, 5, 6, 7]],
                                ins=[cin], outs=[cout]), 1)
                            g.wait_ge(s_ci.h, 17)
                        blk.gpsimd(gex)

                    if "attn" in dbg:
                        with nc.Block() as blk:
                            def datt(g):
                                a2 = dbg["attn"].rearrange("(n p) f -> p n f", p=128)
                                s_dbg.inc(g.dma_start(
                                    a2, aout.ap().rearrange("p n h d -> p n (h d)")), 16)
                                g.wait_ge(s_dbg.h, s_dbg.val)
                            blk.gpsimd(datt)

        if "p3" in phases:
            s_bf = ctx.enter_context(nc.sbuf_tensor("s_bf", [128, 8, DM], BF16))
            lnst = ctx.enter_context(nc.sbuf_tensor("lnst", [128, 8, 4], F32))

            with ExitStack() as p3s:
                e3 = p3s.enter_context
                blk = e3(nc.Block())
                afm = e3(nc.sbuf_tensor("afm", [128, 32, 1024], BF16))
                attm = e3(nc.sbuf_tensor("attm", [128, 8, 1024], BF16))
                wsb = e3(nc.sbuf_tensor("wsb", [128, 2, 32, 256], BF16))
                xrb = e3(nc.sbuf_tensor("xrb", [128, 4, 256], F32))
                ptq0 = e3(nc.psum_tensor("ptq0", [128, 512], BF16))
                ptq1 = e3(nc.psum_tensor("ptq1", [128, 512], BF16))
                py0 = e3(nc.psum_tensor("py0", [128, 256], F32))
                py1 = e3(nc.psum_tensor("py1", [128, 256], F32))
                py2 = e3(nc.psum_tensor("py2", [128, 256], F32))
                py3 = e3(nc.psum_tensor("py3", [128, 256], F32))
                ptqs = [ptq0, ptq1]
                pys = [py0, py1, py2, py3]
                s_at = Sem(ctx, nc, "s_at")
                s_tq = Sem(ctx, nc, "s_tq")
                s_ae = Sem(ctx, nc, "s_ae")
                s_wl = PingSem(ctx, nc, "s_wl", 2)
                s_xr = PingSem(ctx, nc, "s_xr", 4)
                s_ym = Sem(ctx, nc, "s_ym")
                s_ye = Sem(ctx, nc, "s_ye")

                coutf = cout.rearrange("r n f -> (r n) f")

                def g_loads(g):
                    for r in range(4):
                        if r >= 1:
                            g.wait_ge(s_tq.h, 16 * r)
                        for tt in range(8):
                            s_at.inc(g.indirect_dma_start(
                                attm[:, tt, :], None, coutf,
                                IndirectOffsetOnAxis(rsel[:, r, tt:tt + 1], 0)), 16)
                blk.gpsimd(g_loads)

                def t_tr(t):
                    k = 0
                    for r in range(4):
                        t.wait_ge(s_at.h, 16 * 8 * (r + 1))
                        for tt in range(8):
                            for u in range(2):
                                m = 2 * k + u
                                if m >= 2:
                                    t.wait_ge(s_ae.h, m - 1)
                                for q in range(4):
                                    fc = 4 * u + q
                                    ti = t.matmul(ptqs[m % 2][:, 128 * q:128 * q + 128],
                                                  attm[:, tt, 128 * fc:128 * fc + 128],
                                                  identB[:, :], is_transpose=True,
                                                  start=(q == 0), stop=(q == 3))
                                s_tq.inc(ti)
                            k += 1
                blk.tensor(t_tr)

                def sc_ae(sc):
                    k = 0
                    for r in range(4):
                        for tt in range(8):
                            for u in range(2):
                                m = 2 * k + u
                                sc.wait_ge(s_tq.h, m + 1)
                                fc0 = 8 * r + 4 * u
                                s_ae.inc(sc.activation(
                                    afm[:, fc0:fc0 + 4, 128 * tt:128 * tt + 128],
                                    ptqs[m % 2][:, :].rearrange("p (a b) -> p a b", b=128),
                                    AF.Copy))
                            k += 1
                blk.scalar(sc_ae)

                wo3 = woT_bf.rearrange("(c p) m -> p c m", p=128)

                def s_wload(s):
                    for mh in range(16):
                        if mh >= 2:
                            s.wait_ge(s_ym.h, (mh - 1) * 8)
                        s_wl.inc(s.dma_start(wsb[:, mh % 2, :, :],
                                             wo3[:, :, 256 * mh:256 * mh + 256]), mh)
                blk.sync(s_wload)

                def s_xload(g):
                    for mh in range(16):
                        for tt in range(8):
                            k = mh * 8 + tt
                            if k >= 4:
                                g.wait_ge(s_ye.h, k - 3)
                            s_xr.inc(g.dma_start(
                                xrb[:, k % 4, :],
                                xrows[128 * tt:128 * tt + 128,
                                      256 * mh:256 * mh + 256]), k)
                blk.gpsimd(s_xload)

                def t_wo(t):
                    t.wait_ge(s_ae.h, 64)
                    for mh in range(16):
                        s_wl.wait(t, mh)
                        for tt in range(8):
                            k = mh * 8 + tt
                            acc = pys[k % 4]
                            if k >= 4:
                                t.wait_ge(s_ye.h, k - 3)
                            for fc in range(32):
                                yi = t.matmul(acc[:, :], afm[:, fc, 128 * tt:128 * tt + 128],
                                              wsb[:, mh % 2, fc, :],
                                              start=(fc == 0), stop=(fc == 31))
                            s_ym.inc(yi)
                blk.tensor(t_wo)

                def v_wo(v):
                    for mh in range(16):
                        for tt in range(8):
                            k = mh * 8 + tt
                            acc = pys[k % 4]
                            v.wait_ge(s_ym.h, k + 1)
                            s_xr.wait(v, k)
                            s_ye.inc(v.scalar_tensor_tensor(
                                s_bf[:, tt, 256 * mh:256 * mh + 256], acc[:, :],
                                1.0, xrb[:, k % 4, :], ALU.mult, ALU.add))
                blk.vector(v_wo)

            if "s1" in dbg:
                with nc.Block() as blk:
                    def ds1(g):
                        d2 = dbg["s1"].rearrange("(n p) f -> p n f", p=128)
                        s_dbg.inc(g.dma_start(d2, s_bf[:, :, :]), 16)
                        g.wait_ge(s_dbg.h, s_dbg.val)
                    blk.gpsimd(ds1)

        if "p4" in phases:
            with ExitStack() as p4b:
                eb4 = p4b.enter_context
                blk = eb4(nc.Block())
                junk = eb4(nc.sbuf_tensor("junk", [128, DM], BF16))
                stat4 = eb4(nc.sbuf_tensor("stat4", [128, 32], F32))
                h1f = eb4(nc.sbuf_tensor("h1f", [128, 32, 128], BF16))
                zbuf = eb4(nc.sbuf_tensor("zbuf", [128, 8, 24], F32))
                z0 = eb4(nc.sbuf_tensor("z0", [128, 8, 16], F32))
                zp = eb4(nc.sbuf_tensor("zp", [128, 8, 16], F32))
                tsk = eb4(nc.sbuf_tensor("tsk", [128, 128], F32))
                esk = eb4(nc.sbuf_tensor("esk", [128, 128], F32))
                msk = eb4(nc.sbuf_tensor("msk", [128, 32], F32))
                ulse = eb4(nc.sbuf_tensor("ulse", [128, 32], F32))
                vlse = eb4(nc.sbuf_tensor("vlse", [128, 32], F32))
                hres = eb4(nc.sbuf_tensor("hres", [128, 8, 16], F32))
                hpp = eb4(nc.sbuf_tensor("hpp", [128, 8, 8], F32))
                invt = eb4(nc.sbuf_tensor("invt", [128, 8, 2], F32))
                xp = eb4(nc.sbuf_tensor("xp", [128, 1024], BF16))
                xpf = eb4(nc.sbuf_tensor("xpf", [128, 1024], F32))
                xpfm = eb4(nc.sbuf_tensor("xpfm", [128, 8, 128], BF16))
                wbsb = eb4(nc.sbuf_tensor("wbsb", [128, 8, 1024], BF16))
                ybf = eb4(nc.sbuf_tensor("ybf", [128, 8, 1024], BF16))
                s2 = eb4(nc.sbuf_tensor("s2", [128, 4, 1024], F32))
                outb = eb4(nc.sbuf_tensor("outb", [128, 2, DM], F32))
                ph0 = eb4(nc.psum_tensor("ph0", [128, 512], BF16))
                ph1 = eb4(nc.psum_tensor("ph1", [128, 512], BF16))
                plg = eb4(nc.psum_tensor("plg", [128, 24], F32))
                pyb0 = eb4(nc.psum_tensor("pyb0", [128, 512], F32))
                pyb1 = eb4(nc.psum_tensor("pyb1", [128, 512], F32))
                phs = [ph0, ph1]
                pybs = [pyb0, pyb1]
                s_sq1 = Sem(ctx, nc, "s_sq1")
                s_vst = Sem(ctx, nc, "s_vst")
                s_iv1 = Sem(ctx, nc, "s_iv1")
                s_h1 = Sem(ctx, nc, "s_h1")
                s_rm2 = Sem(ctx, nc, "s_rm2")
                s_ht = Sem(ctx, nc, "s_ht")
                s_he = Sem(ctx, nc, "s_he")
                s_lg = Sem(ctx, nc, "s_lg")
                s_zb = Sem(ctx, nc, "s_zb")
                s_sk = Sem(ctx, nc, "s_sk")
                s_xp = Sem(ctx, nc, "s_xp")
                s_xt = Sem(ctx, nc, "s_xt")
                s_xe = Sem(ctx, nc, "s_xe")
                s_yb = Sem(ctx, nc, "s_yb")
                s_yv = Sem(ctx, nc, "s_yv")
                s_s2 = Sem(ctx, nc, "s_s2")
                s_l2 = Sem(ctx, nc, "s_l2")
                s_ob = Sem(ctx, nc, "s_ob")
                s_out = PingSem(ctx, nc, "s_out", 2)
                s_wbl = Sem(ctx, nc, "s_wbl")

                def g_wb(g):
                    wb3 = wb_bf.rearrange("(c p) m -> p c m", p=128)
                    s_wbl.inc(g.dma_start(wbsb[:, :, :], wb3), 16)
                blk.gpsimd(g_wb)

                def sc_ln1(sc):
                    for tt in range(8):
                        s_sq1.inc(sc.activation(junk[:, :], s_bf[:, tt, :], AF.Square,
                                                 accum_out=stat4[:, tt:tt + 1]))
                blk.scalar(sc_ln1)

                def v_ln1(v):
                    for tt in range(8):
                        v.wait_ge(s_sq1.h, tt + 1)
                        v.reduce_sum(stat4[:, 8 + tt:8 + tt + 1], s_bf[:, tt, :], AX.X)
                        v.drain()
                        v.tensor_scalar_mul(lnst[:, tt, 0:1],
                                            stat4[:, 8 + tt:8 + tt + 1], 1.0 / DM)
                        v.tensor_scalar_mul(lnst[:, tt, 3:4], stat4[:, tt:tt + 1], 1.0 / DM)
                        v.drain()
                        v.tensor_mul(lnst[:, tt, 2:3], lnst[:, tt, 0:1], lnst[:, tt, 0:1])
                        v.drain()
                        s_vst.inc(v.tensor_sub(lnst[:, tt, 3:4], lnst[:, tt, 3:4],
                                                lnst[:, tt, 2:3]))
                blk.vector(v_ln1)

                def sc_inv1(sc):
                    for tt in range(8):
                        sc.wait_ge(s_vst.h, tt + 1)
                        sc.activation(lnst[:, tt, 1:2], lnst[:, tt, 3:4], AF.Ln,
                                      bias=epss[:, 1:2])
                        sc.drain()
                        s_iv1.inc(sc.activation(lnst[:, tt, 1:2], lnst[:, tt, 1:2],
                                                 AF.Exp, scale=-0.5))
                blk.scalar(sc_inv1)

                def v_h1(v):
                    for tt in range(8):
                        v.wait_ge(s_iv1.h, tt + 1)
                        s_h1.inc(v.tensor_scalar(s_bf[:, tt, :], s_bf[:, tt, :],
                                                  lnst[:, tt, 0:1], lnst[:, tt, 1:2],
                                                  ALU.subtract, ALU.mult))
                blk.vector(v_h1)

                if "h1" in dbg:
                    def dh1(g):
                        g.wait_ge(s_h1.h, 8)
                        d2 = dbg["h1"].rearrange("(n p) f -> p n f", p=128)
                        s_dbg.inc(g.dma_start(d2, s_bf[:, :, :]), 16)
                        g.wait_ge(s_dbg.h, s_dbg.val)
                    blk.gpsimd(dh1)

                def sc_rms(sc):
                    for tt in range(8):
                        sc.wait_ge(s_h1.h, tt + 1)
                        sc.activation(junk[:, :], s_bf[:, tt, :], AF.Square,
                                      accum_out=invt[:, tt, 0:1])
                        sc.drain()
                        sc.activation(invt[:, tt, 1:2], invt[:, tt, 0:1], AF.Ln,
                                      scale=1.0 / DM, bias=epss[:, 2:3])
                        sc.drain()
                        s_rm2.inc(sc.activation(invt[:, tt, 1:2], invt[:, tt, 1:2],
                                                 AF.Exp, scale=-0.5))
                blk.scalar(sc_rms)

                for tt in range(8):
                    def t_h1t(t, tt=tt):
                        t.wait_ge(s_h1.h, tt + 1)
                        for u in range(8):
                            m = tt * 8 + u
                            if m >= 2:
                                t.wait_ge(s_he.h, m - 1)
                            for q in range(4):
                                fc = 4 * u + q
                                hi = t.matmul(phs[m % 2][:, 128 * q:128 * q + 128],
                                              s_bf[:, tt, 128 * fc:128 * fc + 128],
                                              identB[:, :], is_transpose=True,
                                              start=(q == 0), stop=(q == 3))
                            s_ht.inc(hi)
                    blk.tensor(t_h1t)

                    def sc_h1e(sc, tt=tt):
                        if tt >= 1:
                            sc.wait_ge(s_lg.h, tt)
                        for u in range(8):
                            m = tt * 8 + u
                            sc.wait_ge(s_ht.h, m + 1)
                            s_he.inc(sc.activation(
                                h1f[:, 4 * u:4 * u + 4, :],
                                phs[m % 2][:, :].rearrange("p (a b) -> p a b", b=128),
                                AF.Copy))
                    blk.scalar(sc_h1e)

                    def t_lg(t, tt=tt):
                        t.wait_ge(s_he.h, (tt + 1) * 8)
                        if tt >= 1:
                            t.wait_ge(s_zb.h, tt)
                        for fc in range(32):
                            lgi = t.matmul(plg[:, :], h1f[:, fc, :], wc_sb[:, fc, :],
                                           start=(fc == 0), stop=(fc == 31))
                        s_lg.inc(lgi)
                    blk.tensor(t_lg)

                    def v_z(v, tt=tt):
                        v.wait_ge(s_lg.h, tt + 1)
                        v.wait_ge(s_rm2.h, tt + 1)
                        v.tensor_scalar_mul(invt[:, tt, 0:1], invt[:, tt, 1:2], 10.0)
                        v.drain()
                        v.scalar_tensor_tensor(zbuf[:, tt, 0:16], plg[:, 0:16],
                                               invt[:, tt, 0:1], base10[:, :],
                                               ALU.mult, ALU.add)
                        s_zb.inc(v.scalar_tensor_tensor(
                            zbuf[:, tt, 16:24], plg[:, 16:24],
                            invt[:, tt, 1:2], base_s[:, 16:24],
                            ALU.mult, ALU.add))
                    blk.vector(v_z)

                def v_sm(v):
                    v.wait_ge(s_zb.h, 8)
                    pre = zbuf[:, :, 16:24].rearrange("p a (n f) -> p a n f", f=4)
                    v.tensor_reduce(msk[:, 0:16].rearrange("p (a n) -> p a n", n=2),
                                    pre, AX.X, ALU.max)
                    v.drain()
                    mb = msk[:, 0:16].rearrange("p (a n) -> p a n", n=2).unsqueeze(3) \
                        .broadcast_to([128, 8, 2, 4])
                    s_sk.inc(v.tensor_sub(
                        hpp.ap().rearrange("p a (n f) -> p a n f", f=4), pre, mb))
                blk.vector(v_sm)

                def sc_sm(sc):
                    sc.wait_ge(s_sk.h, 1)
                    s_sk.inc(sc.activation(hpp[:, :, :], hpp[:, :, :], AF.Exp))
                blk.scalar(sc_sm)

                def v_sm2(v):
                    v.wait_ge(s_sk.h, 2)
                    hp4 = hpp.ap().rearrange("p a (n f) -> p a n f", f=4)
                    v.tensor_reduce(msk[:, 0:16].rearrange("p (a n) -> p a n", n=2),
                                    hp4, AX.X, ALU.add)
                    v.drain()
                    v.reciprocal(msk[:, 0:16], msk[:, 0:16])
                    v.drain()
                    mb = msk[:, 0:16].rearrange("p (a n) -> p a n", n=2).unsqueeze(3) \
                        .broadcast_to([128, 8, 2, 4])
                    v.tensor_mul(hp4, hp4, mb)
                    zr = zbuf[:, :, 0:16].rearrange("p a (i j) -> p a i j", j=4)
                    v.drain()
                    v.tensor_reduce(msk[:, 0:32].rearrange("p (a i) -> p a i", i=4),
                                    zr, AX.X, ALU.max)
                    v.drain()
                    mb2 = msk[:, 0:32].rearrange("p (a i) -> p a i", i=4).unsqueeze(3) \
                        .broadcast_to([128, 8, 4, 4])
                    v.tensor_sub(z0.ap().rearrange("p a (i j) -> p a i j", j=4), zr, mb2)
                    v.drain()
                    s_sk.inc(v.tensor_copy(
                        zp.ap().rearrange("p a (j i) -> p a j i", i=4),
                        z0.ap().rearrange("p a (i j) -> p a i j", j=4)
                        .transpose([0, 1, 3, 2])))
                blk.vector(v_sm2)

                for tt in range(8):
                    def v_xp(v, tt=tt):
                        v.wait_ge(s_sk.h, 3)
                        if tt >= 1:
                            v.wait_ge(s_xt.h, 2 * tt)
                        hp = hpp[:, tt, 0:4]
                        v.tensor_scalar_mul(xpf[:, :], s_bf[:, tt, 0:1024], hp[:, 0:1])
                        for j in (1, 2):
                            v.drain()
                            v.scalar_tensor_tensor(xpf[:, :],
                                                   s_bf[:, tt, 1024 * j:1024 * j + 1024],
                                                   hp[:, j:j + 1], xpf[:, :],
                                                   ALU.mult, ALU.add)
                        v.drain()
                        s_xp.inc(v.scalar_tensor_tensor(
                            xp[:, :], s_bf[:, tt, 3072:4096],
                            hp[:, 3:4], xpf[:, :], ALU.mult, ALU.add))
                    blk.vector(v_xp)

                    def t_xpt(t, tt=tt):
                        t.wait_ge(s_xp.h, tt + 1)
                        for u in range(2):
                            m = 2 * tt + u
                            if m >= 2:
                                t.wait_ge(s_xe.h, m - 1)
                            for q in range(4):
                                fc = 4 * u + q
                                xi = t.matmul(phs[m % 2][:, 128 * q:128 * q + 128],
                                              xp[:, 128 * fc:128 * fc + 128],
                                              identB[:, :], is_transpose=True,
                                              start=(q == 0), stop=(q == 3))
                            s_xt.inc(xi)
                    blk.tensor(t_xpt)

                    def sc_xpe(sc, tt=tt):
                        if tt >= 1:
                            sc.wait_ge(s_yb.h, 2 * tt)
                        for u in range(2):
                            m = 2 * tt + u
                            sc.wait_ge(s_xt.h, m + 1)
                            s_xe.inc(sc.activation(
                                xpfm[:, 4 * u:4 * u + 4, :],
                                phs[m % 2][:, :].rearrange("p (a b) -> p a b", b=128),
                                AF.Copy))
                    blk.scalar(sc_xpe)

                    def t_y(t, tt=tt):
                        t.wait_ge(s_xe.h, 2 * (tt + 1))
                        if tt == 0:
                            t.wait_ge(s_wbl.h, 16)
                        for half in range(2):
                            k = 2 * tt + half
                            if k >= 2:
                                t.wait_ge(s_yv.h, k - 1)
                            for cc in range(8):
                                ybi = t.matmul(pybs[k % 2][:, :], xpfm[:, cc, :],
                                               wbsb[:, cc, 512 * half:512 * half + 512],
                                               start=(cc == 0), stop=(cc == 7))
                            s_yb.inc(ybi)
                    blk.tensor(t_y)

                    def v_y(v, tt=tt):
                        for half in range(2):
                            k = 2 * tt + half
                            v.wait_ge(s_yb.h, k + 1)
                            s_yv.inc(v.tensor_copy(
                                ybf[:, tt, 512 * half:512 * half + 512],
                                pybs[k % 2][:, :]))
                    blk.vector(v_y)

                z04 = z0.ap().rearrange("p a (i j) -> p a i j", j=4)
                zp4 = zp.ap().rearrange("p a (j i) -> p a j i", i=4)
                t4 = tsk.ap().rearrange("p (a i j) -> p a i j", i=4, j=4)
                e4 = esk.ap().rearrange("p (a i j) -> p a i j", i=4, j=4)
                m4 = msk.ap().rearrange("p (a i) -> p a i", i=4)
                ul4 = ulse.ap().rearrange("p (a i) -> p a i", i=4)
                vl4 = vlse.ap().rearrange("p (a j) -> p a j", j=4)
                skc = [3]

                def step_v(fn):
                    def body(v, c=skc[0]):
                        v.wait_ge(s_sk.h, c)
                        s_sk.inc(fn(v))
                    blk.vector(body)
                    skc[0] += 1

                def step_s(fn):
                    def body(sc, c=skc[0]):
                        sc.wait_ge(s_sk.h, c)
                        s_sk.inc(fn(sc))
                    blk.scalar(body)
                    skc[0] += 1

                for it in range(50):
                    def u1(v, it=it):
                        if it == 0:
                            v.tensor_copy(tsk[:, :], z0.ap().rearrange("p a f -> p (a f)"))
                        else:
                            vb = vl4.unsqueeze(2).broadcast_to([128, 8, 4, 4])
                            v.tensor_sub(t4, z04, vb)
                        v.drain()
                        v.tensor_reduce(m4, t4, AX.X, ALU.max)
                        v.drain()
                        mb = m4.unsqueeze(3).broadcast_to([128, 8, 4, 4])
                        return v.tensor_sub(t4, t4, mb)
                    step_v(u1)
                    step_s(lambda sc: sc.activation(esk[:, :], tsk[:, :], AF.Exp))
                    step_v(lambda v: v.tensor_reduce(ul4, e4, AX.X, ALU.add))
                    step_s(lambda sc: sc.activation(ulse[:, :], ulse[:, :], AF.Ln))

                    def v1(v):
                        v.tensor_add(ulse[:, :], ulse[:, :], msk[:, :])
                        v.drain()
                        ub = ul4.unsqueeze(2).broadcast_to([128, 8, 4, 4])
                        v.tensor_sub(t4, zp4, ub)
                        v.drain()
                        v.tensor_reduce(m4, t4, AX.X, ALU.max)
                        v.drain()
                        mb = m4.unsqueeze(3).broadcast_to([128, 8, 4, 4])
                        return v.tensor_sub(t4, t4, mb)
                    step_v(v1)
                    step_s(lambda sc: sc.activation(esk[:, :], tsk[:, :], AF.Exp))
                    step_v(lambda v: v.tensor_reduce(vl4, e4, AX.X, ALU.add))
                    step_s(lambda sc: sc.activation(vlse[:, :], vlse[:, :], AF.Ln))
                    step_v(lambda v: v.tensor_add(vlse[:, :], vlse[:, :], msk[:, :]))

                def hr1(v):
                    ub = ul4.unsqueeze(3).broadcast_to([128, 8, 4, 4])
                    vb = vl4.unsqueeze(2).broadcast_to([128, 8, 4, 4])
                    hr4 = hres.ap().rearrange("p a (i j) -> p a i j", j=4)
                    v.tensor_sub(hr4, z04, ub)
                    v.drain()
                    return v.tensor_sub(hr4, hr4, vb)
                step_v(hr1)
                step_s(lambda sc: sc.activation(hres[:, :, :], hres[:, :, :], AF.Exp))

                def hr2(v):
                    eb = eye_s[:, :].unsqueeze(1).broadcast_to([128, 8, 16])
                    v.tensor_scalar_mul(hres[:, :, :], hres[:, :, :], gsig[:, 0:1])
                    v.drain()
                    return v.scalar_tensor_tensor(hres[:, :, :], eb, gsig[:, 1:2],
                                                  hres[:, :, :], ALU.mult, ALU.add)
                step_v(hr2)
                SK_DONE = skc[0]

                if "zb" in dbg:
                    def dzb(g):
                        g.wait_ge(s_zb.h, 8)
                        d2 = dbg["zb"].rearrange("(n p) f -> p n f", p=128)
                        s_dbg.inc(g.dma_start(d2, zbuf[:, :, :]), 16)
                        g.wait_ge(s_dbg.h, s_dbg.val)
                    blk.gpsimd(dzb)
                if "hres" in dbg:
                    def dhr(g):
                        g.wait_ge(s_sk.h, SK_DONE)
                        d2 = dbg["hres"].rearrange("(n p) f -> p n f", p=128)
                        s_dbg.inc(g.dma_start(d2, hres[:, :, :]), 16)
                        g.wait_ge(s_dbg.h, s_dbg.val)
                    blk.gpsimd(dhr)

                for tt in range(8):
                    def v_mix(v, tt=tt):
                        v.wait_ge(s_sk.h, SK_DONE)
                        v.wait_ge(s_yv.h, 2 * (tt + 1))
                        if tt >= 1:
                            v.wait_ge(s_l2.h, 3 * (tt - 1) + 1)
                        for i in range(4):
                            hr = hres[:, tt, :]
                            v.tensor_scalar_mul(s2[:, i, :], s_bf[:, tt, 0:1024],
                                                hr[:, 4 * i:4 * i + 1])
                            for j in (1, 2, 3):
                                v.drain()
                                v.scalar_tensor_tensor(
                                    s2[:, i, :], s_bf[:, tt, 1024 * j:1024 * j + 1024],
                                    hr[:, 4 * i + j:4 * i + j + 1],
                                    s2[:, i, :], ALU.mult, ALU.add)
                            v.drain()
                            s2i = v.scalar_tensor_tensor(s2[:, i, :], ybf[:, tt, :],
                                                          hpp[:, tt, 4 + i:4 + i + 1],
                                                          s2[:, i, :], ALU.mult, ALU.add)
                        s_s2.inc(s2i)
                    blk.vector(v_mix)

                    def sc_ln2(sc, tt=tt):
                        sc.wait_ge(s_s2.h, tt + 1)
                        s_l2.inc(sc.activation(
                            junk[:, :], s2.ap().rearrange("p a f -> p (a f)"),
                            AF.Square, accum_out=stat4[:, 16 + tt:16 + tt + 1]))
                    blk.scalar(sc_ln2)

                    def v_ln2(v, tt=tt):
                        v.wait_ge(s_l2.h, 3 * tt + 1)
                        s2f = s2.ap().rearrange("p a f -> p (a f)")
                        v.reduce_sum(stat4[:, 24 + tt:24 + tt + 1], s2f, AX.X)
                        v.drain()
                        v.tensor_scalar_mul(lnst[:, tt, 0:1],
                                            stat4[:, 24 + tt:24 + tt + 1], 1.0 / DM)
                        v.tensor_scalar_mul(lnst[:, tt, 3:4],
                                            stat4[:, 16 + tt:16 + tt + 1], 1.0 / DM)
                        v.drain()
                        v.tensor_mul(lnst[:, tt, 2:3], lnst[:, tt, 0:1], lnst[:, tt, 0:1])
                        v.drain()
                        s_l2.inc(v.tensor_sub(lnst[:, tt, 3:4], lnst[:, tt, 3:4],
                                              lnst[:, tt, 2:3]))
                    blk.vector(v_ln2)

                    def sc_inv2(sc, tt=tt):
                        sc.wait_ge(s_l2.h, 3 * tt + 2)
                        sc.activation(lnst[:, tt, 1:2], lnst[:, tt, 3:4], AF.Ln,
                                      bias=epss[:, 1:2])
                        sc.drain()
                        s_l2.inc(sc.activation(lnst[:, tt, 1:2], lnst[:, tt, 1:2],
                                                AF.Exp, scale=-0.5))
                    blk.scalar(sc_inv2)

                    def v_out(v, tt=tt):
                        v.wait_ge(s_l2.h, 3 * tt + 3)
                        if tt >= 2:
                            s_out.wait(v, tt - 2)
                        s_ob.inc(v.tensor_scalar(outb[:, tt % 2, :],
                                                  s2.ap().rearrange("p a f -> p (a f)"),
                                                  lnst[:, tt, 0:1], lnst[:, tt, 1:2],
                                                  ALU.subtract, ALU.mult))
                    blk.vector(v_out)

                    def s_st_out(s, tt=tt):
                        s.wait_ge(s_ob.h, tt + 1)
                        s_out.inc(s.dma_start(out[128 * tt:128 * tt + 128, :],
                                              outb[:, tt % 2, :]), tt)
                    blk.sync(s_st_out)

                def s_fin_wait(s):
                    s.wait_ge(s_out.s[0], 16 * 4)
                    s.wait_ge(s_out.s[1], 16 * 4)
                blk.sync(s_fin_wait)

    return nc


def _prep_inputs(L0_attn_x, L0_attn_Wq, L0_attn_Wk, L0_attn_Wv, L0_attn_Wo,
                 L0_attn_q_norm_w, L0_attn_k_norm_w, L0_ffn_W_res, L0_ffn_W_pre,
                 L0_ffn_W_post, L0_ffn_H_res_base, L0_ffn_H_pre_base,
                 L0_ffn_H_post_base, L0_ffn_gate, L0_ffn_W_branch):
    f32 = lambda a: np.ascontiguousarray(np.asarray(a), dtype=np.float32)
    x = f32(L0_attn_x)
    WqT = f32(L0_attn_Wq).T
    WkT = f32(L0_attn_Wk).T
    WvT = f32(L0_attn_Wv).T
    WoT = np.ascontiguousarray(f32(L0_attn_Wo).T)
    wcat = np.concatenate([f32(L0_ffn_W_res), f32(L0_ffn_W_pre), f32(L0_ffn_W_post)],
                          axis=0)
    wcatT = np.ascontiguousarray(wcat.T)
    basetile = np.ascontiguousarray(np.tile(np.concatenate(
        [f32(L0_ffn_H_res_base).ravel(), f32(L0_ffn_H_pre_base).ravel(),
         f32(L0_ffn_H_post_base).ravel()])[None, :], (128, 1)), dtype=np.float32)
    gatet = np.ascontiguousarray(np.tile(f32(L0_ffn_gate).reshape(1, 1), (128, 1)),
                                 dtype=np.float32)
    wbr = f32(L0_ffn_W_branch)
    eye = np.ascontiguousarray(np.tile(np.eye(4, dtype=np.float32).reshape(1, 16),
                                       (128, 1)), dtype=np.float32)
    slopes_all = _alibi_slopes(H)
    wqn = np.ascontiguousarray(f32(L0_attn_q_norm_w).reshape(HD, 1))
    wkn = np.ascontiguousarray(f32(L0_attn_k_norm_w).reshape(HD, 1))

    xTs = [np.ascontiguousarray(x[b].T) for b in range(B_)]
    in_maps = []
    for c in range(8):
        b, j = divmod(c, 4)
        wq_sl = WqT[:, 1024 * j:1024 * j + 1024]
        wk_sl = WkT[:, 256 * j:256 * j + 256]
        wv_sl = WvT[:, 256 * j:256 * j + 256]
        in_maps.append({
            "xT": xTs[b],
            "xrows": np.ascontiguousarray(x[b, 1024 * j:1024 * j + 1024, :]),
            "wqkvT": np.ascontiguousarray(np.concatenate([wq_sl, wk_sl, wv_sl], axis=1)),
            "wqn": wqn, "wkn": wkn,
            "slopes": np.ascontiguousarray(
                np.tile(slopes_all[8 * j:8 * j + 8][None, :], (128, 1))),
            "woT": WoT,
            "wcatT": wcatT,
            "basetile": basetile,
            "gatet": gatet,
            "wbranch": wbr,
            "eye16": eye,
            "rowsel": np.ascontiguousarray(
                (4096 * np.arange(4, dtype=np.uint32)[None, :, None]
                 + 1024 * j + np.arange(128, dtype=np.uint32)[:, None, None]
                 + 128 * np.arange(32, dtype=np.uint32)[None, None, :])),
        })
    return in_maps


_CACHED = {}


def kernel(**inputs):
    in_maps = _prep_inputs(**inputs)
    if "full" not in _CACHED:
        _CACHED["full"] = build()
    nc = _CACHED["full"]
    res = run_bass_kernel_spmd(nc, in_maps, core_ids=list(range(8)))
    outp = np.empty((B_, S, DM), dtype=np.float32)
    for c in range(8):
        b, j = divmod(c, 4)
        outp[b, 1024 * j:1024 * j + 1024, :] = res.results[c]["out"]
    return outp


# version 02
# speedup vs baseline: 1.2315x; 1.2315x over previous
import math
import os
from contextlib import ExitStack
_NTCB = int(os.environ.get("K_NTCB", "16"))
_P1MODE = os.environ.get("K_P1MODE", "full")

import numpy as np
import concourse.bass as bass
import concourse.mybir as mybir
from concourse.bass import IndirectOffsetOnAxis
from concourse.bass_utils import run_bass_kernel_spmd

F32 = mybir.dt.float32
BF16 = mybir.dt.bfloat16
U32 = mybir.dt.uint32
AF = mybir.ActivationFunctionType
ALU = mybir.AluOpType
AX = mybir.AxisListType
ds = bass.ds

B_, S, DM = 2, 4096, 4096
H, HKV, HD = 32, 8, 128
HPC, KVPC = 8, 2
TQ = 1024
EPS_QK = 1e-6
EPS_LN = 1e-5
RMS_EPS = 1.1920929e-07
NEG = -1e30
NT = S // 128
NSUP = S // 512


def _alibi_slopes(n_heads):
    start = 2 ** (-(2 ** -(math.log2(n_heads) - 3)))
    return np.array([start * start ** i for i in range(n_heads)], dtype=np.float32)


class Sem:
    def __init__(self, ctx, nc, name):
        self.h = ctx.enter_context(nc.semaphore(name))
        self.val = 0

    def inc(self, instr, n=1):
        instr.then_inc(self.h, n)
        self.val += n
        return instr


class PingSem:

    def __init__(self, ctx, nc, name, n):
        self.s = [ctx.enter_context(nc.semaphore(f"{name}{i}")) for i in range(n)]
        self.n = n

    def inc(self, instr, k):
        instr.then_inc(self.s[k % self.n], 16)
        return instr

    def wait(self, eng, k):
        eng.wait_ge(self.s[k % self.n], 16 * (k // self.n + 1))


def build(phases=("p1", "p2", "p3", "p4"), debug_outs=()):
    nc = bass.Bass("TRN2", target_bir_lowering=False)

    xT = nc.dram_tensor("xT", [DM, S], F32, kind="ExternalInput").ap()
    xrows = nc.dram_tensor("xrows", [TQ, DM], F32, kind="ExternalInput").ap()
    wqkvT = nc.dram_tensor("wqkvT", [DM, 1536], F32, kind="ExternalInput").ap()
    wqn = nc.dram_tensor("wqn", [HD, 1], F32, kind="ExternalInput").ap()
    wkn = nc.dram_tensor("wkn", [HD, 1], F32, kind="ExternalInput").ap()
    slopes = nc.dram_tensor("slopes", [128, HPC], F32, kind="ExternalInput").ap()
    woT = nc.dram_tensor("woT", [DM, DM], F32, kind="ExternalInput").ap()
    wcatT = nc.dram_tensor("wcatT", [DM, 24], F32, kind="ExternalInput").ap()
    basetile = nc.dram_tensor("basetile", [128, 24], F32, kind="ExternalInput").ap()
    gatet = nc.dram_tensor("gatet", [128, 1], F32, kind="ExternalInput").ap()
    wbranch = nc.dram_tensor("wbranch", [1024, 1024], F32, kind="ExternalInput").ap()
    eye16 = nc.dram_tensor("eye16", [128, 16], F32, kind="ExternalInput").ap()
    rowsel = nc.dram_tensor("rowsel", [128, 4, 32], U32, kind="ExternalInput").ap()

    out = nc.dram_tensor("out", [TQ, DM], F32, kind="ExternalOutput").ap()
    dbg = {}
    for name, shape in (("qfm", [HPC * HD, S]), ("kfm", [KVPC * HD, S]),
                        ("v", [S, KVPC * 129]), ("attn", [S, HPC * HD]),
                        ("s1", [TQ, DM]), ("h1", [TQ, DM]), ("zb", [TQ, 24]),
                        ("hres", [TQ, 16]), ("consts", [128, 16])):
        if name in debug_outs:
            dbg[name] = nc.dram_tensor("dbg_" + name, shape, F32, kind="ExternalOutput").ap()

    wqkv_bf = nc.dram_tensor("wqkv_bf", [DM, 1536], BF16, kind="Internal").ap()
    woT_bf = nc.dram_tensor("woT_bf", [DM, DM], BF16, kind="Internal").ap()
    wb_bf = nc.dram_tensor("wb_bf", [1024, 1024], BF16, kind="Internal").ap()
    cin = nc.dram_tensor("cin", [S, HPC * HD], BF16, kind="Internal").ap()
    cout = nc.dram_tensor("cout", [4, S, HPC * HD], BF16, kind="Internal").ap()

    ctx = ExitStack()
    with ctx:
        sbt = lambda name, shape, dt=F32: ctx.enter_context(nc.sbuf_tensor(name, shape, dt))
        identB = sbt("identB", [128, 128], BF16)
        identF = sbt("identF", [128, 128], F32)
        cq = sbt("cq", [128, 1])
        slp = sbt("slp", [128, HPC])
        strips = sbt("strips", [128, HPC, NT])
        offs = sbt("offs", [128, HPC, 64])
        maskS = sbt("maskS", [128, 4, 512], BF16)
        maskT = sbt("maskT", [128, 4, 512], BF16)
        wc_sb = sbt("wc_sb", [128, 32, 24], BF16)
        base_s = sbt("base_s", [128, 24])
        base10 = sbt("base10", [128, 16])
        eye_s = sbt("eye_s", [128, 16])
        gsig = sbt("gsig", [128, 4])
        onesb = sbt("onesb", [1, 128], BF16)
        epss = sbt("epss", [128, 4])
        rsel = ctx.enter_context(nc.sbuf_tensor("rsel", [128, 4, 32], U32))
        smalls = sbt("smalls", [128, 4])

        s_init = Sem(ctx, nc, "s_init")
        s_cst = Sem(ctx, nc, "s_cst")
        s_dbg = Sem(ctx, nc, "s_dbg")
        n_dbg = [0]

        with nc.Block() as blk:
            s_gp = Sem(ctx, nc, "s_gp")

            def g_body(g):
                nms = 0
                g.memset(identB[:, :], 1.0).then_inc(s_gp.h, 1); nms += 1
                g.memset(identF[:, :], 1.0).then_inc(s_gp.h, 1); nms += 1
                for i in range(4):
                    g.memset(maskS[:, i, :], 0.0).then_inc(s_gp.h, 1); nms += 1
                    g.memset(maskT[:, i, :], 0.0).then_inc(s_gp.h, 1); nms += 1
                g.memset(epss[:, 0:1], EPS_QK)
                g.memset(epss[:, 1:2], EPS_LN)
                g.memset(epss[:, 2:3], RMS_EPS)
                g.memset(epss[:, 3:4], -8.0)
                g.memset(onesb[:, :], 1.0)
                s_init.inc(g.iota(strips[:, 0, :], [[128, NT]], base=0,
                                  channel_multiplier=1,
                                  allow_small_or_imprecise_dtypes=True))
                s_init.inc(g.iota(offs[:, 0, :], [[64, 64]], base=63,
                                  channel_multiplier=0,
                                  allow_small_or_imprecise_dtypes=True))
                g.wait_ge(s_gp.h, nms)
                s_gp.val += nms
                g.affine_select(identB[:, :], identB[:, :], [[1, 128]], ALU.is_equal,
                                0.0, base=0, channel_multiplier=-1)
                g.affine_select(identF[:, :], identF[:, :], [[1, 128]], ALU.is_equal,
                                0.0, base=0, channel_multiplier=-1)
                for i in range(4):
                    g.affine_select(maskS[:, i, :], maskS[:, i, :], [[-1, 512]],
                                    ALU.is_ge, NEG, base=128 * i, channel_multiplier=1)
                    g.affine_select(maskT[:, i, :], maskT[:, i, :], [[1, 512]],
                                    ALU.is_ge, NEG, base=-128 * i, channel_multiplier=-1)
            blk.gpsimd(g_body)

            s_ld0 = Sem(ctx, nc, "s_ld0")
            s_ldg = Sem(ctx, nc, "s_ldg")

            def s_body(s):
                s_ld0.inc(s.dma_start(slp[:, :], slopes), 16)
                s_ld0.inc(s.dma_start(smalls[:, 0:1], wqn), 16)
                s_ld0.inc(s.dma_start(smalls[:, 1:2], wkn), 16)
                s_ld0.inc(s.dma_start(base_s[:, :], basetile), 16)
                s_ld0.inc(s.dma_start(eye_s[:, :], eye16), 16)
                s_ld0.inc(s.dma_start(gsig[:, 0:1], gatet), 16)
                s_ld0.inc(s.dma_start(rsel[:, :, :], rowsel), 16)
            blk.sync(s_body)

            def gc_body(g):
                s_ldg.inc(g.dma_start(wc_sb[:, :, :],
                                      wcatT.rearrange("(c p) o -> p c o", p=128)), 16)
            blk.gpsimd(gc_body)

            def sc0_body(sc):
                sc.wait_ge(s_ld0.h, s_ld0.val)
                s_init.inc(sc.activation(gsig[:, 2:3], gsig[:, 0:1], AF.Exp, scale=-1.0))
            blk.scalar(sc0_body)

            def v0_body(v):
                v.wait_ge(s_ld0.h, s_ld0.val)
                v.wait_ge(s_init.h, s_init.val)
                v.tensor_scalar_add(gsig[:, 2:3], gsig[:, 2:3], 1.0)
                v.drain()
                v.reciprocal(gsig[:, 0:1], gsig[:, 2:3])
                v.tensor_mul(cq[:, :], smalls[:, 0:1], smalls[:, 1:2])
                v.drain()
                v.tensor_scalar(gsig[:, 1:2], gsig[:, 0:1], -1.0, 2.0, ALU.mult, ALU.add)
                v.tensor_scalar_mul(cq[:, :], cq[:, :], 1.0 / math.sqrt(HD))
                v.drain()
                for h in range(HPC - 1, -1, -1):
                    v.tensor_scalar_mul(strips[:, h, :], strips[:, 0, :], slp[:, h:h + 1])
                    v.tensor_scalar_mul(offs[:, h, :], offs[:, 0, :], slp[:, h:h + 1])
                s_init.inc(v.tensor_scalar_mul(base10[:, :], base_s[:, 0:16], 1.0 / 0.1))
            blk.vector(v0_body)

        with ExitStack() as p4s:
            e4 = p4s.enter_context
            blk = e4(nc.Block())
            cb0 = e4(nc.sbuf_tensor("castb0", [128, 2048], BF16))
            cb1 = e4(nc.sbuf_tensor("castb1", [128, 2048], BF16))
            cbs = [cb0, cb1]
            jobs = [(wqkvT, wqkv_bf, DM, 1536), (woT, woT_bf, DM, DM),
                    (wbranch, wb_bf, 1024, 1024)]
            s_ca = PingSem(ctx, nc, "s_ca", 2)
            s_csp = PingSem(ctx, nc, "s_csp", 2)
            ncast = [0]

            def gcast(g):
                k = 0
                for src, dst, rows, cols in jobs:
                    src3 = src.rearrange("(n p) c -> n p c", p=128)
                    for n in range(rows // 128):
                        for c0 in range(0, cols, 2048):
                            cw = min(2048, cols - c0)
                            if k >= 2:
                                s_csp.wait(g, k - 2)
                            s_ca.inc(g.dma_start(cbs[k % 2][:, 0:cw],
                                                 src3[n, :, c0:c0 + cw]), k)
                            k += 1
                ncast[0] = k
            blk.gpsimd(gcast)

            def scast(s):
                k = 0
                for src, dst, rows, cols in jobs:
                    dst3 = dst.rearrange("(n p) c -> n p c", p=128)
                    for n in range(rows // 128):
                        for c0 in range(0, cols, 2048):
                            cw = min(2048, cols - c0)
                            s_ca.wait(s, k)
                            s_csp.inc(s.dma_start(dst3[n, :, c0:c0 + cw],
                                                  cbs[k % 2][:, 0:cw]), k)
                            k += 1
            blk.sync(scast)

        with nc.sbuf_tensor("qfm", [128, HPC, S], BF16) as qfm, \
                nc.sbuf_tensor("kfm", [128, KVPC, S], BF16) as kfm, \
                nc.sbuf_tensor("vp", [128, KVPC, NT, 129], BF16) as vp:

            if "p1" in phases:
                with nc.Block() as blk, \
                        nc.sbuf_tensor("xts", [128, 2, 32, 256], BF16) as xts, \
                        nc.sbuf_tensor("wbb", [128, 2, 32, 256], BF16) as wbb, \
                        nc.sbuf_tensor("qsc", [128, 256], BF16) as qsc, \
                        nc.sbuf_tensor("qtm", [128, 16, 256], BF16) as qtm, \
                        nc.sbuf_tensor("ssb", [128, 8], F32) as ssb, \
                        nc.psum_tensor("pa0", [128, 256], F32) as pa0, \
                        nc.psum_tensor("pa1", [128, 256], F32) as pa1, \
                        nc.psum_tensor("pa2", [128, 256], F32) as pa2, \
                        nc.psum_tensor("pa3", [128, 256], F32) as pa3, \
                        nc.psum_tensor("ptp0", [128, 128], BF16) as ptp0, \
                        nc.psum_tensor("ptp1", [128, 128], BF16) as ptp1:
                    paccs = [pa0, pa1, pa2, pa3]
                    ptps = [ptp0, ptp1]
                    s_x = PingSem(ctx, nc, "s_x", 2)
                    s_w = PingSem(ctx, nc, "s_w", 2)
                    s_mm = Sem(ctx, nc, "s_mm")
                    s_sq = Sem(ctx, nc, "s_sq")
                    s_nrm = Sem(ctx, nc, "s_nrm")
                    s_tp = Sem(ctx, nc, "s_tp")
                    s_ev = Sem(ctx, nc, "s_ev")

                    xT3 = xT.rearrange("(c p) t -> p c t", p=128)
                    wq3 = wqkv_bf.rearrange("(c p) o -> p c o", p=128)

                    def tile_info(n):
                        tc, r = divmod(n, 12)
                        oh, t128 = divmod(r, 2)
                        return tc, oh, t128

                    ntp_before = [0] * 194
                    for n in range(192):
                        _, oh, _ = tile_info(n)
                        ntp_before[n + 1] = ntp_before[n] + (0 if oh == 5 else 2)

                    def gx(g):
                        g.wait_ge(s_csp.s[0], 16 * 52)
                        g.wait_ge(s_csp.s[1], 16 * 52)
                        g.memset(vp[:, 0, :, 128:129], 1.0)
                        g.memset(vp[:, 1, :, 128:129], 1.0)
                        for tc in range(_NTCB):
                            if tc >= 2 and _P1MODE not in ("mm",):
                                g.wait_ge(s_nrm.h, (tc - 1) * 12)
                            s_x.inc(g.dma_start(xts[:, tc % 2, :, :],
                                                xT3[:, :, 256 * tc:256 * tc + 256]), tc)
                    blk.gpsimd(gx)

                    def sw(s):
                        s.wait_ge(s_csp.s[0], 16 * 52)
                        s.wait_ge(s_csp.s[1], 16 * 52)
                        k = 0
                        for tc in range(_NTCB):
                            for oh in range(6):
                                if k >= 2:
                                    s.wait_ge(s_mm.h, max(0, 2 * k - 2))
                                c0 = 256 * (tc * 0 + oh)
                                s_w.inc(s.dma_start(wbb[:, k % 2, :, :],
                                                    wq3[:, :, c0:c0 + 256]), k)
                                k += 1
                    blk.sync(sw)

                    def emit_transposes(t, tcb):
                        for r in range(12):
                            n = tcb * 12 + r
                            _, oh, _ = tile_info(n)
                            if oh == 5:
                                continue
                            for hh in range(2):
                                m = ntp_before[n] + hh
                                if m >= 2:
                                    t.wait_ge(s_ev.h, m - 1)
                                t.wait_ge(s_nrm.h, n + 1)
                                s_tp.inc(t.matmul(ptps[m % 2][:, :],
                                         qtm[:, n % 16, 128 * hh:128 * hh + 128],
                                         identB[:, :], is_transpose=True,
                                         start=True, stop=True))

                    def emit_evacs(sc, tcb):
                        for r in range(12):
                            n = tcb * 12 + r
                            tc2, oh, t128 = tile_info(n)
                            if oh == 5:
                                continue
                            gtok = tc2 * 2 + t128
                            for hh in range(2):
                                m = ntp_before[n] + hh
                                sc.wait_ge(s_tp.h, m + 1)
                                if oh < 4:
                                    head = oh * 2 + hh
                                    ei = sc.activation(qfm[:, head, 128 * gtok:128 * gtok + 128],
                                                       ptps[m % 2][:, :], AF.Copy,
                                                       scale=cq[:, :])
                                else:
                                    ei = sc.activation(kfm[:, hh, 128 * gtok:128 * gtok + 128],
                                                       ptps[m % 2][:, :], AF.Copy)
                                s_ev.inc(ei)

                    for tcb in range(_NTCB):
                        def t_body(t, tcb=tcb):
                            if tcb > 0 and _P1MODE in ("full", "notp_dbg"):
                                emit_transposes(t, tcb - 1)
                            s_x.wait(t, tcb)
                            for r in range(12):
                                n = tcb * 12 + r
                                oh, t128 = divmod(r, 2)
                                s_w.wait(t, tcb * 6 + oh)
                                if n >= 4:
                                    t.wait_ge(s_nrm.h, n - 3)
                                acc = paccs[n % 4]
                                for d in range(32):
                                    mi = t.matmul(acc[:, :],
                                                  xts[:, tcb % 2, d, 128 * t128:128 * t128 + 128],
                                                  wbb[:, (tcb * 6 + oh) % 2, d, :],
                                                  start=(d == 0), stop=(d == 31))
                                s_mm.inc(mi)
                        blk.tensor(t_body)

                        def sc_body(sc, tcb=tcb):
                            if tcb > 0 and _P1MODE == "full":
                                emit_evacs(sc, tcb - 1)
                            if _P1MODE == "mm":
                                for r in range(12):
                                    sc.wait_ge(s_mm.h, tcb * 12 + r + 1)
                                    s_sq.inc(sc.nop())
                                return
                            for r in range(12):
                                n = tcb * 12 + r
                                _, oh, _ = tile_info(n)
                                acc = paccs[n % 4]
                                sc.wait_ge(s_mm.h, n + 1)
                                if n >= 4:
                                    sc.wait_ge(s_nrm.h, n - 3)
                                if oh != 5:
                                    sl = 2 * (n % 4)
                                    if _P1MODE != "mmln":
                                        for hh in range(2):
                                            sc.activation(qsc[:, 128 * hh:128 * hh + 128],
                                                          acc[:, 128 * hh:128 * hh + 128],
                                                          AF.Square,
                                                          accum_out=ssb[:, sl + hh:sl + hh + 1])
                                    if _P1MODE != "mmsqonly":
                                        sc.drain()
                                        sc.activation(ssb[:, sl:sl + 2], ssb[:, sl:sl + 2],
                                                      AF.Ln, scale=1.0 / HD, bias=epss[:, 0:1])
                                        sc.drain()
                                        li = sc.activation(ssb[:, sl:sl + 2], ssb[:, sl:sl + 2],
                                                           AF.Exp, scale=-0.5)
                                        s_sq.inc(li)
                                    else:
                                        sc.drain()
                                        s_sq.inc(sc.nop())
                                else:
                                    sc.drain()
                                    s_sq.inc(sc.nop())
                        blk.scalar(sc_body)

                        def v_body(v, tcb=tcb):
                            if _P1MODE in ("mm", "mmsq"):
                                for r in range(12):
                                    v.wait_ge(s_sq.h, tcb * 12 + r + 1)
                                    s_nrm.inc(v.nop())
                                return
                            for r in range(12):
                                n = tcb * 12 + r
                                tc2, oh, t128 = tile_info(n)
                                acc = paccs[n % 4]
                                v.wait_ge(s_sq.h, n + 1)
                                gtok = tc2 * 2 + t128
                                if oh != 5:
                                    if n >= 16 and _P1MODE == "full":
                                        v.wait_ge(s_tp.h, ntp_before[n - 15])
                                    sl = 2 * (n % 4)
                                    for hh in range(2):
                                        vi = v.tensor_scalar_mul(qtm[:, n % 16, 128 * hh:128 * hh + 128],
                                                                 acc[:, 128 * hh:128 * hh + 128],
                                                                 ssb[:, sl + hh:sl + hh + 1])
                                else:
                                    for hh in range(2):
                                        vi = v.tensor_copy(vp[:, hh, gtok, 0:128],
                                                           acc[:, 128 * hh:128 * hh + 128])
                                s_nrm.inc(vi)
                        blk.vector(v_body)

                    if _P1MODE == "full":
                        blk.tensor(lambda t: emit_transposes(t, _NTCB - 1))
                        blk.scalar(lambda sc: emit_evacs(sc, _NTCB - 1))

                if dbg.keys() & {"qfm", "kfm", "v", "consts"}:
                    with nc.Block() as blk:
                        def dqkv(g):
                            if "qfm" in dbg:
                                q2 = dbg["qfm"].rearrange("(h p) t -> p h t", p=128)
                                s_dbg.inc(g.dma_start(q2, qfm[:, :, :]), 16)
                                n_dbg[0] += 1
                            if "kfm" in dbg:
                                k2 = dbg["kfm"].rearrange("(h p) t -> p h t", p=128)
                                s_dbg.inc(g.dma_start(k2, kfm[:, :, :]), 16)
                                n_dbg[0] += 1
                            if "consts" in dbg:
                                s_dbg.inc(g.dma_start(dbg["consts"][:, 0:4], smalls[:, :]), 16)
                                s_dbg.inc(g.dma_start(dbg["consts"][:, 4:5], cq[:, :], allow_slow_non_contiguous=True), 16)
                                s_dbg.inc(g.dma_start(dbg["consts"][:, 5:9], gsig[:, :]), 16)
                                s_dbg.inc(g.dma_start(dbg["consts"][:, 9:13], epss[:, :]), 16)
                            if "v" in dbg:
                                v2 = dbg["v"].rearrange("(n p) (g c) -> p n g c", p=128, g=KVPC)
                                for gg in range(KVPC):
                                    s_dbg.inc(g.dma_start(
                                        v2[:, :, gg, :], vp[:, gg, :, :]), 16)
                                n_dbg[0] += 2
                            g.wait_ge(s_dbg.h, s_dbg.val)
                        blk.gpsimd(dqkv)

            if "p2" in phases:
                with nc.sbuf_tensor("aout", [128, NT, HPC, 128], BF16) as aout:
                    with ExitStack() as p2s:
                        e2 = p2s.enter_context
                        blk = e2(nc.Block())
                        ptb = e2(nc.sbuf_tensor("ptb", [128, 2, 512], BF16))
                        negmT = e2(nc.sbuf_tensor("negmT", [1, HPC, 512], BF16))
                        statb = e2(nc.sbuf_tensor("statb", [128, 32, 64], F32))
                        negm = e2(nc.sbuf_tensor("negm", [128, NSUP, HPC, 4], F32))
                        rcp = e2(nc.sbuf_tensor("rcp", [128, 4], F32))
                        pst0 = e2(nc.psum_tensor("pst0", [128, 512], F32))
                        pst1 = e2(nc.psum_tensor("pst1", [128, 512], F32))
                        pss0 = e2(nc.psum_tensor("pss0", [128, 512], F32))
                        pss1 = e2(nc.psum_tensor("pss1", [128, 512], F32))
                        ppv0 = e2(nc.psum_tensor("ppv0", [128, 129], F32))
                        ppv1 = e2(nc.psum_tensor("ppv1", [128, 129], F32))
                        ppv2 = e2(nc.psum_tensor("ppv2", [128, 129], F32))
                        ppv3 = e2(nc.psum_tensor("ppv3", [128, 129], F32))
                        psts = [pst0, pst1]
                        psss = [pss0, pss1]
                        ppvs = [ppv0, ppv1, ppv2, ppv3]
                        s_sa = Sem(ctx, nc, "s_sa")
                        s_sv = Sem(ctx, nc, "s_sv")
                        s_nm = Sem(ctx, nc, "s_nm")
                        s_mtT = Sem(ctx, nc, "s_mtT")
                        s_mtE = Sem(ctx, nc, "s_mtE")
                        s_st = Sem(ctx, nc, "s_st")
                        s_pt = Sem(ctx, nc, "s_pt")
                        s_pv = Sem(ctx, nc, "s_pv")
                        s_fin = Sem(ctx, nc, "s_fin")

                        na = [0]
                        nb = [0]
                        nfin = [0]

                        for s in range(NSUP):
                            def tA(t, s=s):
                                for h in range(HPC):
                                    gk = h // 4
                                    for i in range(4):
                                        q0 = 512 * s + 128 * i
                                        for c in range(s + 1):
                                            n = na[0]
                                            bank = psss[n % 2]
                                            if n >= 2:
                                                t.wait_ge(s_sv.h, n - 1)
                                            diag = (c == s)
                                            mi = t.matmul(bank[:, :], qfm[:, h, q0:q0 + 128],
                                                          kfm[:, gk, 512 * c:512 * c + 512],
                                                          start=True, stop=not diag)
                                            if diag:
                                                mi = t.matmul(bank[:, :], identB[:, :],
                                                              maskS[:, i, :], start=False,
                                                              stop=True)
                                            s_sa.inc(mi)
                                            na[0] += 1
                            blk.tensor(tA)

                            def vA(v, s=s):
                                nloc = na[0] - 32 * (s + 1)
                                for h in range(HPC):
                                    for i in range(4):
                                        for c in range(s + 1):
                                            bank = psss[nloc % 2]
                                            v.wait_ge(s_sa.h, nloc + 1)
                                            s_sv.inc(v.tensor_reduce(
                                                statb[:, 4 * h + i, 8 * c:8 * c + 8],
                                                bank[:, :].rearrange("p (a b) -> p a b", b=64),
                                                AX.X, ALU.max))
                                            nloc += 1
                                v.drain()
                                for h in range(HPC):
                                    for i in range(4):
                                        v.tensor_add(statb[:, 4 * h + i, 0:8 * (s + 1)],
                                                     statb[:, 4 * h + i, 0:8 * (s + 1)],
                                                     offs[:, h, 0:8 * (s + 1)])
                                v.drain()
                                for h in range(HPC):
                                    for i in range(4):
                                        ni = v.tensor_reduce(negm[:, s, h, i:i + 1],
                                                             statb[:, 4 * h + i, 0:8 * (s + 1)],
                                                             AX.X, ALU.max, negate=True)
                                    s_nm.inc(ni)
                            blk.vector(vA)

                            def tM(t, s=s):
                                for h in range(HPC):
                                    t.wait_ge(s_nm.h, s * 8 + h + 1)
                                    if s * 8 + h >= 1:
                                        t.wait_ge(s_mtE.h, s * 8 + h)
                                    for i in range(4):
                                        ti = t.matmul(psss[0][0:1, 128 * i:128 * i + 128],
                                                      negm[:, s, h, i:i + 1],
                                                      identF[:, :], is_transpose=True,
                                                      start=(i == 0), stop=(i == 3))
                                    s_mtT.inc(ti)
                            blk.tensor(tM)

                            def sM(sc, s=s):
                                for h in range(HPC):
                                    sc.wait_ge(s_mtT.h, s * 8 + h + 1)
                                    s_mtE.inc(sc.activation(
                                        negmT[0:1, h, :], psss[0][0:1, :],
                                        AF.Copy, bias=-8.0))
                            blk.scalar(sM)

                            nch = 4 * (s + 1)
                            for h in range(HPC):
                                gk = h // 4
                                nb0 = nb[0]

                                def tB(t, s=s, h=h, gk=gk, nb0=nb0, nch=nch,
                                       fin0=nfin[0]):
                                    t.wait_ge(s_mtE.h, s * 8 + h + 1)
                                    if fin0 >= 1:
                                        t.wait_ge(s_fin.h, fin0)
                                    for c in range(nch):
                                        n = nb0 + c
                                        bank = psts[n % 2]
                                        if n >= 2:
                                            t.wait_ge(s_pt.h, n - 1)
                                        t.matmul(bank[:, :], onesb[0:1, :],
                                                 negmT[0:1, h, :],
                                                 start=True, stop=False)
                                        diag = (c >= nch - 4)
                                        mi = t.matmul(bank[:, :],
                                                      kfm[:, gk, 128 * c:128 * c + 128],
                                                      qfm[:, h, 512 * s:512 * s + 512],
                                                      start=False, stop=not diag)
                                        if diag:
                                            mi = t.matmul(bank[:, :], identB[:, :],
                                                          maskT[:, c - (nch - 4), :],
                                                          start=False, stop=True)
                                        s_st.inc(mi)
                                        if c >= 1:
                                            t.wait_ge(s_pt.h, nb0 + c)
                                            for i in range(4):
                                                pvi = t.matmul(
                                                    ppvs[i][:, :],
                                                    ptb[:, (nb0 + c - 1) % 2,
                                                        128 * i:128 * i + 128],
                                                    vp[:, gk, c - 1, :],
                                                    start=(c == 1), stop=False)
                                            s_pv.inc(pvi)
                                    t.wait_ge(s_pt.h, nb0 + nch)
                                    for i in range(4):
                                        pvi = t.matmul(ppvs[i][:, :],
                                                       ptb[:, (nb0 + nch - 1) % 2,
                                                           128 * i:128 * i + 128],
                                                       vp[:, gk, nch - 1, :],
                                                       start=(nch == 1), stop=True)
                                    s_pv.inc(pvi)
                                blk.tensor(tB)

                                def sB(sc, s=s, h=h, nb0=nb0, nch=nch):
                                    for c in range(nch):
                                        n = nb0 + c
                                        bank = psts[n % 2]
                                        sc.wait_ge(s_st.h, n + 1)
                                        if n >= 2:
                                            sc.wait_ge(s_pv.h, n - 1)
                                        s_pt.inc(sc.activation(
                                            ptb[:, n % 2, :], bank[:, :],
                                            AF.Exp, bias=strips[:, h, c:c + 1]))
                                blk.scalar(sB)

                                def vB(v, s=s, h=h, nb0=nb0, nch=nch):
                                    v.wait_ge(s_pv.h, nb0 + nch)
                                    for i in range(4):
                                        v.reciprocal(rcp[:, i:i + 1],
                                                     ppvs[i][:, 128:129])
                                    v.drain()
                                    for i in range(4):
                                        fi = v.tensor_scalar_mul(aout[:, 4 * s + i, h, :],
                                                                 ppvs[i][:, 0:128],
                                                                 rcp[:, i:i + 1])
                                    s_fin.inc(fi)
                                blk.vector(vB)
                                nfin[0] += 1
                                nb[0] += nch

                    with nc.Block() as blk:
                        s_ci = Sem(ctx, nc, "s_ci")
                        s_cc = Sem(ctx, nc, "s_cc")

                        def gex(g):
                            cin4 = cin.rearrange("(n p) (h d) -> p n h d", p=128, h=HPC)
                            s_ci.inc(g.dma_start(cin4, aout[:, :, :, :]), 16)
                            g.wait_ge(s_ci.h, 16)
                            s_cc.inc(g.collective_compute(
                                "AllGather", ALU.bypass,
                                replica_groups=[[0, 1, 2, 3], [4, 5, 6, 7]],
                                ins=[cin], outs=[cout]), 1)
                            g.wait_ge(s_cc.h, 1)
                        blk.gpsimd(gex)

                    if "attn" in dbg:
                        with nc.Block() as blk:
                            def datt(g):
                                a2 = dbg["attn"].rearrange("(n p) f -> p n f", p=128)
                                s_dbg.inc(g.dma_start(
                                    a2, aout.ap().rearrange("p n h d -> p n (h d)")), 16)
                                g.wait_ge(s_dbg.h, s_dbg.val)
                            blk.gpsimd(datt)

        if "p3" in phases:
            s_bf = ctx.enter_context(nc.sbuf_tensor("s_bf", [128, 8, DM], BF16))
            lnst = ctx.enter_context(nc.sbuf_tensor("lnst", [128, 8, 4], F32))

            with ExitStack() as p3s:
                e3 = p3s.enter_context
                blk = e3(nc.Block())
                afm = e3(nc.sbuf_tensor("afm", [128, 32, 1024], BF16))
                attm = e3(nc.sbuf_tensor("attm", [128, 8, 1024], BF16))
                wsb = e3(nc.sbuf_tensor("wsb", [128, 2, 32, 256], BF16))
                xrb = e3(nc.sbuf_tensor("xrb", [128, 4, 256], F32))
                ptq0 = e3(nc.psum_tensor("ptq0", [128, 512], BF16))
                ptq1 = e3(nc.psum_tensor("ptq1", [128, 512], BF16))
                py0 = e3(nc.psum_tensor("py0", [128, 256], F32))
                py1 = e3(nc.psum_tensor("py1", [128, 256], F32))
                py2 = e3(nc.psum_tensor("py2", [128, 256], F32))
                py3 = e3(nc.psum_tensor("py3", [128, 256], F32))
                ptqs = [ptq0, ptq1]
                pys = [py0, py1, py2, py3]
                s_at = Sem(ctx, nc, "s_at")
                s_tq = Sem(ctx, nc, "s_tq")
                s_ae = Sem(ctx, nc, "s_ae")
                s_wl = PingSem(ctx, nc, "s_wl", 2)
                s_xr = PingSem(ctx, nc, "s_xr", 4)
                s_ym = Sem(ctx, nc, "s_ym")
                s_ye = Sem(ctx, nc, "s_ye")

                coutf = cout.rearrange("r n f -> (r n) f")

                def g_loads(g):
                    g.wait_ge(s_ld0.h, s_ld0.val)
                    for r in range(4):
                        if r >= 1:
                            g.wait_ge(s_tq.h, 16 * r)
                        for tt in range(8):
                            s_at.inc(g.indirect_dma_start(
                                attm[:, tt, :], None, coutf,
                                IndirectOffsetOnAxis(rsel[:, r, tt:tt + 1], 0)), 16)
                blk.gpsimd(g_loads)

                def t_tr(t):
                    k = 0
                    for r in range(4):
                        t.wait_ge(s_at.h, 16 * 8 * (r + 1))
                        for tt in range(8):
                            for u in range(2):
                                m = 2 * k + u
                                if m >= 2:
                                    t.wait_ge(s_ae.h, m - 1)
                                for q in range(4):
                                    fc = 4 * u + q
                                    ti = t.matmul(ptqs[m % 2][:, 128 * q:128 * q + 128],
                                                  attm[:, tt, 128 * fc:128 * fc + 128],
                                                  identB[:, :], is_transpose=True,
                                                  start=(q == 0), stop=(q == 3))
                                s_tq.inc(ti)
                            k += 1
                blk.tensor(t_tr)

                def sc_ae(sc):
                    k = 0
                    for r in range(4):
                        for tt in range(8):
                            for u in range(2):
                                m = 2 * k + u
                                sc.wait_ge(s_tq.h, m + 1)
                                fc0 = 8 * r + 4 * u
                                s_ae.inc(sc.activation(
                                    afm[:, fc0:fc0 + 4, 128 * tt:128 * tt + 128],
                                    ptqs[m % 2][:, :].rearrange("p (a b) -> p a b", b=128),
                                    AF.Copy))
                            k += 1
                blk.scalar(sc_ae)

                wo3 = woT_bf.rearrange("(c p) m -> p c m", p=128)

                def s_wload(s):
                    for mh in range(16):
                        if mh >= 2:
                            s.wait_ge(s_ym.h, (mh - 1) * 8)
                        s_wl.inc(s.dma_start(wsb[:, mh % 2, :, :],
                                             wo3[:, :, 256 * mh:256 * mh + 256]), mh)
                blk.sync(s_wload)

                def s_xload(g):
                    for mh in range(16):
                        for tt in range(8):
                            k = mh * 8 + tt
                            if k >= 4:
                                g.wait_ge(s_ye.h, k - 3)
                            s_xr.inc(g.dma_start(
                                xrb[:, k % 4, :],
                                xrows[128 * tt:128 * tt + 128,
                                      256 * mh:256 * mh + 256]), k)
                blk.gpsimd(s_xload)

                def t_wo(t):
                    t.wait_ge(s_ae.h, 64)
                    for mh in range(16):
                        s_wl.wait(t, mh)
                        for tt in range(8):
                            k = mh * 8 + tt
                            acc = pys[k % 4]
                            if k >= 4:
                                t.wait_ge(s_ye.h, k - 3)
                            for fc in range(32):
                                yi = t.matmul(acc[:, :], afm[:, fc, 128 * tt:128 * tt + 128],
                                              wsb[:, mh % 2, fc, :],
                                              start=(fc == 0), stop=(fc == 31))
                            s_ym.inc(yi)
                blk.tensor(t_wo)

                def v_wo(v):
                    for mh in range(16):
                        for tt in range(8):
                            k = mh * 8 + tt
                            acc = pys[k % 4]
                            v.wait_ge(s_ym.h, k + 1)
                            s_xr.wait(v, k)
                            s_ye.inc(v.scalar_tensor_tensor(
                                s_bf[:, tt, 256 * mh:256 * mh + 256], acc[:, :],
                                1.0, xrb[:, k % 4, :], ALU.mult, ALU.add))
                blk.vector(v_wo)

            if "s1" in dbg:
                with nc.Block() as blk:
                    def ds1(g):
                        d2 = dbg["s1"].rearrange("(n p) f -> p n f", p=128)
                        s_dbg.inc(g.dma_start(d2, s_bf[:, :, :]), 16)
                        g.wait_ge(s_dbg.h, s_dbg.val)
                    blk.gpsimd(ds1)

        if "p4" in phases:
            with ExitStack() as p4b:
                eb4 = p4b.enter_context
                blk = eb4(nc.Block())
                junk = eb4(nc.sbuf_tensor("junk", [128, DM], BF16))
                stat4 = eb4(nc.sbuf_tensor("stat4", [128, 32], F32))
                h1f = eb4(nc.sbuf_tensor("h1f", [128, 32, 128], BF16))
                zbuf = eb4(nc.sbuf_tensor("zbuf", [128, 8, 24], F32))
                z0 = eb4(nc.sbuf_tensor("z0", [128, 8, 16], F32))
                zp = eb4(nc.sbuf_tensor("zp", [128, 8, 16], F32))
                tsk = eb4(nc.sbuf_tensor("tsk", [128, 128], F32))
                esk = eb4(nc.sbuf_tensor("esk", [128, 128], F32))
                msk = eb4(nc.sbuf_tensor("msk", [128, 32], F32))
                ulse = eb4(nc.sbuf_tensor("ulse", [128, 32], F32))
                vlse = eb4(nc.sbuf_tensor("vlse", [128, 32], F32))
                hres = eb4(nc.sbuf_tensor("hres", [128, 8, 16], F32))
                hpp = eb4(nc.sbuf_tensor("hpp", [128, 8, 8], F32))
                invt = eb4(nc.sbuf_tensor("invt", [128, 8, 2], F32))
                xp = eb4(nc.sbuf_tensor("xp", [128, 1024], BF16))
                xpf = eb4(nc.sbuf_tensor("xpf", [128, 1024], F32))
                xpfm = eb4(nc.sbuf_tensor("xpfm", [128, 8, 128], BF16))
                wbsb = eb4(nc.sbuf_tensor("wbsb", [128, 8, 1024], BF16))
                ybf = eb4(nc.sbuf_tensor("ybf", [128, 8, 1024], BF16))
                s2 = eb4(nc.sbuf_tensor("s2", [128, 4, 1024], F32))
                outb = eb4(nc.sbuf_tensor("outb", [128, 2, DM], F32))
                ph0 = eb4(nc.psum_tensor("ph0", [128, 512], BF16))
                ph1 = eb4(nc.psum_tensor("ph1", [128, 512], BF16))
                plg = eb4(nc.psum_tensor("plg", [128, 24], F32))
                pyb0 = eb4(nc.psum_tensor("pyb0", [128, 512], F32))
                pyb1 = eb4(nc.psum_tensor("pyb1", [128, 512], F32))
                phs = [ph0, ph1]
                pybs = [pyb0, pyb1]
                s_sq1 = Sem(ctx, nc, "s_sq1")
                s_vst = Sem(ctx, nc, "s_vst")
                s_iv1 = Sem(ctx, nc, "s_iv1")
                s_h1 = Sem(ctx, nc, "s_h1")
                s_rm2 = Sem(ctx, nc, "s_rm2")
                s_ht = Sem(ctx, nc, "s_ht")
                s_he = Sem(ctx, nc, "s_he")
                s_lg = Sem(ctx, nc, "s_lg")
                s_zb = Sem(ctx, nc, "s_zb")
                s_sk = Sem(ctx, nc, "s_sk")
                s_xp = Sem(ctx, nc, "s_xp")
                s_xt = Sem(ctx, nc, "s_xt")
                s_xe = Sem(ctx, nc, "s_xe")
                s_yb = Sem(ctx, nc, "s_yb")
                s_yv = Sem(ctx, nc, "s_yv")
                s_s2 = Sem(ctx, nc, "s_s2")
                s_l2 = Sem(ctx, nc, "s_l2")
                s_ob = Sem(ctx, nc, "s_ob")
                s_out = PingSem(ctx, nc, "s_out", 2)
                s_wbl = Sem(ctx, nc, "s_wbl")

                def g_wb(g):
                    wb3 = wb_bf.rearrange("(c p) m -> p c m", p=128)
                    s_wbl.inc(g.dma_start(wbsb[:, :, :], wb3), 16)
                blk.gpsimd(g_wb)

                def sc_ln1(sc):
                    for tt in range(8):
                        if tt:
                            sc.drain()
                        s_sq1.inc(sc.activation(junk[:, :], s_bf[:, tt, :], AF.Square,
                                                 accum_out=stat4[:, tt:tt + 1]))
                blk.scalar(sc_ln1)

                def v_ln1(v):
                    v.wait_ge(s_ld0.h, s_ld0.val)
                    for tt in range(8):
                        v.wait_ge(s_sq1.h, tt + 1)
                        v.reduce_sum(stat4[:, 8 + tt:8 + tt + 1], s_bf[:, tt, :], AX.X)
                        v.drain()
                        v.tensor_scalar_mul(lnst[:, tt, 0:1],
                                            stat4[:, 8 + tt:8 + tt + 1], 1.0 / DM)
                        v.tensor_scalar_mul(lnst[:, tt, 3:4], stat4[:, tt:tt + 1], 1.0 / DM)
                        v.drain()
                        v.tensor_mul(lnst[:, tt, 2:3], lnst[:, tt, 0:1], lnst[:, tt, 0:1])
                        v.drain()
                        s_vst.inc(v.tensor_sub(lnst[:, tt, 3:4], lnst[:, tt, 3:4],
                                                lnst[:, tt, 2:3]))
                blk.vector(v_ln1)

                def sc_inv1(sc):
                    for tt in range(8):
                        sc.wait_ge(s_vst.h, tt + 1)
                        sc.activation(lnst[:, tt, 1:2], lnst[:, tt, 3:4], AF.Ln,
                                      bias=epss[:, 1:2])
                        sc.drain()
                        s_iv1.inc(sc.activation(lnst[:, tt, 1:2], lnst[:, tt, 1:2],
                                                 AF.Exp, scale=-0.5))
                blk.scalar(sc_inv1)

                def v_h1(v):
                    for tt in range(8):
                        v.wait_ge(s_iv1.h, tt + 1)
                        s_h1.inc(v.tensor_scalar(s_bf[:, tt, :], s_bf[:, tt, :],
                                                  lnst[:, tt, 0:1], lnst[:, tt, 1:2],
                                                  ALU.subtract, ALU.mult))
                blk.vector(v_h1)

                if "h1" in dbg:
                    def dh1(g):
                        g.wait_ge(s_h1.h, 8)
                        d2 = dbg["h1"].rearrange("(n p) f -> p n f", p=128)
                        s_dbg.inc(g.dma_start(d2, s_bf[:, :, :]), 16)
                        g.wait_ge(s_dbg.h, s_dbg.val)
                    blk.gpsimd(dh1)

                def sc_rms(sc):
                    for tt in range(8):
                        sc.wait_ge(s_h1.h, tt + 1)
                        sc.drain()
                        sc.activation(junk[:, :], s_bf[:, tt, :], AF.Square,
                                      accum_out=invt[:, tt, 0:1])
                        sc.drain()
                        sc.activation(invt[:, tt, 1:2], invt[:, tt, 0:1], AF.Ln,
                                      scale=1.0 / DM, bias=epss[:, 2:3])
                        sc.drain()
                        s_rm2.inc(sc.activation(invt[:, tt, 1:2], invt[:, tt, 1:2],
                                                 AF.Exp, scale=-0.5))
                blk.scalar(sc_rms)

                for tt in range(8):
                    def t_h1t(t, tt=tt):
                        t.wait_ge(s_h1.h, tt + 1)
                        for u in range(8):
                            m = tt * 8 + u
                            if m >= 2:
                                t.wait_ge(s_he.h, m - 1)
                            for q in range(4):
                                fc = 4 * u + q
                                hi = t.matmul(phs[m % 2][:, 128 * q:128 * q + 128],
                                              s_bf[:, tt, 128 * fc:128 * fc + 128],
                                              identB[:, :], is_transpose=True,
                                              start=(q == 0), stop=(q == 3))
                            s_ht.inc(hi)
                    blk.tensor(t_h1t)

                    def sc_h1e(sc, tt=tt):
                        if tt >= 1:
                            sc.wait_ge(s_lg.h, tt)
                        for u in range(8):
                            m = tt * 8 + u
                            sc.wait_ge(s_ht.h, m + 1)
                            s_he.inc(sc.activation(
                                h1f[:, 4 * u:4 * u + 4, :],
                                phs[m % 2][:, :].rearrange("p (a b) -> p a b", b=128),
                                AF.Copy))
                    blk.scalar(sc_h1e)

                    def t_lg(t, tt=tt):
                        if tt == 0:
                            t.wait_ge(s_ldg.h, 16)
                        t.wait_ge(s_he.h, (tt + 1) * 8)
                        if tt >= 1:
                            t.wait_ge(s_zb.h, tt)
                        for fc in range(32):
                            lgi = t.matmul(plg[:, :], h1f[:, fc, :], wc_sb[:, fc, :],
                                           start=(fc == 0), stop=(fc == 31))
                        s_lg.inc(lgi)
                    blk.tensor(t_lg)

                    def v_z(v, tt=tt):
                        v.wait_ge(s_lg.h, tt + 1)
                        v.wait_ge(s_rm2.h, tt + 1)
                        v.tensor_scalar_mul(invt[:, tt, 0:1], invt[:, tt, 1:2], 10.0)
                        v.drain()
                        v.scalar_tensor_tensor(zbuf[:, tt, 0:16], plg[:, 0:16],
                                               invt[:, tt, 0:1], base10[:, :],
                                               ALU.mult, ALU.add)
                        s_zb.inc(v.scalar_tensor_tensor(
                            zbuf[:, tt, 16:24], plg[:, 16:24],
                            invt[:, tt, 1:2], base_s[:, 16:24],
                            ALU.mult, ALU.add))
                    blk.vector(v_z)

                def v_sm(v):
                    v.wait_ge(s_zb.h, 8)
                    pre = zbuf[:, :, 16:24].rearrange("p a (n f) -> p a n f", f=4)
                    v.tensor_reduce(msk[:, 0:16].rearrange("p (a n) -> p a n", n=2),
                                    pre, AX.X, ALU.max)
                    v.drain()
                    mb = msk[:, 0:16].rearrange("p (a n) -> p a n", n=2).unsqueeze(3) \
                        .broadcast_to([128, 8, 2, 4])
                    s_sk.inc(v.tensor_sub(
                        hpp.ap().rearrange("p a (n f) -> p a n f", f=4), pre, mb))
                blk.vector(v_sm)

                def sc_sm(sc):
                    sc.wait_ge(s_sk.h, 1)
                    s_sk.inc(sc.activation(hpp[:, :, :], hpp[:, :, :], AF.Exp))
                blk.scalar(sc_sm)

                def v_sm2(v):
                    v.wait_ge(s_sk.h, 2)
                    hp4 = hpp.ap().rearrange("p a (n f) -> p a n f", f=4)
                    v.tensor_reduce(msk[:, 0:16].rearrange("p (a n) -> p a n", n=2),
                                    hp4, AX.X, ALU.add)
                    v.drain()
                    v.reciprocal(msk[:, 0:16], msk[:, 0:16])
                    v.drain()
                    mb = msk[:, 0:16].rearrange("p (a n) -> p a n", n=2).unsqueeze(3) \
                        .broadcast_to([128, 8, 2, 4])
                    v.tensor_mul(hp4, hp4, mb)
                    zr = zbuf[:, :, 0:16].rearrange("p a (i j) -> p a i j", j=4)
                    v.drain()
                    v.tensor_reduce(msk[:, 0:32].rearrange("p (a i) -> p a i", i=4),
                                    zr, AX.X, ALU.max)
                    v.drain()
                    mb2 = msk[:, 0:32].rearrange("p (a i) -> p a i", i=4).unsqueeze(3) \
                        .broadcast_to([128, 8, 4, 4])
                    v.tensor_sub(z0.ap().rearrange("p a (i j) -> p a i j", j=4), zr, mb2)
                    v.drain()
                    s_sk.inc(v.tensor_copy(
                        zp.ap().rearrange("p a (j i) -> p a j i", i=4),
                        z0.ap().rearrange("p a (i j) -> p a i j", j=4)
                        .transpose([0, 1, 3, 2])))
                blk.vector(v_sm2)

                for tt in range(8):
                    def v_xp(v, tt=tt):
                        v.wait_ge(s_sk.h, 3)
                        if tt >= 1:
                            v.wait_ge(s_xt.h, 2 * tt)
                        hp = hpp[:, tt, 0:4]
                        v.tensor_scalar_mul(xpf[:, :], s_bf[:, tt, 0:1024], hp[:, 0:1])
                        for j in (1, 2):
                            v.drain()
                            v.scalar_tensor_tensor(xpf[:, :],
                                                   s_bf[:, tt, 1024 * j:1024 * j + 1024],
                                                   hp[:, j:j + 1], xpf[:, :],
                                                   ALU.mult, ALU.add)
                        v.drain()
                        s_xp.inc(v.scalar_tensor_tensor(
                            xp[:, :], s_bf[:, tt, 3072:4096],
                            hp[:, 3:4], xpf[:, :], ALU.mult, ALU.add))
                    blk.vector(v_xp)

                    def t_xpt(t, tt=tt):
                        t.wait_ge(s_xp.h, tt + 1)
                        for u in range(2):
                            m = 2 * tt + u
                            if m >= 2:
                                t.wait_ge(s_xe.h, m - 1)
                            for q in range(4):
                                fc = 4 * u + q
                                xi = t.matmul(phs[m % 2][:, 128 * q:128 * q + 128],
                                              xp[:, 128 * fc:128 * fc + 128],
                                              identB[:, :], is_transpose=True,
                                              start=(q == 0), stop=(q == 3))
                            s_xt.inc(xi)
                    blk.tensor(t_xpt)

                    def sc_xpe(sc, tt=tt):
                        if tt >= 1:
                            sc.wait_ge(s_yb.h, 2 * tt)
                        for u in range(2):
                            m = 2 * tt + u
                            sc.wait_ge(s_xt.h, m + 1)
                            s_xe.inc(sc.activation(
                                xpfm[:, 4 * u:4 * u + 4, :],
                                phs[m % 2][:, :].rearrange("p (a b) -> p a b", b=128),
                                AF.Copy))
                    blk.scalar(sc_xpe)

                    def t_y(t, tt=tt):
                        t.wait_ge(s_xe.h, 2 * (tt + 1))
                        if tt == 0:
                            t.wait_ge(s_wbl.h, 16)
                        for half in range(2):
                            k = 2 * tt + half
                            if k >= 2:
                                t.wait_ge(s_yv.h, k - 1)
                            for cc in range(8):
                                ybi = t.matmul(pybs[k % 2][:, :], xpfm[:, cc, :],
                                               wbsb[:, cc, 512 * half:512 * half + 512],
                                               start=(cc == 0), stop=(cc == 7))
                            s_yb.inc(ybi)
                    blk.tensor(t_y)

                    def v_y(v, tt=tt):
                        for half in range(2):
                            k = 2 * tt + half
                            v.wait_ge(s_yb.h, k + 1)
                            s_yv.inc(v.tensor_copy(
                                ybf[:, tt, 512 * half:512 * half + 512],
                                pybs[k % 2][:, :]))
                    blk.vector(v_y)

                z04 = z0.ap().rearrange("p a (i j) -> p a i j", j=4)
                zp4 = zp.ap().rearrange("p a (j i) -> p a j i", i=4)
                t4 = tsk.ap().rearrange("p (a i j) -> p a i j", i=4, j=4)
                e4 = esk.ap().rearrange("p (a i j) -> p a i j", i=4, j=4)
                m4 = msk.ap().rearrange("p (a i) -> p a i", i=4)
                ul4 = ulse.ap().rearrange("p (a i) -> p a i", i=4)
                vl4 = vlse.ap().rearrange("p (a j) -> p a j", j=4)
                skc = [3]

                def v_sink(v):
                    c = skc[0]
                    for it in range(50):
                        v.wait_ge(s_sk.h, c)
                        if it == 0:
                            v.tensor_copy(tsk[:, :], z0.ap().rearrange("p a f -> p (a f)"))
                        else:
                            vb = vl4.unsqueeze(2).broadcast_to([128, 8, 4, 4])
                            v.tensor_sub(t4, z04, vb)
                        v.drain()
                        v.tensor_reduce(m4, t4, AX.X, ALU.max)
                        v.drain()
                        mb = m4.unsqueeze(3).broadcast_to([128, 8, 4, 4])
                        s_sk.inc(v.tensor_sub(t4, t4, mb))
                        c += 2
                        v.wait_ge(s_sk.h, c)
                        s_sk.inc(v.tensor_reduce(ul4, e4, AX.X, ALU.add))
                        c += 2
                        v.wait_ge(s_sk.h, c)
                        v.tensor_add(ulse[:, :], ulse[:, :], msk[:, :])
                        v.drain()
                        ub = ul4.unsqueeze(2).broadcast_to([128, 8, 4, 4])
                        v.tensor_sub(t4, zp4, ub)
                        v.drain()
                        v.tensor_reduce(m4, t4, AX.X, ALU.max)
                        v.drain()
                        mb = m4.unsqueeze(3).broadcast_to([128, 8, 4, 4])
                        s_sk.inc(v.tensor_sub(t4, t4, mb))
                        c += 2
                        v.wait_ge(s_sk.h, c)
                        s_sk.inc(v.tensor_reduce(vl4, e4, AX.X, ALU.add))
                        c += 2
                        v.wait_ge(s_sk.h, c)
                        s_sk.inc(v.tensor_add(vlse[:, :], vlse[:, :], msk[:, :]))
                        c += 1
                    v.wait_ge(s_sk.h, c)
                    ub = ul4.unsqueeze(3).broadcast_to([128, 8, 4, 4])
                    vb = vl4.unsqueeze(2).broadcast_to([128, 8, 4, 4])
                    hr4 = hres.ap().rearrange("p a (i j) -> p a i j", j=4)
                    v.tensor_sub(hr4, z04, ub)
                    v.drain()
                    s_sk.inc(v.tensor_sub(hr4, hr4, vb))
                    c += 2
                    v.wait_ge(s_sk.h, c)
                    eb = eye_s[:, :].unsqueeze(1).broadcast_to([128, 8, 16])
                    v.tensor_scalar_mul(hres[:, :, :], hres[:, :, :], gsig[:, 0:1])
                    v.drain()
                    s_sk.inc(v.scalar_tensor_tensor(hres[:, :, :], eb, gsig[:, 1:2],
                                                    hres[:, :, :], ALU.mult, ALU.add))
                    c += 1
                    skc[0] = c
                blk.vector(v_sink)

                def s_sink(sc):
                    c = 3
                    for it in range(50):
                        c += 1
                        sc.wait_ge(s_sk.h, c)
                        s_sk.inc(sc.activation(esk[:, :], tsk[:, :], AF.Exp))
                        c += 1
                        c += 1
                        sc.wait_ge(s_sk.h, c)
                        s_sk.inc(sc.activation(ulse[:, :], ulse[:, :], AF.Ln))
                        c += 1
                        c += 1
                        sc.wait_ge(s_sk.h, c)
                        s_sk.inc(sc.activation(esk[:, :], tsk[:, :], AF.Exp))
                        c += 1
                        c += 1
                        sc.wait_ge(s_sk.h, c)
                        s_sk.inc(sc.activation(vlse[:, :], vlse[:, :], AF.Ln))
                        c += 1
                        c += 1
                    c += 1
                    sc.wait_ge(s_sk.h, c)
                    s_sk.inc(sc.activation(hres[:, :, :], hres[:, :, :], AF.Exp))
                    c += 1
                blk.scalar(s_sink)
                SK_DONE = s_sk.val

                if "zb" in dbg:
                    def dzb(g):
                        g.wait_ge(s_zb.h, 8)
                        d2 = dbg["zb"].rearrange("(n p) f -> p n f", p=128)
                        s_dbg.inc(g.dma_start(d2, zbuf[:, :, :]), 16)
                        g.wait_ge(s_dbg.h, s_dbg.val)
                    blk.gpsimd(dzb)
                if "hres" in dbg:
                    def dhr(g):
                        g.wait_ge(s_sk.h, SK_DONE)
                        d2 = dbg["hres"].rearrange("(n p) f -> p n f", p=128)
                        s_dbg.inc(g.dma_start(d2, hres[:, :, :]), 16)
                        g.wait_ge(s_dbg.h, s_dbg.val)
                    blk.gpsimd(dhr)

                for tt in range(8):
                    def v_mix(v, tt=tt):
                        if tt:
                            v.drain()
                        v.wait_ge(s_sk.h, SK_DONE)
                        v.wait_ge(s_yv.h, 2 * (tt + 1))
                        if tt >= 1:
                            v.wait_ge(s_l2.h, 3 * (tt - 1) + 1)
                        for i in range(4):
                            hr = hres[:, tt, :]
                            v.tensor_scalar_mul(s2[:, i, :], s_bf[:, tt, 0:1024],
                                                hr[:, 4 * i:4 * i + 1])
                            for j in (1, 2, 3):
                                v.drain()
                                v.scalar_tensor_tensor(
                                    s2[:, i, :], s_bf[:, tt, 1024 * j:1024 * j + 1024],
                                    hr[:, 4 * i + j:4 * i + j + 1],
                                    s2[:, i, :], ALU.mult, ALU.add)
                            v.drain()
                            s2i = v.scalar_tensor_tensor(s2[:, i, :], ybf[:, tt, :],
                                                          hpp[:, tt, 4 + i:4 + i + 1],
                                                          s2[:, i, :], ALU.mult, ALU.add)
                        s_s2.inc(s2i)
                    blk.vector(v_mix)

                    def sc_ln2(sc, tt=tt):
                        sc.wait_ge(s_s2.h, tt + 1)
                        sc.drain()
                        s_l2.inc(sc.activation(
                            junk[:, :], s2.ap().rearrange("p a f -> p (a f)"),
                            AF.Square, accum_out=stat4[:, 16 + tt:16 + tt + 1]))
                    blk.scalar(sc_ln2)

                    def v_ln2(v, tt=tt):
                        v.wait_ge(s_l2.h, 3 * tt + 1)
                        s2f = s2.ap().rearrange("p a f -> p (a f)")
                        v.reduce_sum(stat4[:, 24 + tt:24 + tt + 1], s2f, AX.X)
                        v.drain()
                        v.tensor_scalar_mul(lnst[:, tt, 0:1],
                                            stat4[:, 24 + tt:24 + tt + 1], 1.0 / DM)
                        v.tensor_scalar_mul(lnst[:, tt, 3:4],
                                            stat4[:, 16 + tt:16 + tt + 1], 1.0 / DM)
                        v.drain()
                        v.tensor_mul(lnst[:, tt, 2:3], lnst[:, tt, 0:1], lnst[:, tt, 0:1])
                        v.drain()
                        s_l2.inc(v.tensor_sub(lnst[:, tt, 3:4], lnst[:, tt, 3:4],
                                              lnst[:, tt, 2:3]))
                    blk.vector(v_ln2)

                    def sc_inv2(sc, tt=tt):
                        sc.wait_ge(s_l2.h, 3 * tt + 2)
                        sc.activation(lnst[:, tt, 1:2], lnst[:, tt, 3:4], AF.Ln,
                                      bias=epss[:, 1:2])
                        sc.drain()
                        s_l2.inc(sc.activation(lnst[:, tt, 1:2], lnst[:, tt, 1:2],
                                                AF.Exp, scale=-0.5))
                    blk.scalar(sc_inv2)

                    def v_out(v, tt=tt):
                        v.wait_ge(s_l2.h, 3 * tt + 3)
                        if tt >= 2:
                            s_out.wait(v, tt - 2)
                        s_ob.inc(v.tensor_scalar(outb[:, tt % 2, :],
                                                  s2.ap().rearrange("p a f -> p (a f)"),
                                                  lnst[:, tt, 0:1], lnst[:, tt, 1:2],
                                                  ALU.subtract, ALU.mult))
                    blk.vector(v_out)

                    def s_st_out(s, tt=tt):
                        s.wait_ge(s_ob.h, tt + 1)
                        s_out.inc(s.dma_start(out[128 * tt:128 * tt + 128, :],
                                              outb[:, tt % 2, :]), tt)
                    blk.sync(s_st_out)

                def s_fin_wait(s):
                    s.wait_ge(s_out.s[0], 16 * 4)
                    s.wait_ge(s_out.s[1], 16 * 4)
                blk.sync(s_fin_wait)

    return nc


def _prep_inputs(L0_attn_x, L0_attn_Wq, L0_attn_Wk, L0_attn_Wv, L0_attn_Wo,
                 L0_attn_q_norm_w, L0_attn_k_norm_w, L0_ffn_W_res, L0_ffn_W_pre,
                 L0_ffn_W_post, L0_ffn_H_res_base, L0_ffn_H_pre_base,
                 L0_ffn_H_post_base, L0_ffn_gate, L0_ffn_W_branch):
    f32 = lambda a: np.ascontiguousarray(np.asarray(a), dtype=np.float32)
    x = f32(L0_attn_x)
    WqT = f32(L0_attn_Wq).T
    WkT = f32(L0_attn_Wk).T
    WvT = f32(L0_attn_Wv).T
    WoT = np.ascontiguousarray(f32(L0_attn_Wo).T)
    wcat = np.concatenate([f32(L0_ffn_W_res), f32(L0_ffn_W_pre), f32(L0_ffn_W_post)],
                          axis=0)
    wcatT = np.ascontiguousarray(wcat.T)
    basetile = np.ascontiguousarray(np.tile(np.concatenate(
        [f32(L0_ffn_H_res_base).ravel(), f32(L0_ffn_H_pre_base).ravel(),
         f32(L0_ffn_H_post_base).ravel()])[None, :], (128, 1)), dtype=np.float32)
    gatet = np.ascontiguousarray(np.tile(f32(L0_ffn_gate).reshape(1, 1), (128, 1)),
                                 dtype=np.float32)
    wbr = f32(L0_ffn_W_branch)
    eye = np.ascontiguousarray(np.tile(np.eye(4, dtype=np.float32).reshape(1, 16),
                                       (128, 1)), dtype=np.float32)
    slopes_all = _alibi_slopes(H)
    wqn = np.ascontiguousarray(f32(L0_attn_q_norm_w).reshape(HD, 1))
    wkn = np.ascontiguousarray(f32(L0_attn_k_norm_w).reshape(HD, 1))

    xTs = [np.ascontiguousarray(x[b].T) for b in range(B_)]
    in_maps = []
    for c in range(8):
        b, j = divmod(c, 4)
        wq_sl = WqT[:, 1024 * j:1024 * j + 1024]
        wk_sl = WkT[:, 256 * j:256 * j + 256]
        wv_sl = WvT[:, 256 * j:256 * j + 256]
        in_maps.append({
            "xT": xTs[b],
            "xrows": np.ascontiguousarray(x[b, 1024 * j:1024 * j + 1024, :]),
            "wqkvT": np.ascontiguousarray(np.concatenate([wq_sl, wk_sl, wv_sl], axis=1)),
            "wqn": wqn, "wkn": wkn,
            "slopes": np.ascontiguousarray(
                np.tile(slopes_all[8 * j:8 * j + 8][None, :], (128, 1))),
            "woT": WoT,
            "wcatT": wcatT,
            "basetile": basetile,
            "gatet": gatet,
            "wbranch": wbr,
            "eye16": eye,
            "rowsel": np.ascontiguousarray(
                (4096 * np.arange(4, dtype=np.uint32)[None, :, None]
                 + 1024 * j + np.arange(128, dtype=np.uint32)[:, None, None]
                 + 128 * np.arange(32, dtype=np.uint32)[None, None, :])),
        })
    return in_maps


_CACHED = {}


def kernel(**inputs):
    in_maps = _prep_inputs(**inputs)
    if "full" not in _CACHED:
        _CACHED["full"] = build()
    nc = _CACHED["full"]
    res = run_bass_kernel_spmd(nc, in_maps, core_ids=list(range(8)))
    outp = np.empty((B_, S, DM), dtype=np.float32)
    for c in range(8):
        b, j = divmod(c, 4)
        outp[b, 1024 * j:1024 * j + 1024, :] = res.results[c]["out"]
    return outp


# version 06
# speedup vs baseline: 1.5088x; 1.2252x over previous
import math
import os
from contextlib import ExitStack
_NTCB = int(os.environ.get("K_NTCB", "16"))
_P1MODE = os.environ.get("K_P1MODE", "full")

import numpy as np
import concourse.bass as bass
import concourse.mybir as mybir
from concourse.bass import IndirectOffsetOnAxis
from concourse.bass_utils import run_bass_kernel_spmd

F32 = mybir.dt.float32
BF16 = mybir.dt.bfloat16
U32 = mybir.dt.uint32
AF = mybir.ActivationFunctionType
ALU = mybir.AluOpType
AX = mybir.AxisListType
ds = bass.ds

B_, S, DM = 2, 4096, 4096
H, HKV, HD = 32, 8, 128
HPC, KVPC = 8, 2
TQ = 1024
EPS_QK = 1e-6
EPS_LN = 1e-5
RMS_EPS = 1.1920929e-07
NEG = -1e30
NT = S // 128
NSUP = S // 512


def _alibi_slopes(n_heads):
    start = 2 ** (-(2 ** -(math.log2(n_heads) - 3)))
    return np.array([start * start ** i for i in range(n_heads)], dtype=np.float32)


class Sem:
    def __init__(self, ctx, nc, name):
        self.h = ctx.enter_context(nc.semaphore(name))
        self.val = 0

    def inc(self, instr, n=1):
        instr.then_inc(self.h, n)
        self.val += n
        return instr


class PingSem:

    def __init__(self, ctx, nc, name, n):
        self.s = [ctx.enter_context(nc.semaphore(f"{name}{i}")) for i in range(n)]
        self.n = n

    def inc(self, instr, k):
        instr.then_inc(self.s[k % self.n], 16)
        return instr

    def wait(self, eng, k):
        eng.wait_ge(self.s[k % self.n], 16 * (k // self.n + 1))


def build(phases=("p1", "p2", "p3", "p4"), debug_outs=()):
    nc = bass.Bass("TRN2", target_bir_lowering=False)

    xT = nc.dram_tensor("xT", [DM, S], F32, kind="ExternalInput").ap()
    xrows = nc.dram_tensor("xrows", [TQ, DM], F32, kind="ExternalInput").ap()
    wqkvT = nc.dram_tensor("wqkvT", [DM, 1536], F32, kind="ExternalInput").ap()
    wqn = nc.dram_tensor("wqn", [HD, 1], F32, kind="ExternalInput").ap()
    wkn = nc.dram_tensor("wkn", [HD, 1], F32, kind="ExternalInput").ap()
    slopes = nc.dram_tensor("slopes", [128, HPC], F32, kind="ExternalInput").ap()
    woT = nc.dram_tensor("woT", [DM, DM], F32, kind="ExternalInput").ap()
    wcatT = nc.dram_tensor("wcatT", [DM, 24], F32, kind="ExternalInput").ap()
    basetile = nc.dram_tensor("basetile", [128, 24], F32, kind="ExternalInput").ap()
    gatet = nc.dram_tensor("gatet", [128, 1], F32, kind="ExternalInput").ap()
    wbranch = nc.dram_tensor("wbranch", [1024, 1024], F32, kind="ExternalInput").ap()
    eye16 = nc.dram_tensor("eye16", [128, 16], F32, kind="ExternalInput").ap()
    rowsel = nc.dram_tensor("rowsel", [128, 4, 32], U32, kind="ExternalInput").ap()

    out = nc.dram_tensor("out", [TQ, DM], F32, kind="ExternalOutput").ap()
    dbg = {}
    for name, shape in (("qfm", [HPC * HD, S]), ("kfm", [KVPC * HD, S]),
                        ("v", [S, KVPC * 129]), ("attn", [S, HPC * HD]),
                        ("s1", [TQ, DM]), ("h1", [TQ, DM]), ("zb", [TQ, 24]),
                        ("hres", [TQ, 16]), ("consts", [128, 16])):
        if name in debug_outs:
            dbg[name] = nc.dram_tensor("dbg_" + name, shape, F32, kind="ExternalOutput").ap()

    wqkv_bf = nc.dram_tensor("wqkv_bf", [DM, 1536], BF16, kind="Internal").ap()
    cin = nc.dram_tensor("cin", [S, HPC * HD], BF16, kind="Internal").ap()
    cout = nc.dram_tensor("cout", [4, S, HPC * HD], BF16, kind="Internal").ap()

    ctx = ExitStack()
    with ctx:
        sbt = lambda name, shape, dt=F32: ctx.enter_context(nc.sbuf_tensor(name, shape, dt))
        identB = sbt("identB", [128, 128], BF16)
        cq = sbt("cq", [128, 1])
        slp = sbt("slp", [128, HPC])
        strips = sbt("strips", [128, HPC, NT])
        maskT = sbt("maskT", [128, 4, 512], BF16)
        wc_sb = sbt("wc_sb", [128, 32, 24], BF16)
        base_s = sbt("base_s", [128, 24])
        base10 = sbt("base10", [128, 16])
        eye_s = sbt("eye_s", [128, 16])
        gsig = sbt("gsig", [128, 4])
        onesb = sbt("onesb", [1, 128], BF16)
        epss = sbt("epss", [128, 4])
        rsel = ctx.enter_context(nc.sbuf_tensor("rsel", [128, 4, 32], U32))
        smalls = sbt("smalls", [128, 4])

        s_init = Sem(ctx, nc, "s_init")
        s_cst = Sem(ctx, nc, "s_cst")
        s_dbg = Sem(ctx, nc, "s_dbg")
        n_dbg = [0]

        with nc.Block() as blk:
            s_gp = Sem(ctx, nc, "s_gp")

            def g_body(g):
                nms = 0
                g.memset(identB[:, :], 1.0).then_inc(s_gp.h, 1); nms += 1
                for i in range(4):
                    g.memset(maskT[:, i, :], 0.0).then_inc(s_gp.h, 1); nms += 1
                g.memset(epss[:, 0:1], EPS_QK)
                g.memset(epss[:, 1:2], EPS_LN)
                g.memset(epss[:, 2:3], RMS_EPS)
                g.memset(epss[:, 3:4], -8.0)
                g.memset(onesb[:, :], 1.0)
                s_init.inc(g.iota(strips[:, 0, :], [[128, NT]], base=0,
                                  channel_multiplier=1,
                                  allow_small_or_imprecise_dtypes=True))
                g.wait_ge(s_gp.h, nms)
                s_gp.val += nms
                g.affine_select(identB[:, :], identB[:, :], [[1, 128]], ALU.is_equal,
                                0.0, base=0, channel_multiplier=-1)
                for i in range(4):
                    g.affine_select(maskT[:, i, :], maskT[:, i, :], [[1, 512]],
                                    ALU.is_ge, NEG, base=-128 * i, channel_multiplier=-1)
            blk.gpsimd(g_body)

            s_ld0 = Sem(ctx, nc, "s_ld0")
            s_ldg = Sem(ctx, nc, "s_ldg")

            def s_body(s):
                s_ld0.inc(s.dma_start(slp[:, :], slopes), 16)
                s_ld0.inc(s.dma_start(smalls[:, 0:1], wqn), 16)
                s_ld0.inc(s.dma_start(smalls[:, 1:2], wkn), 16)
                s_ld0.inc(s.dma_start(base_s[:, :], basetile), 16)
                s_ld0.inc(s.dma_start(eye_s[:, :], eye16), 16)
                s_ld0.inc(s.dma_start(gsig[:, 0:1], gatet), 16)
                s_ld0.inc(s.dma_start(rsel[:, :, :], rowsel), 16)
            blk.sync(s_body)

            def gc_body(g):
                s_ldg.inc(g.dma_start(wc_sb[:, :, :],
                                      wcatT.rearrange("(c p) o -> p c o", p=128)), 16)
            blk.gpsimd(gc_body)

            def sc0_body(sc):
                sc.wait_ge(s_ld0.h, s_ld0.val)
                s_init.inc(sc.activation(gsig[:, 2:3], gsig[:, 0:1], AF.Exp, scale=-1.0))
            blk.scalar(sc0_body)

            def v0_body(v):
                v.wait_ge(s_ld0.h, s_ld0.val)
                v.wait_ge(s_init.h, s_init.val)
                v.tensor_scalar_add(gsig[:, 2:3], gsig[:, 2:3], 1.0)
                v.drain()
                v.reciprocal(gsig[:, 0:1], gsig[:, 2:3])
                v.tensor_mul(cq[:, :], smalls[:, 0:1], smalls[:, 1:2])
                v.drain()
                v.tensor_scalar(gsig[:, 1:2], gsig[:, 0:1], -1.0, 2.0, ALU.mult, ALU.add)
                v.tensor_scalar_mul(cq[:, :], cq[:, :], 1.0 / math.sqrt(HD))
                v.drain()
                for h in range(HPC - 1, -1, -1):
                    v.tensor_scalar_mul(strips[:, h, :], strips[:, 0, :], slp[:, h:h + 1])
                s_init.inc(v.tensor_scalar_mul(base10[:, :], base_s[:, 0:16], 1.0 / 0.1))
            blk.vector(v0_body)

        with ExitStack() as p4s:
            e4 = p4s.enter_context
            blk = e4(nc.Block())
            cb0 = e4(nc.sbuf_tensor("castb0", [128, 2048], BF16))
            cb1 = e4(nc.sbuf_tensor("castb1", [128, 2048], BF16))
            cbs = [cb0, cb1]
            jobs = [(wqkvT, wqkv_bf, DM, 1536)]
            s_ca = PingSem(ctx, nc, "s_ca", 2)
            s_csp = PingSem(ctx, nc, "s_csp", 2)
            ncast = [0]

            def gcast(g):
                k = 0
                for src, dst, rows, cols in jobs:
                    src3 = src.rearrange("(n p) c -> n p c", p=128)
                    for n in range(rows // 128):
                        for c0 in range(0, cols, 2048):
                            cw = min(2048, cols - c0)
                            if k >= 2:
                                s_csp.wait(g, k - 2)
                            s_ca.inc(g.dma_start(cbs[k % 2][:, 0:cw],
                                                 src3[n, :, c0:c0 + cw]), k)
                            k += 1
                ncast[0] = k
            blk.gpsimd(gcast)

            def scast(s):
                k = 0
                for src, dst, rows, cols in jobs:
                    dst3 = dst.rearrange("(n p) c -> n p c", p=128)
                    for n in range(rows // 128):
                        for c0 in range(0, cols, 2048):
                            cw = min(2048, cols - c0)
                            s_ca.wait(s, k)
                            s_csp.inc(s.dma_start(dst3[n, :, c0:c0 + cw],
                                                  cbs[k % 2][:, 0:cw]), k)
                            k += 1
            blk.sync(scast)

        with nc.sbuf_tensor("qfm", [128, HPC, S], BF16) as qfm, \
                nc.sbuf_tensor("kfm", [128, KVPC, S], BF16) as kfm, \
                nc.sbuf_tensor("vp", [128, KVPC, NT, 129], BF16) as vp:

            if "p1" in phases:
                with nc.Block() as blk, \
                        nc.sbuf_tensor("xts", [128, 2, 32, 256], BF16) as xts, \
                        nc.sbuf_tensor("wbb", [128, 2, 32, 256], BF16) as wbb, \
                        nc.sbuf_tensor("qsc", [128, 256], BF16) as qsc, \
                        nc.sbuf_tensor("qtm", [128, 16, 256], BF16) as qtm, \
                        nc.sbuf_tensor("ssb", [128, 8], F32) as ssb, \
                        nc.psum_tensor("pa0", [128, 256], F32) as pa0, \
                        nc.psum_tensor("pa1", [128, 256], F32) as pa1, \
                        nc.psum_tensor("pa2", [128, 256], F32) as pa2, \
                        nc.psum_tensor("pa3", [128, 256], F32) as pa3, \
                        nc.psum_tensor("ptp0", [128, 128], BF16) as ptp0, \
                        nc.psum_tensor("ptp1", [128, 128], BF16) as ptp1:
                    paccs = [pa0, pa1, pa2, pa3]
                    ptps = [ptp0, ptp1]
                    s_x = PingSem(ctx, nc, "s_x", 2)
                    s_w = PingSem(ctx, nc, "s_w", 2)
                    s_mm = Sem(ctx, nc, "s_mm")
                    s_sq = Sem(ctx, nc, "s_sq")
                    s_nrm = Sem(ctx, nc, "s_nrm")
                    s_tp = Sem(ctx, nc, "s_tp")
                    s_ev = Sem(ctx, nc, "s_ev")

                    xT3 = xT.rearrange("(c p) t -> p c t", p=128)
                    wq3 = wqkv_bf.rearrange("(c p) o -> p c o", p=128)

                    def tile_info(n):
                        tc, r = divmod(n, 12)
                        oh, t128 = divmod(r, 2)
                        return tc, oh, t128

                    ntp_before = [0] * 194
                    for n in range(192):
                        _, oh, _ = tile_info(n)
                        ntp_before[n + 1] = ntp_before[n] + (0 if oh == 5 else 2)

                    def gx(g):
                        g.wait_ge(s_csp.s[0], 16 * 16)
                        g.wait_ge(s_csp.s[1], 16 * 16)
                        g.memset(vp[:, 0, :, 128:129], 1.0)
                        g.memset(vp[:, 1, :, 128:129], 1.0)
                        for tc in range(_NTCB):
                            if tc >= 2 and _P1MODE not in ("mm",):
                                g.wait_ge(s_nrm.h, (tc - 1) * 12)
                            s_x.inc(g.dma_start(xts[:, tc % 2, :, :],
                                                xT3[:, :, 256 * tc:256 * tc + 256]), tc)
                    blk.gpsimd(gx)

                    def sw(s):
                        s.wait_ge(s_csp.s[0], 16 * 16)
                        s.wait_ge(s_csp.s[1], 16 * 16)
                        k = 0
                        for tc in range(_NTCB):
                            for oh in range(6):
                                if k >= 2:
                                    s.wait_ge(s_mm.h, max(0, 2 * k - 2))
                                c0 = 256 * (tc * 0 + oh)
                                s_w.inc(s.dma_start(wbb[:, k % 2, :, :],
                                                    wq3[:, :, c0:c0 + 256]), k)
                                k += 1
                    blk.sync(sw)

                    def emit_transposes(t, tcb):
                        for r in range(12):
                            n = tcb * 12 + r
                            _, oh, _ = tile_info(n)
                            if oh == 5:
                                continue
                            for hh in range(2):
                                m = ntp_before[n] + hh
                                if m >= 2:
                                    t.wait_ge(s_ev.h, m - 1)
                                t.wait_ge(s_nrm.h, n + 1)
                                s_tp.inc(t.matmul(ptps[m % 2][:, :],
                                         qtm[:, n % 16, 128 * hh:128 * hh + 128],
                                         identB[:, :], is_transpose=True,
                                         start=True, stop=True))

                    def emit_evacs(sc, tcb):
                        for r in range(12):
                            n = tcb * 12 + r
                            tc2, oh, t128 = tile_info(n)
                            if oh == 5:
                                continue
                            gtok = tc2 * 2 + t128
                            for hh in range(2):
                                m = ntp_before[n] + hh
                                sc.wait_ge(s_tp.h, m + 1)
                                if oh < 4:
                                    head = oh * 2 + hh
                                    ei = sc.activation(qfm[:, head, 128 * gtok:128 * gtok + 128],
                                                       ptps[m % 2][:, :], AF.Copy,
                                                       scale=cq[:, :])
                                else:
                                    ei = sc.activation(kfm[:, hh, 128 * gtok:128 * gtok + 128],
                                                       ptps[m % 2][:, :], AF.Copy)
                                s_ev.inc(ei)

                    for tcb in range(_NTCB):
                        def t_body(t, tcb=tcb):
                            if tcb > 0 and _P1MODE in ("full", "notp_dbg"):
                                emit_transposes(t, tcb - 1)
                            s_x.wait(t, tcb)
                            for r in range(12):
                                n = tcb * 12 + r
                                oh, t128 = divmod(r, 2)
                                s_w.wait(t, tcb * 6 + oh)
                                if n >= 4:
                                    t.wait_ge(s_nrm.h, n - 3)
                                acc = paccs[n % 4]
                                for d in range(32):
                                    mi = t.matmul(acc[:, :],
                                                  xts[:, tcb % 2, d, 128 * t128:128 * t128 + 128],
                                                  wbb[:, (tcb * 6 + oh) % 2, d, :],
                                                  start=(d == 0), stop=(d == 31))
                                s_mm.inc(mi)
                        blk.tensor(t_body)

                        def sc_body(sc, tcb=tcb):
                            if tcb > 0 and _P1MODE == "full":
                                emit_evacs(sc, tcb - 1)
                            if _P1MODE == "mm":
                                for r in range(12):
                                    sc.wait_ge(s_mm.h, tcb * 12 + r + 1)
                                    s_sq.inc(sc.nop())
                                return
                            for r in range(12):
                                n = tcb * 12 + r
                                _, oh, _ = tile_info(n)
                                acc = paccs[n % 4]
                                sc.wait_ge(s_mm.h, n + 1)
                                if n >= 4:
                                    sc.wait_ge(s_nrm.h, n - 3)
                                if oh != 5:
                                    sl = 2 * (n % 4)
                                    if _P1MODE != "mmln":
                                        for hh in range(2):
                                            sc.activation(qsc[:, 128 * hh:128 * hh + 128],
                                                          acc[:, 128 * hh:128 * hh + 128],
                                                          AF.Square,
                                                          accum_out=ssb[:, sl + hh:sl + hh + 1])
                                    if _P1MODE != "mmsqonly":
                                        sc.drain()
                                        sc.activation(ssb[:, sl:sl + 2], ssb[:, sl:sl + 2],
                                                      AF.Ln, scale=1.0 / HD, bias=epss[:, 0:1])
                                        sc.drain()
                                        li = sc.activation(ssb[:, sl:sl + 2], ssb[:, sl:sl + 2],
                                                           AF.Exp, scale=-0.5)
                                        s_sq.inc(li)
                                    else:
                                        sc.drain()
                                        s_sq.inc(sc.nop())
                                else:
                                    sc.drain()
                                    s_sq.inc(sc.nop())
                        blk.scalar(sc_body)

                        def v_body(v, tcb=tcb):
                            if _P1MODE in ("mm", "mmsq"):
                                for r in range(12):
                                    v.wait_ge(s_sq.h, tcb * 12 + r + 1)
                                    s_nrm.inc(v.nop())
                                return
                            for r in range(12):
                                n = tcb * 12 + r
                                tc2, oh, t128 = tile_info(n)
                                acc = paccs[n % 4]
                                v.wait_ge(s_sq.h, n + 1)
                                gtok = tc2 * 2 + t128
                                if oh != 5:
                                    if n >= 16 and _P1MODE == "full":
                                        v.wait_ge(s_tp.h, ntp_before[n - 15])
                                    sl = 2 * (n % 4)
                                    for hh in range(2):
                                        vi = v.tensor_scalar_mul(qtm[:, n % 16, 128 * hh:128 * hh + 128],
                                                                 acc[:, 128 * hh:128 * hh + 128],
                                                                 ssb[:, sl + hh:sl + hh + 1])
                                else:
                                    for hh in range(2):
                                        vi = v.tensor_copy(vp[:, hh, gtok, 0:128],
                                                           acc[:, 128 * hh:128 * hh + 128])
                                s_nrm.inc(vi)
                        blk.vector(v_body)

                    if _P1MODE == "full":
                        blk.tensor(lambda t: emit_transposes(t, _NTCB - 1))
                        blk.scalar(lambda sc: emit_evacs(sc, _NTCB - 1))

                if dbg.keys() & {"qfm", "kfm", "v", "consts"}:
                    with nc.Block() as blk:
                        def dqkv(g):
                            if "qfm" in dbg:
                                q2 = dbg["qfm"].rearrange("(h p) t -> p h t", p=128)
                                s_dbg.inc(g.dma_start(q2, qfm[:, :, :]), 16)
                                n_dbg[0] += 1
                            if "kfm" in dbg:
                                k2 = dbg["kfm"].rearrange("(h p) t -> p h t", p=128)
                                s_dbg.inc(g.dma_start(k2, kfm[:, :, :]), 16)
                                n_dbg[0] += 1
                            if "consts" in dbg:
                                s_dbg.inc(g.dma_start(dbg["consts"][:, 0:4], smalls[:, :]), 16)
                                s_dbg.inc(g.dma_start(dbg["consts"][:, 4:5], cq[:, :], allow_slow_non_contiguous=True), 16)
                                s_dbg.inc(g.dma_start(dbg["consts"][:, 5:9], gsig[:, :]), 16)
                                s_dbg.inc(g.dma_start(dbg["consts"][:, 9:13], epss[:, :]), 16)
                            if "v" in dbg:
                                v2 = dbg["v"].rearrange("(n p) (g c) -> p n g c", p=128, g=KVPC)
                                for gg in range(KVPC):
                                    s_dbg.inc(g.dma_start(
                                        v2[:, :, gg, :], vp[:, gg, :, :]), 16)
                                n_dbg[0] += 2
                            g.wait_ge(s_dbg.h, s_dbg.val)
                        blk.gpsimd(dqkv)

            if "p2" in phases:
                with nc.sbuf_tensor("aout", [128, NT, HPC, 128], BF16) as aout:
                    with ExitStack() as p2s:
                        e2 = p2s.enter_context
                        blk = e2(nc.Block())
                        ptb = e2(nc.sbuf_tensor("ptb", [128, 2, 512], BF16))
                        negmT = e2(nc.sbuf_tensor("negmT", [1, HPC, 512], BF16))
                        qsq = e2(nc.sbuf_tensor("qsq", [128, 8, 512], BF16))
                        t1row = e2(nc.sbuf_tensor("t1row", [1, 2, 512], F32))
                        t2row = e2(nc.sbuf_tensor("t2row", [1, 512], F32))
                        jrow = e2(nc.sbuf_tensor("jrow", [1, 512], F32))
                        kmsq = e2(nc.sbuf_tensor("kmsq", [1, 2], F32))
                        kqb = e2(nc.sbuf_tensor("kqb", [1, 16], F32))
                        onesc = e2(nc.sbuf_tensor("onesc", [128, 1], BF16))
                        rcp = e2(nc.sbuf_tensor("rcp", [128, 4], F32))
                        pst0 = e2(nc.psum_tensor("pst0", [128, 512], F32))
                        pst1 = e2(nc.psum_tensor("pst1", [128, 512], F32))
                        pss0 = e2(nc.psum_tensor("pss0", [128, 512], F32))
                        pss1 = e2(nc.psum_tensor("pss1", [128, 512], F32))
                        ppv0 = e2(nc.psum_tensor("ppv0", [128, 129], F32))
                        ppv1 = e2(nc.psum_tensor("ppv1", [128, 129], F32))
                        ppv2 = e2(nc.psum_tensor("ppv2", [128, 129], F32))
                        ppv3 = e2(nc.psum_tensor("ppv3", [128, 129], F32))
                        psts = [pst0, pst1]
                        psss = [pss0, pss1]
                        ppvs = [ppv0, ppv1, ppv2, ppv3]
                        s_i2 = Sem(ctx, nc, "s_i2")
                        s_qs = Sem(ctx, nc, "s_qs")
                        s_qm = Sem(ctx, nc, "s_qm")
                        s_kvr = Sem(ctx, nc, "s_kvr")
                        s_kv = Sem(ctx, nc, "s_kv")
                        s_sqr = Sem(ctx, nc, "s_sqr")
                        s_ne = Sem(ctx, nc, "s_ne")
                        s_st = Sem(ctx, nc, "s_st")
                        s_pt = Sem(ctx, nc, "s_pt")
                        s_pv = Sem(ctx, nc, "s_pv")
                        s_fin = Sem(ctx, nc, "s_fin")

                        nb = [0]
                        nfin = [0]
                        nb_end = {}
                        _acc = 0
                        for _s in range(NSUP):
                            for _h in range(HPC):
                                _acc += 4 * (_s + 1)
                                nb_end[(_s, _h)] = _acc

                        def g_i2(g):
                            g.memset(onesc[:, :], 1.0).then_inc(s_i2.h, 1)
                            s_i2.val += 1
                            s_i2.inc(g.iota(jrow[0:1, :], [[1, 512]], base=0,
                                            channel_multiplier=0,
                                            allow_small_or_imprecise_dtypes=True))
                        blk.gpsimd(g_i2)

                        def sc_km(sc):
                            for gk in range(KVPC):
                                for c in range(8):
                                    idx = gk * 8 + c
                                    if idx >= 8:
                                        sc.wait_ge(s_qm.h, idx - 7)
                                    s_qs.inc(sc.activation(
                                        qsq[:, idx % 8, :],
                                        kfm[:, gk, 512 * c:512 * c + 512],
                                        AF.Square))
                        blk.scalar(sc_km)

                        def t_km(t):
                            t.wait_ge(s_i2.h, 2)
                            for gk in range(KVPC):
                                for c in range(8):
                                    idx = gk * 8 + c
                                    t.wait_ge(s_qs.h, idx + 1)
                                    if idx >= 2:
                                        t.wait_ge(s_kvr.h, idx - 1)
                                    s_qm.inc(t.matmul(psts[idx % 2][0:1, :],
                                                      onesc[:, 0:1],
                                                      qsq[:, idx % 8, :],
                                                      start=True, stop=True))
                        blk.tensor(t_km)

                        def v_km(v):
                            for gk in range(KVPC):
                                for c in range(8):
                                    idx = gk * 8 + c
                                    v.wait_ge(s_qm.h, idx + 1)
                                    s_kvr.inc(v.tensor_reduce(
                                        kqb[0:1, idx:idx + 1],
                                        psts[idx % 2][0:1, :], AX.X, ALU.max))
                                v.drain()
                                s_kv.inc(v.tensor_reduce(
                                    kmsq[0:1, gk:gk + 1],
                                    kqb[0:1, 8 * gk:8 * gk + 8], AX.X, ALU.max))
                        blk.vector(v_km)

                        def emit_chain(s2, h):
                            sqidx = 8 * s2 + h
                            gk = h // 4

                            def tQ(t, s2=s2, h=h, sqidx=sqidx):
                                t.wait_ge(s_qs.h, 16 + sqidx + 1)
                                if sqidx >= 2:
                                    t.wait_ge(s_sqr.h, sqidx - 1)
                                s_qm.inc(t.matmul(psss[sqidx % 2][0:1, :],
                                                  onesc[:, 0:1], qsq[:, h, :],
                                                  start=True, stop=True))
                            blk.tensor(tQ)

                            def sQ(sc, s2=s2, h=h, sqidx=sqidx, gk=gk):
                                if s2 >= 1:
                                    sc.wait_ge(s_qm.h, 16 + 8 * (s2 - 1) + h + 1)
                                else:
                                    sc.wait_ge(s_qm.h, 8 + h + 1)
                                s_qs.inc(sc.activation(
                                    qsq[:, h, :],
                                    qfm[:, h, 512 * s2:512 * s2 + 512], AF.Square))
                                sc.wait_ge(s_qm.h, 16 + sqidx + 1)
                                sc.wait_ge(s_kv.h, gk + 1)
                                if sqidx >= 2:
                                    sc.wait_ge(s_ne.h, sqidx - 1)
                                si = sc.activation(t1row[0:1, sqidx % 2, :],
                                                   psss[sqidx % 2][0:1, :],
                                                   AF.Sqrt,
                                                   scale=kmsq[0:1, gk:gk + 1])
                                s_sqr.inc(si)
                            blk.scalar(sQ)

                            def vQ(v, s2=s2, h=h, sqidx=sqidx):
                                v.wait_ge(s_sqr.h, sqidx + 1)
                                v.wait_ge(s_i2.h, 2)
                                if s2 >= 1:
                                    v.wait_ge(s_st.h, nb_end[(s2 - 1, h)])
                                v.tensor_scalar(t2row[0:1, :], jrow[0:1, :],
                                                512.0 * s2, slp[0:1, h:h + 1],
                                                ALU.add, ALU.mult)
                                v.drain()
                                v.tensor_add(t2row[0:1, :], t2row[0:1, :],
                                             t1row[0:1, sqidx % 2, :])
                                v.drain()
                                vi = v.tensor_scalar(negmT[0:1, h, :],
                                                     t2row[0:1, :], -1.0, -8.0,
                                                     ALU.mult, ALU.add)
                                s_ne.inc(vi)
                            blk.vector(vQ)

                        for h in range(HPC):
                            emit_chain(0, h)

                        for s in range(NSUP):
                            nch = 4 * (s + 1)
                            for h in range(HPC):
                                gk = h // 4
                                nb0 = nb[0]

                                def tB(t, s=s, h=h, gk=gk, nb0=nb0, nch=nch,
                                       fin0=nfin[0]):
                                    t.wait_ge(s_ne.h, s * 8 + h + 1)
                                    if fin0 >= 1:
                                        t.wait_ge(s_fin.h, fin0)
                                    for c in range(nch):
                                        n = nb0 + c
                                        bank = psts[n % 2]
                                        if n >= 2:
                                            t.wait_ge(s_pt.h, n - 1)
                                        t.matmul(bank[:, :], onesb[0:1, :],
                                                 negmT[0:1, h, :],
                                                 start=True, stop=False)
                                        diag = (c >= nch - 4)
                                        mi = t.matmul(bank[:, :],
                                                      kfm[:, gk, 128 * c:128 * c + 128],
                                                      qfm[:, h, 512 * s:512 * s + 512],
                                                      start=False, stop=not diag)
                                        if diag:
                                            mi = t.matmul(bank[:, :], identB[:, :],
                                                          maskT[:, c - (nch - 4), :],
                                                          start=False, stop=True)
                                        s_st.inc(mi)
                                        if c >= 1:
                                            t.wait_ge(s_pt.h, nb0 + c)
                                            for i in range(4):
                                                pvi = t.matmul(
                                                    ppvs[i][:, :],
                                                    ptb[:, (nb0 + c - 1) % 2,
                                                        128 * i:128 * i + 128],
                                                    vp[:, gk, c - 1, :],
                                                    start=(c == 1), stop=False)
                                            s_pv.inc(pvi)
                                    t.wait_ge(s_pt.h, nb0 + nch)
                                    for i in range(4):
                                        pvi = t.matmul(ppvs[i][:, :],
                                                       ptb[:, (nb0 + nch - 1) % 2,
                                                           128 * i:128 * i + 128],
                                                       vp[:, gk, nch - 1, :],
                                                       start=(nch == 1), stop=True)
                                    s_pv.inc(pvi)
                                blk.tensor(tB)

                                def sB(sc, s=s, h=h, nb0=nb0, nch=nch):
                                    for c in range(nch):
                                        n = nb0 + c
                                        bank = psts[n % 2]
                                        sc.wait_ge(s_st.h, n + 1)
                                        if n >= 2:
                                            sc.wait_ge(s_pv.h, n - 1)
                                        s_pt.inc(sc.activation(
                                            ptb[:, n % 2, :], bank[:, :],
                                            AF.Exp, bias=strips[:, h, c:c + 1]))
                                blk.scalar(sB)

                                def vB(v, s=s, h=h, nb0=nb0, nch=nch):
                                    v.wait_ge(s_pv.h, nb0 + nch)
                                    for i in range(4):
                                        v.reciprocal(rcp[:, i:i + 1],
                                                     ppvs[i][:, 128:129])
                                    v.drain()
                                    for i in range(4):
                                        fi = v.tensor_scalar_mul(aout[:, 4 * s + i, h, :],
                                                                 ppvs[i][:, 0:128],
                                                                 rcp[:, i:i + 1])
                                    s_fin.inc(fi)
                                blk.vector(vB)
                                nfin[0] += 1
                                nb[0] += nch
                                if s + 1 < NSUP:
                                    emit_chain(s + 1, h)

                    with nc.Block() as blk:
                        s_ci = Sem(ctx, nc, "s_ci")
                        s_cc = Sem(ctx, nc, "s_cc")

                        def gex(g):
                            cin4 = cin.rearrange("(n p) (h d) -> p n h d", p=128, h=HPC)
                            s_ci.inc(g.dma_start(cin4, aout[:, :, :, :]), 16)
                            g.wait_ge(s_ci.h, 16)
                            s_cc.inc(g.collective_compute(
                                "AllGather", ALU.bypass,
                                replica_groups=[[0, 1, 2, 3], [4, 5, 6, 7]],
                                ins=[cin], outs=[cout]), 1)
                            g.wait_ge(s_cc.h, 1)
                        blk.gpsimd(gex)

                    if "attn" in dbg:
                        with nc.Block() as blk:
                            def datt(g):
                                a2 = dbg["attn"].rearrange("(n p) f -> p n f", p=128)
                                s_dbg.inc(g.dma_start(
                                    a2, aout.ap().rearrange("p n h d -> p n (h d)")), 16)
                                g.wait_ge(s_dbg.h, s_dbg.val)
                            blk.gpsimd(datt)

        if "p3" in phases:
            s_bf = ctx.enter_context(nc.sbuf_tensor("s_bf", [128, 8, DM], BF16))
            lnst = ctx.enter_context(nc.sbuf_tensor("lnst", [128, 8, 4], F32))

            with ExitStack() as p3s:
                e3 = p3s.enter_context
                afm = e3(nc.sbuf_tensor("afm", [128, 32, 1024], BF16))
                s_at = Sem(ctx, nc, "s_at")
                s_tq = Sem(ctx, nc, "s_tq")
                s_ae = Sem(ctx, nc, "s_ae")
                s_wl = PingSem(ctx, nc, "s_wl", 2)
                s_xr = PingSem(ctx, nc, "s_xr", 2)
                s_ym = Sem(ctx, nc, "s_ym")
                s_ye = Sem(ctx, nc, "s_ye")

                coutf = cout.rearrange("r n f -> (r n) f")

                with ExitStack() as p3g:
                    eg = p3g.enter_context
                    blkg = eg(nc.Block())
                    attm = eg(nc.sbuf_tensor("attm", [128, 8, 1024], BF16))
                    ptq0 = eg(nc.psum_tensor("ptq0", [128, 512], BF16))
                    ptq1 = eg(nc.psum_tensor("ptq1", [128, 512], BF16))
                    ptqs = [ptq0, ptq1]

                    def g_loads(g):
                        g.wait_ge(s_ld0.h, s_ld0.val)
                        for r in range(4):
                            if r >= 1:
                                g.wait_ge(s_tq.h, 16 * r)
                            for tt in range(8):
                                s_at.inc(g.indirect_dma_start(
                                    attm[:, tt, :], None, coutf,
                                    IndirectOffsetOnAxis(rsel[:, r, tt:tt + 1], 0)), 16)
                    blkg.gpsimd(g_loads)

                    def t_tr(t):
                        k = 0
                        for r in range(4):
                            t.wait_ge(s_at.h, 16 * 8 * (r + 1))
                            for tt in range(8):
                                for u in range(2):
                                    m = 2 * k + u
                                    if m >= 2:
                                        t.wait_ge(s_ae.h, m - 1)
                                    for q in range(4):
                                        fc = 4 * u + q
                                        ti = t.matmul(ptqs[m % 2][:, 128 * q:128 * q + 128],
                                                      attm[:, tt, 128 * fc:128 * fc + 128],
                                                      identB[:, :], is_transpose=True,
                                                      start=(q == 0), stop=(q == 3))
                                    s_tq.inc(ti)
                                k += 1
                    blkg.tensor(t_tr)

                    def sc_ae(sc):
                        k = 0
                        for r in range(4):
                            for tt in range(8):
                                for u in range(2):
                                    m = 2 * k + u
                                    sc.wait_ge(s_tq.h, m + 1)
                                    fc0 = 8 * r + 4 * u
                                    s_ae.inc(sc.activation(
                                        afm[:, fc0:fc0 + 4, 128 * tt:128 * tt + 128],
                                        ptqs[m % 2][:, :].rearrange("p (a b) -> p a b", b=128),
                                        AF.Copy))
                                k += 1
                    blkg.scalar(sc_ae)

                blk = e3(nc.Block())
                wsb = e3(nc.sbuf_tensor("wsb", [128, 2, 32, 512], BF16))
                xrb = e3(nc.sbuf_tensor("xrb", [128, 2, 512], F32))
                py0 = e3(nc.psum_tensor("py0", [128, 512], F32))
                py1 = e3(nc.psum_tensor("py1", [128, 512], F32))
                py2 = e3(nc.psum_tensor("py2", [128, 512], F32))
                py3 = e3(nc.psum_tensor("py3", [128, 512], F32))
                pys = [py0, py1, py2, py3]

                wo3 = woT.rearrange("(c p) m -> p c m", p=128)

                def s_wxload(g):
                    for mh in range(8):
                        if mh >= 2:
                            g.wait_ge(s_ym.h, (mh - 1) * 8)
                        s_wl.inc(g.dma_start(wsb[:, mh % 2, :, :],
                                             wo3[:, :, 512 * mh:512 * mh + 512]), mh)
                        for tt in range(8):
                            k = mh * 8 + tt
                            if k >= 2:
                                g.wait_ge(s_ye.h, k - 1)
                            s_xr.inc(g.dma_start(
                                xrb[:, k % 2, :],
                                xrows[128 * tt:128 * tt + 128,
                                      512 * mh:512 * mh + 512]), k)
                blk.gpsimd(s_wxload)

                def t_wo(t):
                    t.wait_ge(s_ae.h, 64)
                    for mh in range(8):
                        s_wl.wait(t, mh)
                        for tt in range(8):
                            k = mh * 8 + tt
                            acc = pys[k % 4]
                            if k >= 4:
                                t.wait_ge(s_ye.h, k - 3)
                            for fc in range(32):
                                yi = t.matmul(acc[:, :], afm[:, fc, 128 * tt:128 * tt + 128],
                                              wsb[:, mh % 2, fc, :],
                                              start=(fc == 0), stop=(fc == 31))
                            s_ym.inc(yi)
                blk.tensor(t_wo)

                def v_wo(v):
                    for mh in range(8):
                        for tt in range(8):
                            k = mh * 8 + tt
                            acc = pys[k % 4]
                            v.wait_ge(s_ym.h, k + 1)
                            s_xr.wait(v, k)
                            s_ye.inc(v.scalar_tensor_tensor(
                                s_bf[:, tt, 512 * mh:512 * mh + 512], acc[:, :],
                                1.0, xrb[:, k % 2, :], ALU.mult, ALU.add))
                blk.vector(v_wo)

            if "s1" in dbg:
                with nc.Block() as blk:
                    def ds1(g):
                        d2 = dbg["s1"].rearrange("(n p) f -> p n f", p=128)
                        s_dbg.inc(g.dma_start(d2, s_bf[:, :, :]), 16)
                        g.wait_ge(s_dbg.h, s_dbg.val)
                    blk.gpsimd(ds1)

        if "p4" in phases:
            with ExitStack() as p4b:
                eb4 = p4b.enter_context
                blk = eb4(nc.Block())
                junk = eb4(nc.sbuf_tensor("junk", [128, DM], BF16))
                stat4 = eb4(nc.sbuf_tensor("stat4", [128, 32], F32))
                h1f = eb4(nc.sbuf_tensor("h1f", [128, 32, 128], BF16))
                zbuf = eb4(nc.sbuf_tensor("zbuf", [128, 8, 24], F32))
                z0 = eb4(nc.sbuf_tensor("z0", [128, 8, 16], F32))
                zp = eb4(nc.sbuf_tensor("zp", [128, 8, 16], F32))
                tsk = eb4(nc.sbuf_tensor("tsk", [128, 128], F32))
                esk = eb4(nc.sbuf_tensor("esk", [128, 128], F32))
                msk = eb4(nc.sbuf_tensor("msk", [128, 32], F32))
                ulse = eb4(nc.sbuf_tensor("ulse", [128, 32], F32))
                vlse = eb4(nc.sbuf_tensor("vlse", [128, 32], F32))
                hres = eb4(nc.sbuf_tensor("hres", [128, 8, 16], F32))
                hpp = eb4(nc.sbuf_tensor("hpp", [128, 8, 8], F32))
                invt = eb4(nc.sbuf_tensor("invt", [128, 8, 2], F32))
                xp = eb4(nc.sbuf_tensor("xp", [128, 1024], BF16))
                xpf = eb4(nc.sbuf_tensor("xpf", [128, 1024], F32))
                xpfm = eb4(nc.sbuf_tensor("xpfm", [128, 8, 128], BF16))
                wbsb = eb4(nc.sbuf_tensor("wbsb", [128, 8, 1024], BF16))
                ybf = eb4(nc.sbuf_tensor("ybf", [128, 8, 1024], BF16))
                s2 = eb4(nc.sbuf_tensor("s2", [128, 4, 1024], F32))
                outb = eb4(nc.sbuf_tensor("outb", [128, 2, DM], F32))
                ph0 = eb4(nc.psum_tensor("ph0", [128, 512], BF16))
                ph1 = eb4(nc.psum_tensor("ph1", [128, 512], BF16))
                plg = eb4(nc.psum_tensor("plg", [128, 24], F32))
                pyb0 = eb4(nc.psum_tensor("pyb0", [128, 512], F32))
                pyb1 = eb4(nc.psum_tensor("pyb1", [128, 512], F32))
                phs = [ph0, ph1]
                pybs = [pyb0, pyb1]
                s_sq1 = Sem(ctx, nc, "s_sq1")
                s_vst = Sem(ctx, nc, "s_vst")
                s_iv1 = Sem(ctx, nc, "s_iv1")
                s_h1 = Sem(ctx, nc, "s_h1")
                s_rm2 = Sem(ctx, nc, "s_rm2")
                s_ht = Sem(ctx, nc, "s_ht")
                s_he = Sem(ctx, nc, "s_he")
                s_lg = Sem(ctx, nc, "s_lg")
                s_zb = Sem(ctx, nc, "s_zb")
                s_sk = Sem(ctx, nc, "s_sk")
                s_xp = Sem(ctx, nc, "s_xp")
                s_xt = Sem(ctx, nc, "s_xt")
                s_xe = Sem(ctx, nc, "s_xe")
                s_yb = Sem(ctx, nc, "s_yb")
                s_yv = Sem(ctx, nc, "s_yv")
                s_s2 = Sem(ctx, nc, "s_s2")
                s_l2 = Sem(ctx, nc, "s_l2")
                s_ob = Sem(ctx, nc, "s_ob")
                s_out = PingSem(ctx, nc, "s_out", 2)
                s_wbl = Sem(ctx, nc, "s_wbl")

                def g_wb(g):
                    wb3 = wbranch.rearrange("(c p) m -> p c m", p=128)
                    s_wbl.inc(g.dma_start(wbsb[:, :, :], wb3), 16)
                blk.gpsimd(g_wb)

                def sc_ln1(sc):
                    for tt in range(8):
                        if tt:
                            sc.drain()
                        s_sq1.inc(sc.activation(junk[:, :], s_bf[:, tt, :], AF.Square,
                                                 accum_out=stat4[:, tt:tt + 1]))
                blk.scalar(sc_ln1)

                def v_ln1(v):
                    v.wait_ge(s_ld0.h, s_ld0.val)
                    for tt in range(8):
                        v.wait_ge(s_sq1.h, tt + 1)
                        v.reduce_sum(stat4[:, 8 + tt:8 + tt + 1], s_bf[:, tt, :], AX.X)
                        v.drain()
                        v.tensor_scalar_mul(lnst[:, tt, 0:1],
                                            stat4[:, 8 + tt:8 + tt + 1], 1.0 / DM)
                        v.tensor_scalar_mul(lnst[:, tt, 3:4], stat4[:, tt:tt + 1], 1.0 / DM)
                        v.drain()
                        v.tensor_mul(lnst[:, tt, 2:3], lnst[:, tt, 0:1], lnst[:, tt, 0:1])
                        v.drain()
                        s_vst.inc(v.tensor_sub(lnst[:, tt, 3:4], lnst[:, tt, 3:4],
                                                lnst[:, tt, 2:3]))
                blk.vector(v_ln1)

                def sc_inv1(sc):
                    for tt in range(8):
                        sc.wait_ge(s_vst.h, tt + 1)
                        sc.activation(lnst[:, tt, 1:2], lnst[:, tt, 3:4], AF.Ln,
                                      bias=epss[:, 1:2])
                        sc.drain()
                        s_iv1.inc(sc.activation(lnst[:, tt, 1:2], lnst[:, tt, 1:2],
                                                 AF.Exp, scale=-0.5))
                blk.scalar(sc_inv1)

                def v_h1(v):
                    for tt in range(8):
                        v.wait_ge(s_iv1.h, tt + 1)
                        s_h1.inc(v.tensor_scalar(s_bf[:, tt, :], s_bf[:, tt, :],
                                                  lnst[:, tt, 0:1], lnst[:, tt, 1:2],
                                                  ALU.subtract, ALU.mult))
                blk.vector(v_h1)

                if "h1" in dbg:
                    def dh1(g):
                        g.wait_ge(s_h1.h, 8)
                        d2 = dbg["h1"].rearrange("(n p) f -> p n f", p=128)
                        s_dbg.inc(g.dma_start(d2, s_bf[:, :, :]), 16)
                        g.wait_ge(s_dbg.h, s_dbg.val)
                    blk.gpsimd(dh1)

                def sc_rms(sc):
                    for tt in range(8):
                        sc.wait_ge(s_h1.h, tt + 1)
                        sc.drain()
                        sc.activation(junk[:, :], s_bf[:, tt, :], AF.Square,
                                      accum_out=invt[:, tt, 0:1])
                        sc.drain()
                        sc.activation(invt[:, tt, 1:2], invt[:, tt, 0:1], AF.Ln,
                                      scale=1.0 / DM, bias=epss[:, 2:3])
                        sc.drain()
                        s_rm2.inc(sc.activation(invt[:, tt, 1:2], invt[:, tt, 1:2],
                                                 AF.Exp, scale=-0.5))
                blk.scalar(sc_rms)

                for tt in range(8):
                    def t_h1t(t, tt=tt):
                        t.wait_ge(s_h1.h, tt + 1)
                        for u in range(8):
                            m = tt * 8 + u
                            if m >= 2:
                                t.wait_ge(s_he.h, m - 1)
                            for q in range(4):
                                fc = 4 * u + q
                                hi = t.matmul(phs[m % 2][:, 128 * q:128 * q + 128],
                                              s_bf[:, tt, 128 * fc:128 * fc + 128],
                                              identB[:, :], is_transpose=True,
                                              start=(q == 0), stop=(q == 3))
                            s_ht.inc(hi)
                    blk.tensor(t_h1t)

                    def sc_h1e(sc, tt=tt):
                        if tt >= 1:
                            sc.wait_ge(s_lg.h, tt)
                        for u in range(8):
                            m = tt * 8 + u
                            sc.wait_ge(s_ht.h, m + 1)
                            s_he.inc(sc.activation(
                                h1f[:, 4 * u:4 * u + 4, :],
                                phs[m % 2][:, :].rearrange("p (a b) -> p a b", b=128),
                                AF.Copy))
                    blk.scalar(sc_h1e)

                    def t_lg(t, tt=tt):
                        if tt == 0:
                            t.wait_ge(s_ldg.h, 16)
                        t.wait_ge(s_he.h, (tt + 1) * 8)
                        if tt >= 1:
                            t.wait_ge(s_zb.h, tt)
                        for fc in range(32):
                            lgi = t.matmul(plg[:, :], h1f[:, fc, :], wc_sb[:, fc, :],
                                           start=(fc == 0), stop=(fc == 31))
                        s_lg.inc(lgi)
                    blk.tensor(t_lg)

                    def v_z(v, tt=tt):
                        v.wait_ge(s_lg.h, tt + 1)
                        v.wait_ge(s_rm2.h, tt + 1)
                        v.tensor_scalar_mul(invt[:, tt, 0:1], invt[:, tt, 1:2], 10.0)
                        v.drain()
                        v.scalar_tensor_tensor(zbuf[:, tt, 0:16], plg[:, 0:16],
                                               invt[:, tt, 0:1], base10[:, :],
                                               ALU.mult, ALU.add)
                        s_zb.inc(v.scalar_tensor_tensor(
                            zbuf[:, tt, 16:24], plg[:, 16:24],
                            invt[:, tt, 1:2], base_s[:, 16:24],
                            ALU.mult, ALU.add))
                    blk.vector(v_z)

                def v_sm(v):
                    v.wait_ge(s_zb.h, 8)
                    pre = zbuf[:, :, 16:24].rearrange("p a (n f) -> p a n f", f=4)
                    v.tensor_reduce(msk[:, 0:16].rearrange("p (a n) -> p a n", n=2),
                                    pre, AX.X, ALU.max)
                    v.drain()
                    mb = msk[:, 0:16].rearrange("p (a n) -> p a n", n=2).unsqueeze(3) \
                        .broadcast_to([128, 8, 2, 4])
                    s_sk.inc(v.tensor_sub(
                        hpp.ap().rearrange("p a (n f) -> p a n f", f=4), pre, mb))
                blk.vector(v_sm)

                def sc_sm(sc):
                    sc.wait_ge(s_sk.h, 1)
                    s_sk.inc(sc.activation(hpp[:, :, :], hpp[:, :, :], AF.Exp))
                blk.scalar(sc_sm)

                def v_sm2(v):
                    v.wait_ge(s_sk.h, 2)
                    hp4 = hpp.ap().rearrange("p a (n f) -> p a n f", f=4)
                    v.tensor_reduce(msk[:, 0:16].rearrange("p (a n) -> p a n", n=2),
                                    hp4, AX.X, ALU.add)
                    v.drain()
                    v.reciprocal(msk[:, 0:16], msk[:, 0:16])
                    v.drain()
                    mb = msk[:, 0:16].rearrange("p (a n) -> p a n", n=2).unsqueeze(3) \
                        .broadcast_to([128, 8, 2, 4])
                    v.tensor_mul(hp4, hp4, mb)
                    zr = zbuf[:, :, 0:16].rearrange("p a (i j) -> p a i j", j=4)
                    v.drain()
                    v.tensor_reduce(msk[:, 0:32].rearrange("p (a i) -> p a i", i=4),
                                    zr, AX.X, ALU.max)
                    v.drain()
                    mb2 = msk[:, 0:32].rearrange("p (a i) -> p a i", i=4).unsqueeze(3) \
                        .broadcast_to([128, 8, 4, 4])
                    v.tensor_sub(z0.ap().rearrange("p a (i j) -> p a i j", j=4), zr, mb2)
                    v.drain()
                    s_sk.inc(v.tensor_copy(
                        zp.ap().rearrange("p a (j i) -> p a j i", i=4),
                        z0.ap().rearrange("p a (i j) -> p a i j", j=4)
                        .transpose([0, 1, 3, 2])))
                blk.vector(v_sm2)

                for tt in range(8):
                    def v_xp(v, tt=tt):
                        v.wait_ge(s_sk.h, 3)
                        if tt >= 1:
                            v.wait_ge(s_xt.h, 2 * tt)
                        hp = hpp[:, tt, 0:4]
                        v.tensor_scalar_mul(xpf[:, :], s_bf[:, tt, 0:1024], hp[:, 0:1])
                        for j in (1, 2):
                            v.drain()
                            v.scalar_tensor_tensor(xpf[:, :],
                                                   s_bf[:, tt, 1024 * j:1024 * j + 1024],
                                                   hp[:, j:j + 1], xpf[:, :],
                                                   ALU.mult, ALU.add)
                        v.drain()
                        s_xp.inc(v.scalar_tensor_tensor(
                            xp[:, :], s_bf[:, tt, 3072:4096],
                            hp[:, 3:4], xpf[:, :], ALU.mult, ALU.add))
                    blk.vector(v_xp)

                    def t_xpt(t, tt=tt):
                        t.wait_ge(s_xp.h, tt + 1)
                        for u in range(2):
                            m = 2 * tt + u
                            if m >= 2:
                                t.wait_ge(s_xe.h, m - 1)
                            for q in range(4):
                                fc = 4 * u + q
                                xi = t.matmul(phs[m % 2][:, 128 * q:128 * q + 128],
                                              xp[:, 128 * fc:128 * fc + 128],
                                              identB[:, :], is_transpose=True,
                                              start=(q == 0), stop=(q == 3))
                            s_xt.inc(xi)
                    blk.tensor(t_xpt)

                    def sc_xpe(sc, tt=tt):
                        if tt >= 1:
                            sc.wait_ge(s_yb.h, 2 * tt)
                        for u in range(2):
                            m = 2 * tt + u
                            sc.wait_ge(s_xt.h, m + 1)
                            s_xe.inc(sc.activation(
                                xpfm[:, 4 * u:4 * u + 4, :],
                                phs[m % 2][:, :].rearrange("p (a b) -> p a b", b=128),
                                AF.Copy))
                    blk.scalar(sc_xpe)

                    def t_y(t, tt=tt):
                        t.wait_ge(s_xe.h, 2 * (tt + 1))
                        if tt == 0:
                            t.wait_ge(s_wbl.h, 16)
                        for half in range(2):
                            k = 2 * tt + half
                            if k >= 2:
                                t.wait_ge(s_yv.h, k - 1)
                            for cc in range(8):
                                ybi = t.matmul(pybs[k % 2][:, :], xpfm[:, cc, :],
                                               wbsb[:, cc, 512 * half:512 * half + 512],
                                               start=(cc == 0), stop=(cc == 7))
                            s_yb.inc(ybi)
                    blk.tensor(t_y)

                    def v_y(v, tt=tt):
                        for half in range(2):
                            k = 2 * tt + half
                            v.wait_ge(s_yb.h, k + 1)
                            s_yv.inc(v.tensor_copy(
                                ybf[:, tt, 512 * half:512 * half + 512],
                                pybs[k % 2][:, :]))
                    blk.vector(v_y)

                z04 = z0.ap().rearrange("p a (i j) -> p a i j", j=4)
                zp4 = zp.ap().rearrange("p a (j i) -> p a j i", i=4)
                t4 = tsk.ap().rearrange("p (a i j) -> p a i j", i=4, j=4)
                e4 = esk.ap().rearrange("p (a i j) -> p a i j", i=4, j=4)
                m4 = msk.ap().rearrange("p (a i) -> p a i", i=4)
                ul4 = ulse.ap().rearrange("p (a i) -> p a i", i=4)
                vl4 = vlse.ap().rearrange("p (a j) -> p a j", j=4)
                skc = [3]

                def v_sink(v):
                    c = skc[0]
                    for it in range(50):
                        v.wait_ge(s_sk.h, c)
                        if it == 0:
                            v.tensor_copy(tsk[:, :], z0.ap().rearrange("p a f -> p (a f)"))
                        else:
                            vb = vl4.unsqueeze(2).broadcast_to([128, 8, 4, 4])
                            v.tensor_sub(t4, z04, vb)
                        v.drain()
                        v.tensor_reduce(m4, t4, AX.X, ALU.max)
                        v.drain()
                        mb = m4.unsqueeze(3).broadcast_to([128, 8, 4, 4])
                        s_sk.inc(v.tensor_sub(t4, t4, mb))
                        c += 2
                        v.wait_ge(s_sk.h, c)
                        s_sk.inc(v.tensor_reduce(ul4, e4, AX.X, ALU.add))
                        c += 2
                        v.wait_ge(s_sk.h, c)
                        v.tensor_add(ulse[:, :], ulse[:, :], msk[:, :])
                        v.drain()
                        ub = ul4.unsqueeze(2).broadcast_to([128, 8, 4, 4])
                        v.tensor_sub(t4, zp4, ub)
                        v.drain()
                        v.tensor_reduce(m4, t4, AX.X, ALU.max)
                        v.drain()
                        mb = m4.unsqueeze(3).broadcast_to([128, 8, 4, 4])
                        s_sk.inc(v.tensor_sub(t4, t4, mb))
                        c += 2
                        v.wait_ge(s_sk.h, c)
                        s_sk.inc(v.tensor_reduce(vl4, e4, AX.X, ALU.add))
                        c += 2
                        v.wait_ge(s_sk.h, c)
                        s_sk.inc(v.tensor_add(vlse[:, :], vlse[:, :], msk[:, :]))
                        c += 1
                    v.wait_ge(s_sk.h, c)
                    ub = ul4.unsqueeze(3).broadcast_to([128, 8, 4, 4])
                    vb = vl4.unsqueeze(2).broadcast_to([128, 8, 4, 4])
                    hr4 = hres.ap().rearrange("p a (i j) -> p a i j", j=4)
                    v.tensor_sub(hr4, z04, ub)
                    v.drain()
                    s_sk.inc(v.tensor_sub(hr4, hr4, vb))
                    c += 2
                    v.wait_ge(s_sk.h, c)
                    eb = eye_s[:, :].unsqueeze(1).broadcast_to([128, 8, 16])
                    v.tensor_scalar_mul(hres[:, :, :], hres[:, :, :], gsig[:, 0:1])
                    v.drain()
                    s_sk.inc(v.scalar_tensor_tensor(hres[:, :, :], eb, gsig[:, 1:2],
                                                    hres[:, :, :], ALU.mult, ALU.add))
                    c += 1
                    skc[0] = c
                blk.vector(v_sink)

                def s_sink(sc):
                    c = 3
                    for it in range(50):
                        c += 1
                        sc.wait_ge(s_sk.h, c)
                        s_sk.inc(sc.activation(esk[:, :], tsk[:, :], AF.Exp))
                        c += 1
                        c += 1
                        sc.wait_ge(s_sk.h, c)
                        s_sk.inc(sc.activation(ulse[:, :], ulse[:, :], AF.Ln))
                        c += 1
                        c += 1
                        sc.wait_ge(s_sk.h, c)
                        s_sk.inc(sc.activation(esk[:, :], tsk[:, :], AF.Exp))
                        c += 1
                        c += 1
                        sc.wait_ge(s_sk.h, c)
                        s_sk.inc(sc.activation(vlse[:, :], vlse[:, :], AF.Ln))
                        c += 1
                        c += 1
                    c += 1
                    sc.wait_ge(s_sk.h, c)
                    s_sk.inc(sc.activation(hres[:, :, :], hres[:, :, :], AF.Exp))
                    c += 1
                blk.scalar(s_sink)
                SK_DONE = s_sk.val

                if "zb" in dbg:
                    def dzb(g):
                        g.wait_ge(s_zb.h, 8)
                        d2 = dbg["zb"].rearrange("(n p) f -> p n f", p=128)
                        s_dbg.inc(g.dma_start(d2, zbuf[:, :, :]), 16)
                        g.wait_ge(s_dbg.h, s_dbg.val)
                    blk.gpsimd(dzb)
                if "hres" in dbg:
                    def dhr(g):
                        g.wait_ge(s_sk.h, SK_DONE)
                        d2 = dbg["hres"].rearrange("(n p) f -> p n f", p=128)
                        s_dbg.inc(g.dma_start(d2, hres[:, :, :]), 16)
                        g.wait_ge(s_dbg.h, s_dbg.val)
                    blk.gpsimd(dhr)

                for tt in range(8):
                    def v_mix(v, tt=tt):
                        if tt:
                            v.drain()
                        v.wait_ge(s_sk.h, SK_DONE)
                        v.wait_ge(s_yv.h, 2 * (tt + 1))
                        if tt >= 1:
                            v.wait_ge(s_l2.h, 3 * (tt - 1) + 1)
                        for i in range(4):
                            hr = hres[:, tt, :]
                            v.tensor_scalar_mul(s2[:, i, :], s_bf[:, tt, 0:1024],
                                                hr[:, 4 * i:4 * i + 1])
                            for j in (1, 2, 3):
                                v.drain()
                                v.scalar_tensor_tensor(
                                    s2[:, i, :], s_bf[:, tt, 1024 * j:1024 * j + 1024],
                                    hr[:, 4 * i + j:4 * i + j + 1],
                                    s2[:, i, :], ALU.mult, ALU.add)
                            v.drain()
                            s2i = v.scalar_tensor_tensor(s2[:, i, :], ybf[:, tt, :],
                                                          hpp[:, tt, 4 + i:4 + i + 1],
                                                          s2[:, i, :], ALU.mult, ALU.add)
                        s_s2.inc(s2i)
                    blk.vector(v_mix)

                    def sc_ln2(sc, tt=tt):
                        sc.wait_ge(s_s2.h, tt + 1)
                        sc.drain()
                        s_l2.inc(sc.activation(
                            junk[:, :], s2.ap().rearrange("p a f -> p (a f)"),
                            AF.Square, accum_out=stat4[:, 16 + tt:16 + tt + 1]))
                    blk.scalar(sc_ln2)

                    def v_ln2(v, tt=tt):
                        v.wait_ge(s_l2.h, 3 * tt + 1)
                        s2f = s2.ap().rearrange("p a f -> p (a f)")
                        v.reduce_sum(stat4[:, 24 + tt:24 + tt + 1], s2f, AX.X)
                        v.drain()
                        v.tensor_scalar_mul(lnst[:, tt, 0:1],
                                            stat4[:, 24 + tt:24 + tt + 1], 1.0 / DM)
                        v.tensor_scalar_mul(lnst[:, tt, 3:4],
                                            stat4[:, 16 + tt:16 + tt + 1], 1.0 / DM)
                        v.drain()
                        v.tensor_mul(lnst[:, tt, 2:3], lnst[:, tt, 0:1], lnst[:, tt, 0:1])
                        v.drain()
                        s_l2.inc(v.tensor_sub(lnst[:, tt, 3:4], lnst[:, tt, 3:4],
                                              lnst[:, tt, 2:3]))
                    blk.vector(v_ln2)

                    def sc_inv2(sc, tt=tt):
                        sc.wait_ge(s_l2.h, 3 * tt + 2)
                        sc.activation(lnst[:, tt, 1:2], lnst[:, tt, 3:4], AF.Ln,
                                      bias=epss[:, 1:2])
                        sc.drain()
                        s_l2.inc(sc.activation(lnst[:, tt, 1:2], lnst[:, tt, 1:2],
                                                AF.Exp, scale=-0.5))
                    blk.scalar(sc_inv2)

                    def v_out(v, tt=tt):
                        v.wait_ge(s_l2.h, 3 * tt + 3)
                        if tt >= 2:
                            s_out.wait(v, tt - 2)
                        s_ob.inc(v.tensor_scalar(outb[:, tt % 2, :],
                                                  s2.ap().rearrange("p a f -> p (a f)"),
                                                  lnst[:, tt, 0:1], lnst[:, tt, 1:2],
                                                  ALU.subtract, ALU.mult))
                    blk.vector(v_out)

                    def s_st_out(s, tt=tt):
                        s.wait_ge(s_ob.h, tt + 1)
                        s_out.inc(s.dma_start(out[128 * tt:128 * tt + 128, :],
                                              outb[:, tt % 2, :]), tt)
                    blk.sync(s_st_out)

                def s_fin_wait(s):
                    s.wait_ge(s_out.s[0], 16 * 4)
                    s.wait_ge(s_out.s[1], 16 * 4)
                blk.sync(s_fin_wait)

    return nc


def _prep_inputs(L0_attn_x, L0_attn_Wq, L0_attn_Wk, L0_attn_Wv, L0_attn_Wo,
                 L0_attn_q_norm_w, L0_attn_k_norm_w, L0_ffn_W_res, L0_ffn_W_pre,
                 L0_ffn_W_post, L0_ffn_H_res_base, L0_ffn_H_pre_base,
                 L0_ffn_H_post_base, L0_ffn_gate, L0_ffn_W_branch):
    f32 = lambda a: np.ascontiguousarray(np.asarray(a), dtype=np.float32)
    x = f32(L0_attn_x)
    WqT = f32(L0_attn_Wq).T
    WkT = f32(L0_attn_Wk).T
    WvT = f32(L0_attn_Wv).T
    WoT = np.ascontiguousarray(f32(L0_attn_Wo).T)
    wcat = np.concatenate([f32(L0_ffn_W_res), f32(L0_ffn_W_pre), f32(L0_ffn_W_post)],
                          axis=0)
    wcatT = np.ascontiguousarray(wcat.T)
    basetile = np.ascontiguousarray(np.tile(np.concatenate(
        [f32(L0_ffn_H_res_base).ravel(), f32(L0_ffn_H_pre_base).ravel(),
         f32(L0_ffn_H_post_base).ravel()])[None, :], (128, 1)), dtype=np.float32)
    gatet = np.ascontiguousarray(np.tile(f32(L0_ffn_gate).reshape(1, 1), (128, 1)),
                                 dtype=np.float32)
    wbr = f32(L0_ffn_W_branch)
    eye = np.ascontiguousarray(np.tile(np.eye(4, dtype=np.float32).reshape(1, 16),
                                       (128, 1)), dtype=np.float32)
    slopes_all = _alibi_slopes(H)
    wqn = np.ascontiguousarray(f32(L0_attn_q_norm_w).reshape(HD, 1))
    wkn = np.ascontiguousarray(f32(L0_attn_k_norm_w).reshape(HD, 1))

    xTs = [np.ascontiguousarray(x[b].T) for b in range(B_)]
    in_maps = []
    for c in range(8):
        b, j = divmod(c, 4)
        wq_sl = WqT[:, 1024 * j:1024 * j + 1024]
        wk_sl = WkT[:, 256 * j:256 * j + 256]
        wv_sl = WvT[:, 256 * j:256 * j + 256]
        in_maps.append({
            "xT": xTs[b],
            "xrows": np.ascontiguousarray(x[b, 1024 * j:1024 * j + 1024, :]),
            "wqkvT": np.ascontiguousarray(np.concatenate([wq_sl, wk_sl, wv_sl], axis=1)),
            "wqn": wqn, "wkn": wkn,
            "slopes": np.ascontiguousarray(
                np.tile(slopes_all[8 * j:8 * j + 8][None, :], (128, 1))),
            "woT": WoT,
            "wcatT": wcatT,
            "basetile": basetile,
            "gatet": gatet,
            "wbranch": wbr,
            "eye16": eye,
            "rowsel": np.ascontiguousarray(
                (4096 * np.arange(4, dtype=np.uint32)[None, :, None]
                 + 1024 * j + np.arange(128, dtype=np.uint32)[:, None, None]
                 + 128 * np.arange(32, dtype=np.uint32)[None, None, :])),
        })
    return in_maps


_CACHED = {}


def kernel(**inputs):
    in_maps = _prep_inputs(**inputs)
    if "full" not in _CACHED:
        _CACHED["full"] = build()
    nc = _CACHED["full"]
    res = run_bass_kernel_spmd(nc, in_maps, core_ids=list(range(8)))
    outp = np.empty((B_, S, DM), dtype=np.float32)
    for c in range(8):
        b, j = divmod(c, 4)
        outp[b, 1024 * j:1024 * j + 1024, :] = res.results[c]["out"]
    return outp


# version 13
# speedup vs baseline: 1.6322x; 1.0818x over previous
import math
import os
from contextlib import ExitStack
_NTCB = int(os.environ.get("K_NTCB", "16"))
_P1MODE = os.environ.get("K_P1MODE", "full")

import numpy as np
import concourse.bass as bass
import concourse.mybir as mybir
from concourse.bass import IndirectOffsetOnAxis
from concourse.bass_utils import run_bass_kernel_spmd

F32 = mybir.dt.float32
BF16 = mybir.dt.bfloat16
U32 = mybir.dt.uint32
AF = mybir.ActivationFunctionType
ALU = mybir.AluOpType
AX = mybir.AxisListType
ds = bass.ds

B_, S, DM = 2, 4096, 4096
H, HKV, HD = 32, 8, 128
HPC, KVPC = 8, 2
TQ = 1024
EPS_QK = 1e-6
EPS_LN = 1e-5
RMS_EPS = 1.1920929e-07
NEG = -1e30
NT = S // 128
NSUP = S // 512


def _alibi_slopes(n_heads):
    start = 2 ** (-(2 ** -(math.log2(n_heads) - 3)))
    return np.array([start * start ** i for i in range(n_heads)], dtype=np.float32)


class Sem:
    def __init__(self, ctx, nc, name):
        self.h = ctx.enter_context(nc.semaphore(name))
        self.val = 0

    def inc(self, instr, n=1):
        instr.then_inc(self.h, n)
        self.val += n
        return instr


class PingSem:

    def __init__(self, ctx, nc, name, n):
        self.s = [ctx.enter_context(nc.semaphore(f"{name}{i}")) for i in range(n)]
        self.n = n

    def inc(self, instr, k):
        instr.then_inc(self.s[k % self.n], 16)
        return instr

    def wait(self, eng, k):
        eng.wait_ge(self.s[k % self.n], 16 * (k // self.n + 1))


def build(phases=("p1", "p2", "p3", "p4"), debug_outs=()):
    nc = bass.Bass("TRN2", target_bir_lowering=False)

    xT = nc.dram_tensor("xT", [DM, S], F32, kind="ExternalInput").ap()
    xrows = nc.dram_tensor("xrows", [TQ, DM], F32, kind="ExternalInput").ap()
    wqkvT = nc.dram_tensor("wqkvT", [DM, 1536], F32, kind="ExternalInput").ap()
    wqn = nc.dram_tensor("wqn", [HD, 1], F32, kind="ExternalInput").ap()
    wkn = nc.dram_tensor("wkn", [HD, 1], F32, kind="ExternalInput").ap()
    slopes = nc.dram_tensor("slopes", [128, HPC], F32, kind="ExternalInput").ap()
    woT = nc.dram_tensor("woT", [DM, DM], F32, kind="ExternalInput").ap()
    wcatT = nc.dram_tensor("wcatT", [DM, 24], F32, kind="ExternalInput").ap()
    basetile = nc.dram_tensor("basetile", [128, 24], F32, kind="ExternalInput").ap()
    gatet = nc.dram_tensor("gatet", [128, 1], F32, kind="ExternalInput").ap()
    wbranch = nc.dram_tensor("wbranch", [1024, 1024], F32, kind="ExternalInput").ap()
    eye16 = nc.dram_tensor("eye16", [128, 16], F32, kind="ExternalInput").ap()
    rowsel = nc.dram_tensor("rowsel", [128, 4, 32], U32, kind="ExternalInput").ap()

    out = nc.dram_tensor("out", [TQ, DM], F32, kind="ExternalOutput").ap()
    dbg = {}
    for name, shape in (("qfm", [HPC * HD, S]), ("kfm", [KVPC * HD, S]),
                        ("v", [S, KVPC * 129]), ("attn", [S, HPC * HD]),
                        ("s1", [TQ, DM]), ("h1", [TQ, DM]), ("zb", [TQ, 24]),
                        ("hres", [TQ, 16]), ("consts", [128, 16])):
        if name in debug_outs:
            dbg[name] = nc.dram_tensor("dbg_" + name, shape, F32, kind="ExternalOutput").ap()

    wqkv_bf = nc.dram_tensor("wqkv_bf", [DM, 1536], BF16, kind="Internal").ap()
    cin = nc.dram_tensor("cin", [S, HPC * HD], BF16, kind="Internal").ap()
    cout = nc.dram_tensor("cout", [NSUP, 4, 512, HPC * HD], BF16, kind="Internal").ap()

    ctx = ExitStack()
    with ctx:
        sbt = lambda name, shape, dt=F32: ctx.enter_context(nc.sbuf_tensor(name, shape, dt))
        identB = sbt("identB", [128, 128], BF16)
        cq = sbt("cq", [128, 1])
        slp = sbt("slp", [128, HPC])
        strips = sbt("strips", [128, HPC, NT])
        maskT = sbt("maskT", [128, 4, 512], BF16)
        wc_sb = sbt("wc_sb", [128, 32, 24], BF16)
        base_s = sbt("base_s", [128, 24])
        base10 = sbt("base10", [128, 16])
        eye_s = sbt("eye_s", [128, 16])
        gsig = sbt("gsig", [128, 4])
        onesb = sbt("onesb", [1, 128], BF16)
        epss = sbt("epss", [128, 4])
        rsel = ctx.enter_context(nc.sbuf_tensor("rsel", [128, 4, 32], U32))
        smalls = sbt("smalls", [128, 4])

        s_init = Sem(ctx, nc, "s_init")
        s_cst = Sem(ctx, nc, "s_cst")
        s_dbg = Sem(ctx, nc, "s_dbg")
        n_dbg = [0]

        with nc.Block() as blk:
            s_gp = Sem(ctx, nc, "s_gp")

            def g_body(g):
                nms = 0
                g.memset(identB[:, :], 1.0).then_inc(s_gp.h, 1); nms += 1
                for i in range(4):
                    g.memset(maskT[:, i, :], 0.0).then_inc(s_gp.h, 1); nms += 1
                g.memset(epss[:, 0:1], EPS_QK)
                g.memset(epss[:, 1:2], EPS_LN)
                g.memset(epss[:, 2:3], RMS_EPS)
                g.memset(epss[:, 3:4], -8.0)
                g.memset(onesb[:, :], 1.0)
                s_init.inc(g.iota(strips[:, 0, :], [[128, NT]], base=0,
                                  channel_multiplier=1,
                                  allow_small_or_imprecise_dtypes=True))
                g.wait_ge(s_gp.h, nms)
                s_gp.val += nms
                g.affine_select(identB[:, :], identB[:, :], [[1, 128]], ALU.is_equal,
                                0.0, base=0, channel_multiplier=-1)
                for i in range(4):
                    g.affine_select(maskT[:, i, :], maskT[:, i, :], [[1, 512]],
                                    ALU.is_ge, NEG, base=-128 * i, channel_multiplier=-1)
            blk.gpsimd(g_body)

            s_ld0 = Sem(ctx, nc, "s_ld0")
            s_ldg = Sem(ctx, nc, "s_ldg")

            def s_body(s):
                s_ld0.inc(s.dma_start(slp[:, :], slopes), 16)
                s_ld0.inc(s.dma_start(smalls[:, 0:1], wqn), 16)
                s_ld0.inc(s.dma_start(smalls[:, 1:2], wkn), 16)
                s_ld0.inc(s.dma_start(base_s[:, :], basetile), 16)
                s_ld0.inc(s.dma_start(eye_s[:, :], eye16), 16)
                s_ld0.inc(s.dma_start(gsig[:, 0:1], gatet), 16)
                s_ld0.inc(s.dma_start(rsel[:, :, :], rowsel), 16)
            blk.sync(s_body)

            def gc_body(g):
                s_ldg.inc(g.dma_start(wc_sb[:, :, :],
                                      wcatT.rearrange("(c p) o -> p c o", p=128)), 16)
            blk.gpsimd(gc_body)

            def sc0_body(sc):
                sc.wait_ge(s_ld0.h, s_ld0.val)
                s_init.inc(sc.activation(gsig[:, 2:3], gsig[:, 0:1], AF.Exp, scale=-1.0))
            blk.scalar(sc0_body)

            def v0_body(v):
                v.wait_ge(s_ld0.h, s_ld0.val)
                v.wait_ge(s_init.h, s_init.val)
                v.tensor_scalar_add(gsig[:, 2:3], gsig[:, 2:3], 1.0)
                v.drain()
                v.reciprocal(gsig[:, 0:1], gsig[:, 2:3])
                v.tensor_mul(cq[:, :], smalls[:, 0:1], smalls[:, 1:2])
                v.drain()
                v.tensor_scalar(gsig[:, 1:2], gsig[:, 0:1], -1.0, 2.0, ALU.mult, ALU.add)
                v.tensor_scalar_mul(cq[:, :], cq[:, :], 1.0 / math.sqrt(HD))
                v.drain()
                for h in range(HPC - 1, -1, -1):
                    v.tensor_scalar_mul(strips[:, h, :], strips[:, 0, :], slp[:, h:h + 1])
                s_init.inc(v.tensor_scalar_mul(base10[:, :], base_s[:, 0:16], 1.0 / 0.1))
            blk.vector(v0_body)

        with ExitStack() as p4s:
            e4 = p4s.enter_context
            blk = e4(nc.Block())
            cb0 = e4(nc.sbuf_tensor("castb0", [128, 2048], BF16))
            cb1 = e4(nc.sbuf_tensor("castb1", [128, 2048], BF16))
            cbs = [cb0, cb1]
            jobs = [(wqkvT, wqkv_bf, DM, 1536)]
            s_ca = PingSem(ctx, nc, "s_ca", 2)
            s_csp = PingSem(ctx, nc, "s_csp", 2)
            ncast = [0]

            def gcast(g):
                k = 0
                for src, dst, rows, cols in jobs:
                    src3 = src.rearrange("(n p) c -> n p c", p=128)
                    for n in range(rows // 128):
                        for c0 in range(0, cols, 2048):
                            cw = min(2048, cols - c0)
                            if k >= 2:
                                s_csp.wait(g, k - 2)
                            s_ca.inc(g.dma_start(cbs[k % 2][:, 0:cw],
                                                 src3[n, :, c0:c0 + cw]), k)
                            k += 1
                ncast[0] = k
            blk.gpsimd(gcast)

            def scast(s):
                k = 0
                for src, dst, rows, cols in jobs:
                    dst3 = dst.rearrange("(n p) c -> n p c", p=128)
                    for n in range(rows // 128):
                        for c0 in range(0, cols, 2048):
                            cw = min(2048, cols - c0)
                            s_ca.wait(s, k)
                            s_csp.inc(s.dma_start(dst3[n, :, c0:c0 + cw],
                                                  cbs[k % 2][:, 0:cw]), k)
                            k += 1
            blk.sync(scast)

        with nc.sbuf_tensor("qfm", [128, HPC, S], BF16) as qfm, \
                nc.sbuf_tensor("kfm", [128, KVPC, S], BF16) as kfm, \
                nc.sbuf_tensor("vp", [128, KVPC, NT, 129], BF16) as vp:

            if "p1" in phases:
                with nc.Block() as blk, \
                        nc.sbuf_tensor("xts", [128, 2, 32, 256], BF16) as xts, \
                        nc.sbuf_tensor("wbb", [128, 2, 32, 256], BF16) as wbb, \
                        nc.sbuf_tensor("qsc", [128, 256], BF16) as qsc, \
                        nc.sbuf_tensor("qtm", [128, 16, 256], BF16) as qtm, \
                        nc.sbuf_tensor("ssb", [128, 8], F32) as ssb, \
                        nc.psum_tensor("pa0", [128, 256], F32) as pa0, \
                        nc.psum_tensor("pa1", [128, 256], F32) as pa1, \
                        nc.psum_tensor("pa2", [128, 256], F32) as pa2, \
                        nc.psum_tensor("pa3", [128, 256], F32) as pa3, \
                        nc.psum_tensor("ptp0", [128, 128], BF16) as ptp0, \
                        nc.psum_tensor("ptp1", [128, 128], BF16) as ptp1:
                    paccs = [pa0, pa1, pa2, pa3]
                    ptps = [ptp0, ptp1]
                    s_x = PingSem(ctx, nc, "s_x", 2)
                    s_w = PingSem(ctx, nc, "s_w", 2)
                    s_mm = Sem(ctx, nc, "s_mm")
                    s_sq = Sem(ctx, nc, "s_sq")
                    s_nrm = Sem(ctx, nc, "s_nrm")
                    s_tp = Sem(ctx, nc, "s_tp")
                    s_ev = Sem(ctx, nc, "s_ev")

                    xT3 = xT.rearrange("(c p) t -> p c t", p=128)
                    wq3 = wqkv_bf.rearrange("(c p) o -> p c o", p=128)

                    def tile_info(n):
                        tc, r = divmod(n, 12)
                        oh, t128 = divmod(r, 2)
                        return tc, oh, t128

                    ntp_before = [0] * 194
                    for n in range(192):
                        _, oh, _ = tile_info(n)
                        ntp_before[n + 1] = ntp_before[n] + (0 if oh == 5 else 2)

                    def gx(g):
                        g.wait_ge(s_csp.s[0], 16 * 16)
                        g.wait_ge(s_csp.s[1], 16 * 16)
                        g.memset(vp[:, 0, :, 128:129], 1.0)
                        g.memset(vp[:, 1, :, 128:129], 1.0)
                        for tc in range(_NTCB):
                            if tc >= 2 and _P1MODE not in ("mm",):
                                g.wait_ge(s_nrm.h, (tc - 1) * 12)
                            s_x.inc(g.dma_start(xts[:, tc % 2, :, :],
                                                xT3[:, :, 256 * tc:256 * tc + 256]), tc)
                    blk.gpsimd(gx)

                    def sw(s):
                        s.wait_ge(s_csp.s[0], 16 * 16)
                        s.wait_ge(s_csp.s[1], 16 * 16)
                        k = 0
                        for tc in range(_NTCB):
                            for oh in range(6):
                                if k >= 2:
                                    s.wait_ge(s_mm.h, max(0, 2 * k - 2))
                                c0 = 256 * (tc * 0 + oh)
                                s_w.inc(s.dma_start(wbb[:, k % 2, :, :],
                                                    wq3[:, :, c0:c0 + 256]), k)
                                k += 1
                    blk.sync(sw)

                    def emit_transposes(t, tcb):
                        for r in range(12):
                            n = tcb * 12 + r
                            _, oh, _ = tile_info(n)
                            if oh == 5:
                                continue
                            for hh in range(2):
                                m = ntp_before[n] + hh
                                if m >= 2:
                                    t.wait_ge(s_ev.h, m - 1)
                                t.wait_ge(s_nrm.h, n + 1)
                                s_tp.inc(t.matmul(ptps[m % 2][:, :],
                                         qtm[:, n % 16, 128 * hh:128 * hh + 128],
                                         identB[:, :], is_transpose=True,
                                         start=True, stop=True))

                    def emit_evacs(sc, tcb):
                        for r in range(12):
                            n = tcb * 12 + r
                            tc2, oh, t128 = tile_info(n)
                            if oh == 5:
                                continue
                            gtok = tc2 * 2 + t128
                            for hh in range(2):
                                m = ntp_before[n] + hh
                                sc.wait_ge(s_tp.h, m + 1)
                                if oh < 4:
                                    head = oh * 2 + hh
                                    ei = sc.activation(qfm[:, head, 128 * gtok:128 * gtok + 128],
                                                       ptps[m % 2][:, :], AF.Copy,
                                                       scale=cq[:, :])
                                else:
                                    ei = sc.activation(kfm[:, hh, 128 * gtok:128 * gtok + 128],
                                                       ptps[m % 2][:, :], AF.Copy)
                                s_ev.inc(ei)

                    for tcb in range(_NTCB):
                        def t_body(t, tcb=tcb):
                            if tcb > 0 and _P1MODE in ("full", "notp_dbg"):
                                emit_transposes(t, tcb - 1)
                            s_x.wait(t, tcb)
                            for r in range(12):
                                n = tcb * 12 + r
                                oh, t128 = divmod(r, 2)
                                s_w.wait(t, tcb * 6 + oh)
                                if n >= 4:
                                    t.wait_ge(s_nrm.h, n - 3)
                                acc = paccs[n % 4]
                                for d in range(32):
                                    mi = t.matmul(acc[:, :],
                                                  xts[:, tcb % 2, d, 128 * t128:128 * t128 + 128],
                                                  wbb[:, (tcb * 6 + oh) % 2, d, :],
                                                  start=(d == 0), stop=(d == 31))
                                s_mm.inc(mi)
                        blk.tensor(t_body)

                        def sc_body(sc, tcb=tcb):
                            if tcb > 0 and _P1MODE == "full":
                                emit_evacs(sc, tcb - 1)
                            if _P1MODE == "mm":
                                for r in range(12):
                                    sc.wait_ge(s_mm.h, tcb * 12 + r + 1)
                                    s_sq.inc(sc.nop())
                                return
                            for r in range(12):
                                n = tcb * 12 + r
                                _, oh, _ = tile_info(n)
                                acc = paccs[n % 4]
                                sc.wait_ge(s_mm.h, n + 1)
                                if n >= 4:
                                    sc.wait_ge(s_nrm.h, n - 3)
                                if oh != 5:
                                    sl = 2 * (n % 4)
                                    if _P1MODE != "mmln":
                                        for hh in range(2):
                                            sc.activation(qsc[:, 128 * hh:128 * hh + 128],
                                                          acc[:, 128 * hh:128 * hh + 128],
                                                          AF.Square,
                                                          accum_out=ssb[:, sl + hh:sl + hh + 1])
                                    if _P1MODE != "mmsqonly":
                                        sc.drain()
                                        sc.activation(ssb[:, sl:sl + 2], ssb[:, sl:sl + 2],
                                                      AF.Ln, scale=1.0 / HD, bias=epss[:, 0:1])
                                        sc.drain()
                                        li = sc.activation(ssb[:, sl:sl + 2], ssb[:, sl:sl + 2],
                                                           AF.Exp, scale=-0.5)
                                        s_sq.inc(li)
                                    else:
                                        sc.drain()
                                        s_sq.inc(sc.nop())
                                else:
                                    sc.drain()
                                    s_sq.inc(sc.nop())
                        blk.scalar(sc_body)

                        def v_body(v, tcb=tcb):
                            if _P1MODE in ("mm", "mmsq"):
                                for r in range(12):
                                    v.wait_ge(s_sq.h, tcb * 12 + r + 1)
                                    s_nrm.inc(v.nop())
                                return
                            for r in range(12):
                                n = tcb * 12 + r
                                tc2, oh, t128 = tile_info(n)
                                acc = paccs[n % 4]
                                v.wait_ge(s_sq.h, n + 1)
                                gtok = tc2 * 2 + t128
                                if oh != 5:
                                    if n >= 16 and _P1MODE == "full":
                                        v.wait_ge(s_tp.h, ntp_before[n - 15])
                                    sl = 2 * (n % 4)
                                    for hh in range(2):
                                        vi = v.tensor_scalar_mul(qtm[:, n % 16, 128 * hh:128 * hh + 128],
                                                                 acc[:, 128 * hh:128 * hh + 128],
                                                                 ssb[:, sl + hh:sl + hh + 1])
                                else:
                                    for hh in range(2):
                                        vi = v.tensor_copy(vp[:, hh, gtok, 0:128],
                                                           acc[:, 128 * hh:128 * hh + 128])
                                s_nrm.inc(vi)
                        blk.vector(v_body)

                    if _P1MODE == "full":
                        blk.tensor(lambda t: emit_transposes(t, _NTCB - 1))
                        blk.scalar(lambda sc: emit_evacs(sc, _NTCB - 1))

                if dbg.keys() & {"qfm", "kfm", "v", "consts"}:
                    with nc.Block() as blk:
                        def dqkv(g):
                            if "qfm" in dbg:
                                q2 = dbg["qfm"].rearrange("(h p) t -> p h t", p=128)
                                s_dbg.inc(g.dma_start(q2, qfm[:, :, :]), 16)
                                n_dbg[0] += 1
                            if "kfm" in dbg:
                                k2 = dbg["kfm"].rearrange("(h p) t -> p h t", p=128)
                                s_dbg.inc(g.dma_start(k2, kfm[:, :, :]), 16)
                                n_dbg[0] += 1
                            if "consts" in dbg:
                                s_dbg.inc(g.dma_start(dbg["consts"][:, 0:4], smalls[:, :]), 16)
                                s_dbg.inc(g.dma_start(dbg["consts"][:, 4:5], cq[:, :], allow_slow_non_contiguous=True), 16)
                                s_dbg.inc(g.dma_start(dbg["consts"][:, 5:9], gsig[:, :]), 16)
                                s_dbg.inc(g.dma_start(dbg["consts"][:, 9:13], epss[:, :]), 16)
                            if "v" in dbg:
                                v2 = dbg["v"].rearrange("(n p) (g c) -> p n g c", p=128, g=KVPC)
                                for gg in range(KVPC):
                                    s_dbg.inc(g.dma_start(
                                        v2[:, :, gg, :], vp[:, gg, :, :]), 16)
                                n_dbg[0] += 2
                            g.wait_ge(s_dbg.h, s_dbg.val)
                        blk.gpsimd(dqkv)

            if "p2" in phases:
                with ExitStack() as aosx:
                    aouts = [aosx.enter_context(nc.sbuf_tensor(
                        f"aout{s_}", [128, 4, HPC, 128], BF16)) for s_ in range(NSUP)]
                    with ExitStack() as p2s:
                        e2 = p2s.enter_context
                        blk = e2(nc.Block())
                        ptb = e2(nc.sbuf_tensor("ptb", [128, 2, 512], BF16))
                        negmT = e2(nc.sbuf_tensor("negmT", [1, HPC, 512], BF16))
                        qsq = e2(nc.sbuf_tensor("qsq", [128, 8, 512], BF16))
                        t1row = e2(nc.sbuf_tensor("t1row", [1, 2, 512], F32))
                        t2row = e2(nc.sbuf_tensor("t2row", [1, 512], F32))
                        jrow = e2(nc.sbuf_tensor("jrow", [1, 512], F32))
                        kmsq = e2(nc.sbuf_tensor("kmsq", [1, 2], F32))
                        kqb = e2(nc.sbuf_tensor("kqb", [1, 16], F32))
                        onesc = e2(nc.sbuf_tensor("onesc", [128, 1], BF16))
                        rcp = e2(nc.sbuf_tensor("rcp", [128, 4], F32))
                        pst0 = e2(nc.psum_tensor("pst0", [128, 512], F32))
                        pst1 = e2(nc.psum_tensor("pst1", [128, 512], F32))
                        pss0 = e2(nc.psum_tensor("pss0", [128, 512], F32))
                        pss1 = e2(nc.psum_tensor("pss1", [128, 512], F32))
                        ppv0 = e2(nc.psum_tensor("ppv0", [128, 129], F32))
                        ppv1 = e2(nc.psum_tensor("ppv1", [128, 129], F32))
                        ppv2 = e2(nc.psum_tensor("ppv2", [128, 129], F32))
                        ppv3 = e2(nc.psum_tensor("ppv3", [128, 129], F32))
                        psts = [pst0, pst1]
                        psss = [pss0, pss1]
                        ppvs = [ppv0, ppv1, ppv2, ppv3]
                        s_i2 = Sem(ctx, nc, "s_i2")
                        s_qs = Sem(ctx, nc, "s_qs")
                        s_qm = Sem(ctx, nc, "s_qm")
                        s_kvr = Sem(ctx, nc, "s_kvr")
                        s_kv = Sem(ctx, nc, "s_kv")
                        s_sqr = Sem(ctx, nc, "s_sqr")
                        s_ne = Sem(ctx, nc, "s_ne")
                        s_st = Sem(ctx, nc, "s_st")
                        s_pt = Sem(ctx, nc, "s_pt")
                        s_pv = Sem(ctx, nc, "s_pv")
                        s_fin = Sem(ctx, nc, "s_fin")

                        nb = [0]
                        nfin = [0]
                        nb_end = {}
                        _acc = 0
                        for _s in range(NSUP):
                            for _h in range(HPC):
                                _acc += 4 * (_s + 1)
                                nb_end[(_s, _h)] = _acc

                        def g_i2(g):
                            g.memset(onesc[:, :], 1.0).then_inc(s_i2.h, 1)
                            s_i2.val += 1
                            s_i2.inc(g.iota(jrow[0:1, :], [[1, 512]], base=0,
                                            channel_multiplier=0,
                                            allow_small_or_imprecise_dtypes=True))
                        blk.gpsimd(g_i2)

                        def sc_km(sc):
                            for gk in range(KVPC):
                                for c in range(8):
                                    idx = gk * 8 + c
                                    if idx >= 8:
                                        sc.wait_ge(s_qm.h, idx - 7)
                                    s_qs.inc(sc.activation(
                                        qsq[:, idx % 8, :],
                                        kfm[:, gk, 512 * c:512 * c + 512],
                                        AF.Square))
                        blk.scalar(sc_km)

                        def t_km(t):
                            t.wait_ge(s_i2.h, 2)
                            for gk in range(KVPC):
                                for c in range(8):
                                    idx = gk * 8 + c
                                    t.wait_ge(s_qs.h, idx + 1)
                                    if idx >= 2:
                                        t.wait_ge(s_kvr.h, idx - 1)
                                    s_qm.inc(t.matmul(psts[idx % 2][0:1, :],
                                                      onesc[:, 0:1],
                                                      qsq[:, idx % 8, :],
                                                      start=True, stop=True))
                        blk.tensor(t_km)

                        def v_km(v):
                            for gk in range(KVPC):
                                for c in range(8):
                                    idx = gk * 8 + c
                                    v.wait_ge(s_qm.h, idx + 1)
                                    s_kvr.inc(v.tensor_reduce(
                                        kqb[0:1, idx:idx + 1],
                                        psts[idx % 2][0:1, :], AX.X, ALU.max))
                                v.drain()
                                s_kv.inc(v.tensor_reduce(
                                    kmsq[0:1, gk:gk + 1],
                                    kqb[0:1, 8 * gk:8 * gk + 8], AX.X, ALU.max))
                        blk.vector(v_km)

                        def emit_chain(s2, h):
                            sqidx = 8 * s2 + h
                            gk = h // 4

                            def tQ(t, s2=s2, h=h, sqidx=sqidx):
                                t.wait_ge(s_qs.h, 16 + sqidx + 1)
                                if sqidx >= 2:
                                    t.wait_ge(s_sqr.h, sqidx - 1)
                                s_qm.inc(t.matmul(psss[sqidx % 2][0:1, :],
                                                  onesc[:, 0:1], qsq[:, h, :],
                                                  start=True, stop=True))
                            blk.tensor(tQ)

                            def sQ(sc, s2=s2, h=h, sqidx=sqidx, gk=gk):
                                if s2 >= 1:
                                    sc.wait_ge(s_qm.h, 16 + 8 * (s2 - 1) + h + 1)
                                else:
                                    sc.wait_ge(s_qm.h, 8 + h + 1)
                                s_qs.inc(sc.activation(
                                    qsq[:, h, :],
                                    qfm[:, h, 512 * s2:512 * s2 + 512], AF.Square))
                                sc.wait_ge(s_qm.h, 16 + sqidx + 1)
                                sc.wait_ge(s_kv.h, gk + 1)
                                if sqidx >= 2:
                                    sc.wait_ge(s_ne.h, sqidx - 1)
                                si = sc.activation(t1row[0:1, sqidx % 2, :],
                                                   psss[sqidx % 2][0:1, :],
                                                   AF.Sqrt,
                                                   scale=kmsq[0:1, gk:gk + 1])
                                s_sqr.inc(si)
                            blk.scalar(sQ)

                            def vQ(v, s2=s2, h=h, sqidx=sqidx):
                                v.wait_ge(s_sqr.h, sqidx + 1)
                                v.wait_ge(s_i2.h, 2)
                                if s2 >= 1:
                                    v.wait_ge(s_st.h, nb_end[(s2 - 1, h)])
                                v.tensor_scalar(t2row[0:1, :], jrow[0:1, :],
                                                512.0 * s2, slp[0:1, h:h + 1],
                                                ALU.add, ALU.mult)
                                v.drain()
                                v.tensor_add(t2row[0:1, :], t2row[0:1, :],
                                             t1row[0:1, sqidx % 2, :])
                                v.drain()
                                vi = v.tensor_scalar(negmT[0:1, h, :],
                                                     t2row[0:1, :], -1.0, -8.0,
                                                     ALU.mult, ALU.add)
                                s_ne.inc(vi)
                            blk.vector(vQ)

                        for h in range(HPC):
                            emit_chain(0, h)

                        for s in range(NSUP):
                            nch = 4 * (s + 1)
                            for h in range(HPC):
                                gk = h // 4
                                nb0 = nb[0]

                                def tB(t, s=s, h=h, gk=gk, nb0=nb0, nch=nch,
                                       fin0=nfin[0]):
                                    t.wait_ge(s_ne.h, s * 8 + h + 1)
                                    if fin0 >= 1:
                                        t.wait_ge(s_fin.h, fin0)
                                    for c in range(nch):
                                        n = nb0 + c
                                        bank = psts[n % 2]
                                        if n >= 2:
                                            t.wait_ge(s_pt.h, n - 1)
                                        t.matmul(bank[:, :], onesb[0:1, :],
                                                 negmT[0:1, h, :],
                                                 start=True, stop=False)
                                        diag = (c >= nch - 4)
                                        mi = t.matmul(bank[:, :],
                                                      kfm[:, gk, 128 * c:128 * c + 128],
                                                      qfm[:, h, 512 * s:512 * s + 512],
                                                      start=False, stop=not diag)
                                        if diag:
                                            mi = t.matmul(bank[:, :], identB[:, :],
                                                          maskT[:, c - (nch - 4), :],
                                                          start=False, stop=True)
                                        s_st.inc(mi)
                                        if c >= 1:
                                            t.wait_ge(s_pt.h, nb0 + c)
                                            for i in range(4):
                                                pvi = t.matmul(
                                                    ppvs[i][:, :],
                                                    ptb[:, (nb0 + c - 1) % 2,
                                                        128 * i:128 * i + 128],
                                                    vp[:, gk, c - 1, :],
                                                    start=(c == 1), stop=False)
                                            s_pv.inc(pvi)
                                    t.wait_ge(s_pt.h, nb0 + nch)
                                    for i in range(4):
                                        pvi = t.matmul(ppvs[i][:, :],
                                                       ptb[:, (nb0 + nch - 1) % 2,
                                                           128 * i:128 * i + 128],
                                                       vp[:, gk, nch - 1, :],
                                                       start=(nch == 1), stop=True)
                                    s_pv.inc(pvi)
                                blk.tensor(tB)

                                def sB(sc, s=s, h=h, nb0=nb0, nch=nch):
                                    for c in range(nch):
                                        n = nb0 + c
                                        bank = psts[n % 2]
                                        sc.wait_ge(s_st.h, n + 1)
                                        if n >= 2:
                                            sc.wait_ge(s_pv.h, n - 1)
                                        s_pt.inc(sc.activation(
                                            ptb[:, n % 2, :], bank[:, :],
                                            AF.Exp, bias=strips[:, h, c:c + 1]))
                                blk.scalar(sB)

                                def vB(v, s=s, h=h, nb0=nb0, nch=nch):
                                    v.wait_ge(s_pv.h, nb0 + nch)
                                    for i in range(4):
                                        v.reciprocal(rcp[:, i:i + 1],
                                                     ppvs[i][:, 128:129])
                                    v.drain()
                                    for i in range(4):
                                        fi = v.tensor_scalar_mul(aouts[s][:, i, h, :],
                                                                 ppvs[i][:, 0:128],
                                                                 rcp[:, i:i + 1])
                                    s_fin.inc(fi)
                                blk.vector(vB)
                                nfin[0] += 1
                                nb[0] += nch
                                if s + 1 < NSUP:
                                    emit_chain(s + 1, h)

                        s_ci = Sem(ctx, nc, "s_ci")
                        s_cc = Sem(ctx, nc, "s_cc")

                        def gex(g):
                            cin4 = cin.rearrange("(n p) (h d) -> p n h d", p=128, h=HPC)
                            for s in range(NSUP):
                                g.wait_ge(s_fin.h, 8 * (s + 1))
                                s_ci.inc(g.dma_start(cin4[:, 4 * s:4 * s + 4, :, :],
                                                     aouts[s][:, :, :, :]), 16)
                                g.wait_ge(s_ci.h, 16 * (s + 1))
                                if s >= 1:
                                    g.wait_ge(s_cc.h, s)
                                s_cc.inc(g.collective_compute(
                                    "AllGather", ALU.bypass,
                                    replica_groups=[[0, 1, 2, 3], [4, 5, 6, 7]],
                                    ins=[cin[512 * s:512 * s + 512, :]],
                                    outs=[cout[s]]), 1)
                        blk.gpsimd(gex)

                    if "attn" in dbg:
                        with nc.Block() as blk:
                            def datt(g):
                                a2 = dbg["attn"].rearrange("(n p) f -> p n f", p=128)
                                for s_ in range(NSUP):
                                    s_dbg.inc(g.dma_start(
                                        a2[:, 4 * s_:4 * s_ + 4, :],
                                        aouts[s_].ap().rearrange("p n h d -> p n (h d)")), 16)
                                g.wait_ge(s_dbg.h, s_dbg.val)
                            blk.gpsimd(datt)

        if "p3" in phases:
            s_bf = ctx.enter_context(nc.sbuf_tensor("s_bf", [128, 8, DM], BF16))
            lnst = ctx.enter_context(nc.sbuf_tensor("lnst", [128, 8, 4], F32))

            with ExitStack() as p3s:
                e3 = p3s.enter_context
                afm = e3(nc.sbuf_tensor("afm", [128, 32, 1024], BF16))
                s_at = Sem(ctx, nc, "s_at")
                s_tq = Sem(ctx, nc, "s_tq")
                s_ae = Sem(ctx, nc, "s_ae")
                s_wl = PingSem(ctx, nc, "s_wl", 2)
                s_xr = PingSem(ctx, nc, "s_xr", 2)
                s_ym = Sem(ctx, nc, "s_ym")
                s_ye = Sem(ctx, nc, "s_ye")

                coutf = cout.rearrange("s r t f -> (s r t) f")

                with ExitStack() as p3g:
                    eg = p3g.enter_context
                    blkg = eg(nc.Block())
                    attm = eg(nc.sbuf_tensor("attm", [128, 8, 1024], BF16))
                    ptq0 = eg(nc.psum_tensor("ptq0", [128, 512], BF16))
                    ptq1 = eg(nc.psum_tensor("ptq1", [128, 512], BF16))
                    ptqs = [ptq0, ptq1]

                    def g_loads(g):
                        g.wait_ge(s_ld0.h, s_ld0.val)
                        g.wait_ge(s_cc.h, NSUP)
                        for tt in range(8):
                            if tt >= 2:
                                g.wait_ge(s_tq.h, 8 * (tt - 1))
                            for r in range(4):
                                s_at.inc(g.indirect_dma_start(
                                    attm[:, 4 * (tt % 2) + r, :], None, coutf,
                                    IndirectOffsetOnAxis(rsel[:, r, tt:tt + 1], 0)), 16)
                    blkg.gpsimd(g_loads)

                    def t_tr(t):
                        k = 0
                        for tt in range(8):
                            for r in range(4):
                                t.wait_ge(s_at.h, 16 * (4 * tt + r + 1))
                                for u in range(2):
                                    m = 2 * k + u
                                    if m >= 2:
                                        t.wait_ge(s_ae.h, m - 1)
                                    for q in range(4):
                                        fc = 4 * u + q
                                        ti = t.matmul(ptqs[m % 2][:, 128 * q:128 * q + 128],
                                                      attm[:, 4 * (tt % 2) + r,
                                                           128 * fc:128 * fc + 128],
                                                      identB[:, :], is_transpose=True,
                                                      start=(q == 0), stop=(q == 3))
                                    s_tq.inc(ti)
                                k += 1
                    blkg.tensor(t_tr)

                    def sc_ae(sc):
                        k = 0
                        for tt in range(8):
                            for r in range(4):
                                for u in range(2):
                                    m = 2 * k + u
                                    sc.wait_ge(s_tq.h, m + 1)
                                    fc0 = 8 * r + 4 * u
                                    s_ae.inc(sc.activation(
                                        afm[:, fc0:fc0 + 4, 128 * tt:128 * tt + 128],
                                        ptqs[m % 2][:, :].rearrange("p (a b) -> p a b", b=128),
                                        AF.Copy))
                                k += 1
                    blkg.scalar(sc_ae)

                blk = e3(nc.Block())
                wsb = e3(nc.sbuf_tensor("wsb", [128, 2, 32, 512], BF16))
                xrb = e3(nc.sbuf_tensor("xrb", [128, 2, 512], F32))
                py0 = e3(nc.psum_tensor("py0", [128, 512], F32))
                py1 = e3(nc.psum_tensor("py1", [128, 512], F32))
                py2 = e3(nc.psum_tensor("py2", [128, 512], F32))
                py3 = e3(nc.psum_tensor("py3", [128, 512], F32))
                pys = [py0, py1, py2, py3]

                wo3 = woT.rearrange("(c p) m -> p c m", p=128)

                def s_wxload(g):
                    for mh in range(8):
                        if mh >= 2:
                            g.wait_ge(s_ym.h, (mh - 1) * 8)
                        s_wl.inc(g.dma_start(wsb[:, mh % 2, :, :],
                                             wo3[:, :, 512 * mh:512 * mh + 512]), mh)
                        for tt in range(8):
                            k = mh * 8 + tt
                            if k >= 2:
                                g.wait_ge(s_ye.h, k - 1)
                            s_xr.inc(g.dma_start(
                                xrb[:, k % 2, :],
                                xrows[128 * tt:128 * tt + 128,
                                      512 * mh:512 * mh + 512]), k)
                blk.gpsimd(s_wxload)

                def t_wo(t):
                    for mh in range(8):
                        s_wl.wait(t, mh)
                        for tt in range(8):
                            k = mh * 8 + tt
                            acc = pys[k % 4]
                            t.wait_ge(s_ae.h, 8 * (tt + 1))
                            if k >= 4:
                                t.wait_ge(s_ye.h, k - 3)
                            for fc in range(32):
                                yi = t.matmul(acc[:, :], afm[:, fc, 128 * tt:128 * tt + 128],
                                              wsb[:, mh % 2, fc, :],
                                              start=(fc == 0), stop=(fc == 31))
                            s_ym.inc(yi)
                blk.tensor(t_wo)

                def v_wo(v):
                    for mh in range(8):
                        for tt in range(8):
                            k = mh * 8 + tt
                            acc = pys[k % 4]
                            v.wait_ge(s_ym.h, k + 1)
                            s_xr.wait(v, k)
                            s_ye.inc(v.scalar_tensor_tensor(
                                s_bf[:, tt, 512 * mh:512 * mh + 512], acc[:, :],
                                1.0, xrb[:, k % 2, :], ALU.mult, ALU.add))
                blk.vector(v_wo)

            if "s1" in dbg:
                with nc.Block() as blk:
                    def ds1(g):
                        d2 = dbg["s1"].rearrange("(n p) f -> p n f", p=128)
                        s_dbg.inc(g.dma_start(d2, s_bf[:, :, :]), 16)
                        g.wait_ge(s_dbg.h, s_dbg.val)
                    blk.gpsimd(ds1)

        if "p4" in phases:
            with ExitStack() as p4b:
                eb4 = p4b.enter_context
                blk = eb4(nc.Block())
                junk = eb4(nc.sbuf_tensor("junk", [128, DM], BF16))
                stat4 = eb4(nc.sbuf_tensor("stat4", [128, 32], F32))
                h1f = eb4(nc.sbuf_tensor("h1f", [128, 32, 128], BF16))
                zbuf = eb4(nc.sbuf_tensor("zbuf", [128, 8, 24], F32))
                z0 = eb4(nc.sbuf_tensor("z0", [128, 8, 16], F32))
                zp = eb4(nc.sbuf_tensor("zp", [128, 8, 16], F32))
                tsk = eb4(nc.sbuf_tensor("tsk", [128, 128], F32))
                esk = eb4(nc.sbuf_tensor("esk", [128, 128], F32))
                msk = eb4(nc.sbuf_tensor("msk", [128, 32], F32))
                ulse = eb4(nc.sbuf_tensor("ulse", [128, 32], F32))
                vlse = eb4(nc.sbuf_tensor("vlse", [128, 32], F32))
                hres = eb4(nc.sbuf_tensor("hres", [128, 8, 16], F32))
                hpp = eb4(nc.sbuf_tensor("hpp", [128, 8, 8], F32))
                invt = eb4(nc.sbuf_tensor("invt", [128, 8, 2], F32))
                xp = eb4(nc.sbuf_tensor("xp", [128, 1024], BF16))
                xpf = eb4(nc.sbuf_tensor("xpf", [128, 1024], F32))
                xpfm = eb4(nc.sbuf_tensor("xpfm", [128, 8, 128], BF16))
                wbsb = eb4(nc.sbuf_tensor("wbsb", [128, 8, 1024], BF16))
                ybf = eb4(nc.sbuf_tensor("ybf", [128, 8, 1024], BF16))
                s2 = eb4(nc.sbuf_tensor("s2", [128, 4, 1024], F32))
                outb = eb4(nc.sbuf_tensor("outb", [128, 2, DM], F32))
                ph0 = eb4(nc.psum_tensor("ph0", [128, 512], BF16))
                ph1 = eb4(nc.psum_tensor("ph1", [128, 512], BF16))
                plg = eb4(nc.psum_tensor("plg", [128, 24], F32))
                pyb0 = eb4(nc.psum_tensor("pyb0", [128, 512], F32))
                pyb1 = eb4(nc.psum_tensor("pyb1", [128, 512], F32))
                phs = [ph0, ph1]
                pybs = [pyb0, pyb1]
                s_sq1 = Sem(ctx, nc, "s_sq1")
                s_vst = Sem(ctx, nc, "s_vst")
                s_iv1 = Sem(ctx, nc, "s_iv1")
                s_h1 = Sem(ctx, nc, "s_h1")
                s_rm2 = Sem(ctx, nc, "s_rm2")
                s_ht = Sem(ctx, nc, "s_ht")
                s_he = Sem(ctx, nc, "s_he")
                s_lg = Sem(ctx, nc, "s_lg")
                s_zb = Sem(ctx, nc, "s_zb")
                s_sk = Sem(ctx, nc, "s_sk")
                s_xp = Sem(ctx, nc, "s_xp")
                s_xt = Sem(ctx, nc, "s_xt")
                s_xe = Sem(ctx, nc, "s_xe")
                s_yb = Sem(ctx, nc, "s_yb")
                s_yv = Sem(ctx, nc, "s_yv")
                s_s2 = Sem(ctx, nc, "s_s2")
                s_l2 = Sem(ctx, nc, "s_l2")
                s_ob = Sem(ctx, nc, "s_ob")
                s_out = PingSem(ctx, nc, "s_out", 2)
                s_wbl = Sem(ctx, nc, "s_wbl")

                def g_wb(g):
                    wb3 = wbranch.rearrange("(c p) m -> p c m", p=128)
                    s_wbl.inc(g.dma_start(wbsb[:, :, :], wb3), 16)
                blk.gpsimd(g_wb)

                def sc_ln1(sc):
                    for tt in range(8):
                        if tt:
                            sc.drain()
                        s_sq1.inc(sc.activation(junk[:, :], s_bf[:, tt, :], AF.Square,
                                                 accum_out=stat4[:, tt:tt + 1]))
                blk.scalar(sc_ln1)

                def v_ln1(v):
                    v.wait_ge(s_ld0.h, s_ld0.val)
                    for tt in range(8):
                        v.wait_ge(s_sq1.h, tt + 1)
                        v.reduce_sum(stat4[:, 8 + tt:8 + tt + 1], s_bf[:, tt, :], AX.X)
                        v.drain()
                        v.tensor_scalar_mul(lnst[:, tt, 0:1],
                                            stat4[:, 8 + tt:8 + tt + 1], 1.0 / DM)
                        v.tensor_scalar_mul(lnst[:, tt, 3:4], stat4[:, tt:tt + 1], 1.0 / DM)
                        v.drain()
                        v.tensor_mul(lnst[:, tt, 2:3], lnst[:, tt, 0:1], lnst[:, tt, 0:1])
                        v.drain()
                        s_vst.inc(v.tensor_sub(lnst[:, tt, 3:4], lnst[:, tt, 3:4],
                                                lnst[:, tt, 2:3]))
                blk.vector(v_ln1)

                def sc_inv1(sc):
                    for tt in range(8):
                        sc.wait_ge(s_vst.h, tt + 1)
                        sc.activation(lnst[:, tt, 1:2], lnst[:, tt, 3:4], AF.Ln,
                                      bias=epss[:, 1:2])
                        sc.drain()
                        s_iv1.inc(sc.activation(lnst[:, tt, 1:2], lnst[:, tt, 1:2],
                                                 AF.Exp, scale=-0.5))
                blk.scalar(sc_inv1)

                def v_h1(v):
                    for tt in range(8):
                        v.wait_ge(s_iv1.h, tt + 1)
                        s_h1.inc(v.tensor_scalar(s_bf[:, tt, :], s_bf[:, tt, :],
                                                  lnst[:, tt, 0:1], lnst[:, tt, 1:2],
                                                  ALU.subtract, ALU.mult))
                blk.vector(v_h1)

                if "h1" in dbg:
                    def dh1(g):
                        g.wait_ge(s_h1.h, 8)
                        d2 = dbg["h1"].rearrange("(n p) f -> p n f", p=128)
                        s_dbg.inc(g.dma_start(d2, s_bf[:, :, :]), 16)
                        g.wait_ge(s_dbg.h, s_dbg.val)
                    blk.gpsimd(dh1)

                def sc_rms(sc):
                    for tt in range(8):
                        sc.wait_ge(s_h1.h, tt + 1)
                        sc.drain()
                        sc.activation(junk[:, :], s_bf[:, tt, :], AF.Square,
                                      accum_out=invt[:, tt, 0:1])
                        sc.drain()
                        sc.activation(invt[:, tt, 1:2], invt[:, tt, 0:1], AF.Ln,
                                      scale=1.0 / DM, bias=epss[:, 2:3])
                        sc.drain()
                        s_rm2.inc(sc.activation(invt[:, tt, 1:2], invt[:, tt, 1:2],
                                                 AF.Exp, scale=-0.5))
                blk.scalar(sc_rms)

                for tt in range(8):
                    def t_h1t(t, tt=tt):
                        t.wait_ge(s_h1.h, tt + 1)
                        for u in range(8):
                            m = tt * 8 + u
                            if m >= 2:
                                t.wait_ge(s_he.h, m - 1)
                            for q in range(4):
                                fc = 4 * u + q
                                hi = t.matmul(phs[m % 2][:, 128 * q:128 * q + 128],
                                              s_bf[:, tt, 128 * fc:128 * fc + 128],
                                              identB[:, :], is_transpose=True,
                                              start=(q == 0), stop=(q == 3))
                            s_ht.inc(hi)
                    blk.tensor(t_h1t)

                    def sc_h1e(sc, tt=tt):
                        if tt >= 1:
                            sc.wait_ge(s_lg.h, tt)
                        for u in range(8):
                            m = tt * 8 + u
                            sc.wait_ge(s_ht.h, m + 1)
                            s_he.inc(sc.activation(
                                h1f[:, 4 * u:4 * u + 4, :],
                                phs[m % 2][:, :].rearrange("p (a b) -> p a b", b=128),
                                AF.Copy))
                    blk.scalar(sc_h1e)

                    def t_lg(t, tt=tt):
                        if tt == 0:
                            t.wait_ge(s_ldg.h, 16)
                        t.wait_ge(s_he.h, (tt + 1) * 8)
                        if tt >= 1:
                            t.wait_ge(s_zb.h, tt)
                        for fc in range(32):
                            lgi = t.matmul(plg[:, :], h1f[:, fc, :], wc_sb[:, fc, :],
                                           start=(fc == 0), stop=(fc == 31))
                        s_lg.inc(lgi)
                    blk.tensor(t_lg)

                    def v_z(v, tt=tt):
                        v.wait_ge(s_lg.h, tt + 1)
                        v.wait_ge(s_rm2.h, tt + 1)
                        v.tensor_scalar_mul(invt[:, tt, 0:1], invt[:, tt, 1:2], 10.0)
                        v.drain()
                        v.scalar_tensor_tensor(zbuf[:, tt, 0:16], plg[:, 0:16],
                                               invt[:, tt, 0:1], base10[:, :],
                                               ALU.mult, ALU.add)
                        s_zb.inc(v.scalar_tensor_tensor(
                            zbuf[:, tt, 16:24], plg[:, 16:24],
                            invt[:, tt, 1:2], base_s[:, 16:24],
                            ALU.mult, ALU.add))
                    blk.vector(v_z)

                def v_sm(v):
                    v.wait_ge(s_zb.h, 8)
                    pre = zbuf[:, :, 16:24].rearrange("p a (n f) -> p a n f", f=4)
                    v.tensor_reduce(msk[:, 0:16].rearrange("p (a n) -> p a n", n=2),
                                    pre, AX.X, ALU.max)
                    v.drain()
                    mb = msk[:, 0:16].rearrange("p (a n) -> p a n", n=2).unsqueeze(3) \
                        .broadcast_to([128, 8, 2, 4])
                    s_sk.inc(v.tensor_sub(
                        hpp.ap().rearrange("p a (n f) -> p a n f", f=4), pre, mb))
                blk.vector(v_sm)

                def sc_sm(sc):
                    sc.wait_ge(s_sk.h, 1)
                    s_sk.inc(sc.activation(hpp[:, :, :], hpp[:, :, :], AF.Exp))
                blk.scalar(sc_sm)

                def v_sm2(v):
                    v.wait_ge(s_sk.h, 2)
                    hp4 = hpp.ap().rearrange("p a (n f) -> p a n f", f=4)
                    v.tensor_reduce(msk[:, 0:16].rearrange("p (a n) -> p a n", n=2),
                                    hp4, AX.X, ALU.add)
                    v.drain()
                    v.reciprocal(msk[:, 0:16], msk[:, 0:16])
                    v.drain()
                    mb = msk[:, 0:16].rearrange("p (a n) -> p a n", n=2).unsqueeze(3) \
                        .broadcast_to([128, 8, 2, 4])
                    v.tensor_mul(hp4, hp4, mb)
                    zr = zbuf[:, :, 0:16].rearrange("p a (i j) -> p a i j", j=4)
                    v.drain()
                    v.tensor_reduce(msk[:, 0:32].rearrange("p (a i) -> p a i", i=4),
                                    zr, AX.X, ALU.max)
                    v.drain()
                    mb2 = msk[:, 0:32].rearrange("p (a i) -> p a i", i=4).unsqueeze(3) \
                        .broadcast_to([128, 8, 4, 4])
                    v.tensor_sub(z0.ap().rearrange("p a (i j) -> p a i j", j=4), zr, mb2)
                    v.drain()
                    s_sk.inc(v.tensor_copy(
                        zp.ap().rearrange("p a (j i) -> p a j i", i=4),
                        z0.ap().rearrange("p a (i j) -> p a i j", j=4)
                        .transpose([0, 1, 3, 2])))
                blk.vector(v_sm2)

                for tt in range(8):
                    def v_xp(v, tt=tt):
                        v.wait_ge(s_sk.h, 3)
                        if tt >= 1:
                            v.wait_ge(s_xt.h, 2 * tt)
                        hp = hpp[:, tt, 0:4]
                        v.tensor_scalar_mul(xpf[:, :], s_bf[:, tt, 0:1024], hp[:, 0:1])
                        for j in (1, 2):
                            v.drain()
                            v.scalar_tensor_tensor(xpf[:, :],
                                                   s_bf[:, tt, 1024 * j:1024 * j + 1024],
                                                   hp[:, j:j + 1], xpf[:, :],
                                                   ALU.mult, ALU.add)
                        v.drain()
                        s_xp.inc(v.scalar_tensor_tensor(
                            xp[:, :], s_bf[:, tt, 3072:4096],
                            hp[:, 3:4], xpf[:, :], ALU.mult, ALU.add))
                    blk.vector(v_xp)

                    def t_xpt(t, tt=tt):
                        t.wait_ge(s_xp.h, tt + 1)
                        for u in range(2):
                            m = 2 * tt + u
                            if m >= 2:
                                t.wait_ge(s_xe.h, m - 1)
                            for q in range(4):
                                fc = 4 * u + q
                                xi = t.matmul(phs[m % 2][:, 128 * q:128 * q + 128],
                                              xp[:, 128 * fc:128 * fc + 128],
                                              identB[:, :], is_transpose=True,
                                              start=(q == 0), stop=(q == 3))
                            s_xt.inc(xi)
                    blk.tensor(t_xpt)

                    def sc_xpe(sc, tt=tt):
                        if tt >= 1:
                            sc.wait_ge(s_yb.h, 2 * tt)
                        for u in range(2):
                            m = 2 * tt + u
                            sc.wait_ge(s_xt.h, m + 1)
                            s_xe.inc(sc.activation(
                                xpfm[:, 4 * u:4 * u + 4, :],
                                phs[m % 2][:, :].rearrange("p (a b) -> p a b", b=128),
                                AF.Copy))
                    blk.scalar(sc_xpe)

                    def t_y(t, tt=tt):
                        t.wait_ge(s_xe.h, 2 * (tt + 1))
                        if tt == 0:
                            t.wait_ge(s_wbl.h, 16)
                        for half in range(2):
                            k = 2 * tt + half
                            if k >= 2:
                                t.wait_ge(s_yv.h, k - 1)
                            for cc in range(8):
                                ybi = t.matmul(pybs[k % 2][:, :], xpfm[:, cc, :],
                                               wbsb[:, cc, 512 * half:512 * half + 512],
                                               start=(cc == 0), stop=(cc == 7))
                            s_yb.inc(ybi)
                    blk.tensor(t_y)

                    def v_y(v, tt=tt):
                        for half in range(2):
                            k = 2 * tt + half
                            v.wait_ge(s_yb.h, k + 1)
                            s_yv.inc(v.tensor_copy(
                                ybf[:, tt, 512 * half:512 * half + 512],
                                pybs[k % 2][:, :]))
                    blk.vector(v_y)

                z04 = z0.ap().rearrange("p a (i j) -> p a i j", j=4)
                zp4 = zp.ap().rearrange("p a (j i) -> p a j i", i=4)
                t4 = tsk.ap().rearrange("p (a i j) -> p a i j", i=4, j=4)
                e4 = esk.ap().rearrange("p (a i j) -> p a i j", i=4, j=4)
                m4 = msk.ap().rearrange("p (a i) -> p a i", i=4)
                ul4 = ulse.ap().rearrange("p (a i) -> p a i", i=4)
                vl4 = vlse.ap().rearrange("p (a j) -> p a j", j=4)
                skc = [3]

                def v_sink(v):
                    c = skc[0]
                    for it in range(50):
                        v.wait_ge(s_sk.h, c)
                        if it == 0:
                            v.tensor_copy(tsk[:, :], z0.ap().rearrange("p a f -> p (a f)"))
                        else:
                            vb = vl4.unsqueeze(2).broadcast_to([128, 8, 4, 4])
                            v.tensor_sub(t4, z04, vb)
                        v.drain()
                        v.tensor_reduce(m4, t4, AX.X, ALU.max)
                        v.drain()
                        mb = m4.unsqueeze(3).broadcast_to([128, 8, 4, 4])
                        s_sk.inc(v.tensor_sub(t4, t4, mb))
                        c += 2
                        v.wait_ge(s_sk.h, c)
                        s_sk.inc(v.tensor_reduce(ul4, e4, AX.X, ALU.add))
                        c += 2
                        v.wait_ge(s_sk.h, c)
                        v.tensor_add(ulse[:, :], ulse[:, :], msk[:, :])
                        v.drain()
                        ub = ul4.unsqueeze(2).broadcast_to([128, 8, 4, 4])
                        v.tensor_sub(t4, zp4, ub)
                        v.drain()
                        v.tensor_reduce(m4, t4, AX.X, ALU.max)
                        v.drain()
                        mb = m4.unsqueeze(3).broadcast_to([128, 8, 4, 4])
                        s_sk.inc(v.tensor_sub(t4, t4, mb))
                        c += 2
                        v.wait_ge(s_sk.h, c)
                        s_sk.inc(v.tensor_reduce(vl4, e4, AX.X, ALU.add))
                        c += 2
                        v.wait_ge(s_sk.h, c)
                        s_sk.inc(v.tensor_add(vlse[:, :], vlse[:, :], msk[:, :]))
                        c += 1
                    v.wait_ge(s_sk.h, c)
                    ub = ul4.unsqueeze(3).broadcast_to([128, 8, 4, 4])
                    vb = vl4.unsqueeze(2).broadcast_to([128, 8, 4, 4])
                    hr4 = hres.ap().rearrange("p a (i j) -> p a i j", j=4)
                    v.tensor_sub(hr4, z04, ub)
                    v.drain()
                    s_sk.inc(v.tensor_sub(hr4, hr4, vb))
                    c += 2
                    v.wait_ge(s_sk.h, c)
                    eb = eye_s[:, :].unsqueeze(1).broadcast_to([128, 8, 16])
                    v.tensor_scalar_mul(hres[:, :, :], hres[:, :, :], gsig[:, 0:1])
                    v.drain()
                    s_sk.inc(v.scalar_tensor_tensor(hres[:, :, :], eb, gsig[:, 1:2],
                                                    hres[:, :, :], ALU.mult, ALU.add))
                    c += 1
                    skc[0] = c
                blk.vector(v_sink)

                def s_sink(sc):
                    c = 3
                    for it in range(50):
                        c += 1
                        sc.wait_ge(s_sk.h, c)
                        s_sk.inc(sc.activation(esk[:, :], tsk[:, :], AF.Exp))
                        c += 1
                        c += 1
                        sc.wait_ge(s_sk.h, c)
                        s_sk.inc(sc.activation(ulse[:, :], ulse[:, :], AF.Ln))
                        c += 1
                        c += 1
                        sc.wait_ge(s_sk.h, c)
                        s_sk.inc(sc.activation(esk[:, :], tsk[:, :], AF.Exp))
                        c += 1
                        c += 1
                        sc.wait_ge(s_sk.h, c)
                        s_sk.inc(sc.activation(vlse[:, :], vlse[:, :], AF.Ln))
                        c += 1
                        c += 1
                    c += 1
                    sc.wait_ge(s_sk.h, c)
                    s_sk.inc(sc.activation(hres[:, :, :], hres[:, :, :], AF.Exp))
                    c += 1
                blk.scalar(s_sink)
                SK_DONE = s_sk.val

                if "zb" in dbg:
                    def dzb(g):
                        g.wait_ge(s_zb.h, 8)
                        d2 = dbg["zb"].rearrange("(n p) f -> p n f", p=128)
                        s_dbg.inc(g.dma_start(d2, zbuf[:, :, :]), 16)
                        g.wait_ge(s_dbg.h, s_dbg.val)
                    blk.gpsimd(dzb)
                if "hres" in dbg:
                    def dhr(g):
                        g.wait_ge(s_sk.h, SK_DONE)
                        d2 = dbg["hres"].rearrange("(n p) f -> p n f", p=128)
                        s_dbg.inc(g.dma_start(d2, hres[:, :, :]), 16)
                        g.wait_ge(s_dbg.h, s_dbg.val)
                    blk.gpsimd(dhr)

                for tt in range(8):
                    def v_mix(v, tt=tt):
                        if tt:
                            v.drain()
                        v.wait_ge(s_sk.h, SK_DONE)
                        v.wait_ge(s_yv.h, 2 * (tt + 1))
                        if tt >= 1:
                            v.wait_ge(s_l2.h, 3 * (tt - 1) + 1)
                        for i in range(4):
                            hr = hres[:, tt, :]
                            v.tensor_scalar_mul(s2[:, i, :], s_bf[:, tt, 0:1024],
                                                hr[:, 4 * i:4 * i + 1])
                            for j in (1, 2, 3):
                                v.drain()
                                v.scalar_tensor_tensor(
                                    s2[:, i, :], s_bf[:, tt, 1024 * j:1024 * j + 1024],
                                    hr[:, 4 * i + j:4 * i + j + 1],
                                    s2[:, i, :], ALU.mult, ALU.add)
                            v.drain()
                            s2i = v.scalar_tensor_tensor(s2[:, i, :], ybf[:, tt, :],
                                                          hpp[:, tt, 4 + i:4 + i + 1],
                                                          s2[:, i, :], ALU.mult, ALU.add)
                        s_s2.inc(s2i)
                    blk.vector(v_mix)

                    def sc_ln2(sc, tt=tt):
                        sc.wait_ge(s_s2.h, tt + 1)
                        sc.drain()
                        s_l2.inc(sc.activation(
                            junk[:, :], s2.ap().rearrange("p a f -> p (a f)"),
                            AF.Square, accum_out=stat4[:, 16 + tt:16 + tt + 1]))
                    blk.scalar(sc_ln2)

                    def v_ln2(v, tt=tt):
                        v.wait_ge(s_l2.h, 3 * tt + 1)
                        s2f = s2.ap().rearrange("p a f -> p (a f)")
                        v.reduce_sum(stat4[:, 24 + tt:24 + tt + 1], s2f, AX.X)
                        v.drain()
                        v.tensor_scalar_mul(lnst[:, tt, 0:1],
                                            stat4[:, 24 + tt:24 + tt + 1], 1.0 / DM)
                        v.tensor_scalar_mul(lnst[:, tt, 3:4],
                                            stat4[:, 16 + tt:16 + tt + 1], 1.0 / DM)
                        v.drain()
                        v.tensor_mul(lnst[:, tt, 2:3], lnst[:, tt, 0:1], lnst[:, tt, 0:1])
                        v.drain()
                        s_l2.inc(v.tensor_sub(lnst[:, tt, 3:4], lnst[:, tt, 3:4],
                                              lnst[:, tt, 2:3]))
                    blk.vector(v_ln2)

                    def sc_inv2(sc, tt=tt):
                        sc.wait_ge(s_l2.h, 3 * tt + 2)
                        sc.activation(lnst[:, tt, 1:2], lnst[:, tt, 3:4], AF.Ln,
                                      bias=epss[:, 1:2])
                        sc.drain()
                        s_l2.inc(sc.activation(lnst[:, tt, 1:2], lnst[:, tt, 1:2],
                                                AF.Exp, scale=-0.5))
                    blk.scalar(sc_inv2)

                    def v_out(v, tt=tt):
                        v.wait_ge(s_l2.h, 3 * tt + 3)
                        if tt >= 2:
                            s_out.wait(v, tt - 2)
                        s_ob.inc(v.tensor_scalar(outb[:, tt % 2, :],
                                                  s2.ap().rearrange("p a f -> p (a f)"),
                                                  lnst[:, tt, 0:1], lnst[:, tt, 1:2],
                                                  ALU.subtract, ALU.mult))
                    blk.vector(v_out)

                    def s_st_out(s, tt=tt):
                        s.wait_ge(s_ob.h, tt + 1)
                        s_out.inc(s.dma_start(out[128 * tt:128 * tt + 128, :],
                                              outb[:, tt % 2, :]), tt)
                    blk.sync(s_st_out)

                def s_fin_wait(s):
                    s.wait_ge(s_out.s[0], 16 * 4)
                    s.wait_ge(s_out.s[1], 16 * 4)
                blk.sync(s_fin_wait)

    return nc


def _prep_inputs(L0_attn_x, L0_attn_Wq, L0_attn_Wk, L0_attn_Wv, L0_attn_Wo,
                 L0_attn_q_norm_w, L0_attn_k_norm_w, L0_ffn_W_res, L0_ffn_W_pre,
                 L0_ffn_W_post, L0_ffn_H_res_base, L0_ffn_H_pre_base,
                 L0_ffn_H_post_base, L0_ffn_gate, L0_ffn_W_branch):
    f32 = lambda a: np.ascontiguousarray(np.asarray(a), dtype=np.float32)
    x = f32(L0_attn_x)
    WqT = f32(L0_attn_Wq).T
    WkT = f32(L0_attn_Wk).T
    WvT = f32(L0_attn_Wv).T
    WoT = np.ascontiguousarray(f32(L0_attn_Wo).T)
    wcat = np.concatenate([f32(L0_ffn_W_res), f32(L0_ffn_W_pre), f32(L0_ffn_W_post)],
                          axis=0)
    wcatT = np.ascontiguousarray(wcat.T)
    basetile = np.ascontiguousarray(np.tile(np.concatenate(
        [f32(L0_ffn_H_res_base).ravel(), f32(L0_ffn_H_pre_base).ravel(),
         f32(L0_ffn_H_post_base).ravel()])[None, :], (128, 1)), dtype=np.float32)
    gatet = np.ascontiguousarray(np.tile(f32(L0_ffn_gate).reshape(1, 1), (128, 1)),
                                 dtype=np.float32)
    wbr = f32(L0_ffn_W_branch)
    eye = np.ascontiguousarray(np.tile(np.eye(4, dtype=np.float32).reshape(1, 16),
                                       (128, 1)), dtype=np.float32)
    slopes_all = _alibi_slopes(H)
    wqn = np.ascontiguousarray(f32(L0_attn_q_norm_w).reshape(HD, 1))
    wkn = np.ascontiguousarray(f32(L0_attn_k_norm_w).reshape(HD, 1))

    xTs = [np.ascontiguousarray(x[b].T) for b in range(B_)]
    in_maps = []
    for c in range(8):
        b, j = divmod(c, 4)
        ridx = (512 * (np.arange(TQ) // 128) + 128 * j
                + np.arange(TQ) % 128)
        wq_sl = WqT[:, 1024 * j:1024 * j + 1024]
        wk_sl = WkT[:, 256 * j:256 * j + 256]
        wv_sl = WvT[:, 256 * j:256 * j + 256]
        in_maps.append({
            "xT": xTs[b],
            "xrows": np.ascontiguousarray(x[b, ridx, :]),
            "wqkvT": np.ascontiguousarray(np.concatenate([wq_sl, wk_sl, wv_sl], axis=1)),
            "wqn": wqn, "wkn": wkn,
            "slopes": np.ascontiguousarray(
                np.tile(slopes_all[8 * j:8 * j + 8][None, :], (128, 1))),
            "woT": WoT,
            "wcatT": wcatT,
            "basetile": basetile,
            "gatet": gatet,
            "wbranch": wbr,
            "eye16": eye,
            "rowsel": np.ascontiguousarray(
                (512 * np.arange(4, dtype=np.uint32)[None, :, None]
                 + 128 * j + np.arange(128, dtype=np.uint32)[:, None, None]
                 + 2048 * (np.arange(32, dtype=np.uint32)[None, None, :] % 8))),
        })
    return in_maps


_CACHED = {}


def kernel(**inputs):
    in_maps = _prep_inputs(**inputs)
    if "full" not in _CACHED:
        _CACHED["full"] = build()
    nc = _CACHED["full"]
    res = run_bass_kernel_spmd(nc, in_maps, core_ids=list(range(8)))
    outp = np.empty((B_, S, DM), dtype=np.float32)
    for c in range(8):
        b, j = divmod(c, 4)
        ridx = (512 * (np.arange(TQ) // 128) + 128 * j
                + np.arange(TQ) % 128)
        outp[b, ridx, :] = res.results[c]["out"]
    return outp


# version 16
# speedup vs baseline: 1.6669x; 1.0213x over previous
import math
import os
from contextlib import ExitStack
_NTCB = int(os.environ.get("K_NTCB", "16"))
_P1MODE = os.environ.get("K_P1MODE", "full")

import numpy as np
import concourse.bass as bass
import concourse.mybir as mybir
from concourse.bass import IndirectOffsetOnAxis
from concourse.bass_utils import run_bass_kernel_spmd

F32 = mybir.dt.float32
BF16 = mybir.dt.bfloat16
U32 = mybir.dt.uint32
AF = mybir.ActivationFunctionType
ALU = mybir.AluOpType
AX = mybir.AxisListType
ds = bass.ds

B_, S, DM = 2, 4096, 4096
H, HKV, HD = 32, 8, 128
HPC, KVPC = 8, 2
TQ = 1024
EPS_QK = 1e-6
EPS_LN = 1e-5
RMS_EPS = 1.1920929e-07
NEG = -1e30
NT = S // 128
NSUP = S // 512


def _alibi_slopes(n_heads):
    start = 2 ** (-(2 ** -(math.log2(n_heads) - 3)))
    return np.array([start * start ** i for i in range(n_heads)], dtype=np.float32)


class Sem:
    def __init__(self, ctx, nc, name):
        self.h = ctx.enter_context(nc.semaphore(name))
        self.val = 0

    def inc(self, instr, n=1):
        instr.then_inc(self.h, n)
        self.val += n
        return instr


class PingSem:

    def __init__(self, ctx, nc, name, n):
        self.s = [ctx.enter_context(nc.semaphore(f"{name}{i}")) for i in range(n)]
        self.n = n

    def inc(self, instr, k):
        instr.then_inc(self.s[k % self.n], 16)
        return instr

    def wait(self, eng, k):
        eng.wait_ge(self.s[k % self.n], 16 * (k // self.n + 1))


def build(phases=("p1", "p2", "p3", "p4"), debug_outs=()):
    nc = bass.Bass("TRN2", target_bir_lowering=False)

    xT = nc.dram_tensor("xT", [DM, S], F32, kind="ExternalInput").ap()
    xrows = nc.dram_tensor("xrows", [TQ, DM], F32, kind="ExternalInput").ap()
    wqkvT = nc.dram_tensor("wqkvT", [DM, 1536], F32, kind="ExternalInput").ap()
    wqn = nc.dram_tensor("wqn", [HD, 1], F32, kind="ExternalInput").ap()
    wkn = nc.dram_tensor("wkn", [HD, 1], F32, kind="ExternalInput").ap()
    slopes = nc.dram_tensor("slopes", [128, HPC], F32, kind="ExternalInput").ap()
    woT = nc.dram_tensor("woT", [DM, DM], F32, kind="ExternalInput").ap()
    wcatT = nc.dram_tensor("wcatT", [DM, 24], F32, kind="ExternalInput").ap()
    basetile = nc.dram_tensor("basetile", [128, 24], F32, kind="ExternalInput").ap()
    gatet = nc.dram_tensor("gatet", [128, 1], F32, kind="ExternalInput").ap()
    wbranch = nc.dram_tensor("wbranch", [1024, 1024], F32, kind="ExternalInput").ap()
    eye16 = nc.dram_tensor("eye16", [128, 16], F32, kind="ExternalInput").ap()
    rowsel = nc.dram_tensor("rowsel", [128, 4, 32], U32, kind="ExternalInput").ap()

    out = nc.dram_tensor("out", [TQ, DM], F32, kind="ExternalOutput").ap()
    dbg = {}
    for name, shape in (("qfm", [HPC * HD, S]), ("kfm", [KVPC * HD, S]),
                        ("v", [S, KVPC * 129]), ("attn", [S, HPC * HD]),
                        ("s1", [TQ, DM]), ("h1", [TQ, DM]), ("zb", [TQ, 24]),
                        ("hres", [TQ, 16]), ("consts", [128, 16])):
        if name in debug_outs:
            dbg[name] = nc.dram_tensor("dbg_" + name, shape, F32, kind="ExternalOutput").ap()

    wqkv_bf = nc.dram_tensor("wqkv_bf", [DM, 1536], BF16, kind="Internal").ap()
    cin = nc.dram_tensor("cin", [S, HPC * HD], BF16, kind="Internal").ap()
    cout = nc.dram_tensor("cout", [NSUP, 4, 512, HPC * HD], BF16, kind="Internal").ap()

    ctx = ExitStack()
    with ctx:
        sbt = lambda name, shape, dt=F32: ctx.enter_context(nc.sbuf_tensor(name, shape, dt))
        identB = sbt("identB", [128, 128], BF16)
        cq = sbt("cq", [128, 1])
        slp = sbt("slp", [128, HPC])
        strips = sbt("strips", [128, HPC, NT])
        maskT = sbt("maskT", [128, 4, 512], BF16)
        wc_sb = sbt("wc_sb", [128, 32, 24], BF16)
        base_s = sbt("base_s", [128, 24])
        base10 = sbt("base10", [128, 16])
        eye_s = sbt("eye_s", [128, 16])
        gsig = sbt("gsig", [128, 4])
        onesb = sbt("onesb", [1, 128], BF16)
        epss = sbt("epss", [128, 4])
        rsel = ctx.enter_context(nc.sbuf_tensor("rsel", [128, 4, 32], U32))
        smalls = sbt("smalls", [128, 4])

        s_init = Sem(ctx, nc, "s_init")
        s_cst = Sem(ctx, nc, "s_cst")
        s_dbg = Sem(ctx, nc, "s_dbg")
        n_dbg = [0]

        with nc.Block() as blk:
            s_gp = Sem(ctx, nc, "s_gp")

            def g_body(g):
                nms = 0
                g.memset(identB[:, :], 1.0).then_inc(s_gp.h, 1); nms += 1
                for i in range(4):
                    g.memset(maskT[:, i, :], 0.0).then_inc(s_gp.h, 1); nms += 1
                g.memset(epss[:, 0:1], EPS_QK)
                g.memset(epss[:, 1:2], EPS_LN)
                g.memset(epss[:, 2:3], RMS_EPS)
                g.memset(epss[:, 3:4], -8.0)
                g.memset(onesb[:, :], 1.0)
                s_init.inc(g.iota(strips[:, 0, :], [[128, NT]], base=0,
                                  channel_multiplier=1,
                                  allow_small_or_imprecise_dtypes=True))
                g.wait_ge(s_gp.h, nms)
                s_gp.val += nms
                g.affine_select(identB[:, :], identB[:, :], [[1, 128]], ALU.is_equal,
                                0.0, base=0, channel_multiplier=-1)
                for i in range(4):
                    g.affine_select(maskT[:, i, :], maskT[:, i, :], [[1, 512]],
                                    ALU.is_ge, NEG, base=-128 * i, channel_multiplier=-1)
            blk.gpsimd(g_body)

            s_ld0 = Sem(ctx, nc, "s_ld0")
            s_ldg = Sem(ctx, nc, "s_ldg")

            def s_body(s):
                s_ld0.inc(s.dma_start(slp[:, :], slopes), 16)
                s_ld0.inc(s.dma_start(smalls[:, 0:1], wqn), 16)
                s_ld0.inc(s.dma_start(smalls[:, 1:2], wkn), 16)
                s_ld0.inc(s.dma_start(base_s[:, :], basetile), 16)
                s_ld0.inc(s.dma_start(eye_s[:, :], eye16), 16)
                s_ld0.inc(s.dma_start(gsig[:, 0:1], gatet), 16)
                s_ld0.inc(s.dma_start(rsel[:, :, :], rowsel), 16)
            blk.sync(s_body)

            def gc_body(g):
                s_ldg.inc(g.dma_start(wc_sb[:, :, :],
                                      wcatT.rearrange("(c p) o -> p c o", p=128)), 16)
            blk.gpsimd(gc_body)

            def sc0_body(sc):
                sc.wait_ge(s_ld0.h, s_ld0.val)
                s_init.inc(sc.activation(gsig[:, 2:3], gsig[:, 0:1], AF.Exp, scale=-1.0))
            blk.scalar(sc0_body)

            def v0_body(v):
                v.wait_ge(s_ld0.h, s_ld0.val)
                v.wait_ge(s_init.h, s_init.val)
                v.tensor_scalar_add(gsig[:, 2:3], gsig[:, 2:3], 1.0)
                v.drain()
                v.reciprocal(gsig[:, 0:1], gsig[:, 2:3])
                v.tensor_mul(cq[:, :], smalls[:, 0:1], smalls[:, 1:2])
                v.drain()
                v.tensor_scalar(gsig[:, 1:2], gsig[:, 0:1], -1.0, 2.0, ALU.mult, ALU.add)
                v.tensor_scalar_mul(cq[:, :], cq[:, :], 1.0 / math.sqrt(HD))
                v.drain()
                for h in range(HPC - 1, -1, -1):
                    v.tensor_scalar_mul(strips[:, h, :], strips[:, 0, :], slp[:, h:h + 1])
                s_init.inc(v.tensor_scalar_mul(base10[:, :], base_s[:, 0:16], 1.0 / 0.1))
            blk.vector(v0_body)

        with ExitStack() as p4s:
            e4 = p4s.enter_context
            blk = e4(nc.Block())
            cb0 = e4(nc.sbuf_tensor("castb0", [128, 2048], BF16))
            cb1 = e4(nc.sbuf_tensor("castb1", [128, 2048], BF16))
            cbs = [cb0, cb1]
            jobs = [(wqkvT, wqkv_bf, DM, 1536)]
            s_ca = PingSem(ctx, nc, "s_ca", 2)
            s_csp = PingSem(ctx, nc, "s_csp", 2)
            ncast = [0]

            def gcast(g):
                k = 0
                for src, dst, rows, cols in jobs:
                    src3 = src.rearrange("(n p) c -> n p c", p=128)
                    for n in range(rows // 128):
                        for c0 in range(0, cols, 2048):
                            cw = min(2048, cols - c0)
                            if k >= 2:
                                s_csp.wait(g, k - 2)
                            s_ca.inc(g.dma_start(cbs[k % 2][:, 0:cw],
                                                 src3[n, :, c0:c0 + cw]), k)
                            k += 1
                ncast[0] = k
            blk.gpsimd(gcast)

            def scast(s):
                k = 0
                for src, dst, rows, cols in jobs:
                    dst3 = dst.rearrange("(n p) c -> n p c", p=128)
                    for n in range(rows // 128):
                        for c0 in range(0, cols, 2048):
                            cw = min(2048, cols - c0)
                            s_ca.wait(s, k)
                            s_csp.inc(s.dma_start(dst3[n, :, c0:c0 + cw],
                                                  cbs[k % 2][:, 0:cw]), k)
                            k += 1
            blk.sync(scast)

        with nc.sbuf_tensor("qfm", [128, HPC, S], BF16) as qfm, \
                nc.sbuf_tensor("kfm", [128, KVPC, S], BF16) as kfm, \
                nc.sbuf_tensor("vp", [128, KVPC, NT, 129], BF16) as vp:

            if "p1" in phases:
                with nc.Block() as blk, \
                        nc.sbuf_tensor("xts", [128, 2, 32, 256], BF16) as xts, \
                        nc.sbuf_tensor("wbb", [128, 2, 32, 256], BF16) as wbb, \
                        nc.sbuf_tensor("qsc", [128, 256], BF16) as qsc, \
                        nc.sbuf_tensor("qtm", [128, 16, 256], BF16) as qtm, \
                        nc.sbuf_tensor("ssb", [128, 8], F32) as ssb, \
                        nc.psum_tensor("pa0", [128, 256], F32) as pa0, \
                        nc.psum_tensor("pa1", [128, 256], F32) as pa1, \
                        nc.psum_tensor("pa2", [128, 256], F32) as pa2, \
                        nc.psum_tensor("pa3", [128, 256], F32) as pa3, \
                        nc.psum_tensor("ptp0", [128, 128], BF16) as ptp0, \
                        nc.psum_tensor("ptp1", [128, 128], BF16) as ptp1:
                    paccs = [pa0, pa1, pa2, pa3]
                    ptps = [ptp0, ptp1]
                    s_x = PingSem(ctx, nc, "s_x", 2)
                    s_w = PingSem(ctx, nc, "s_w", 2)
                    s_mm = Sem(ctx, nc, "s_mm")
                    s_sq = Sem(ctx, nc, "s_sq")
                    s_nrm = Sem(ctx, nc, "s_nrm")
                    s_tp = Sem(ctx, nc, "s_tp")
                    s_ev = Sem(ctx, nc, "s_ev")

                    xT3 = xT.rearrange("(c p) t -> p c t", p=128)
                    wq3 = wqkv_bf.rearrange("(c p) o -> p c o", p=128)

                    def tile_info(n):
                        tc, r = divmod(n, 12)
                        oh, t128 = divmod(r, 2)
                        return tc, oh, t128

                    ntp_before = [0] * 194
                    for n in range(192):
                        _, oh, _ = tile_info(n)
                        ntp_before[n + 1] = ntp_before[n] + (0 if oh == 5 else 2)

                    def gx(g):
                        g.wait_ge(s_csp.s[0], 16 * 16)
                        g.wait_ge(s_csp.s[1], 16 * 16)
                        g.memset(vp[:, 0, :, 128:129], 1.0)
                        g.memset(vp[:, 1, :, 128:129], 1.0)
                        for tc in range(_NTCB):
                            if tc >= 2 and _P1MODE not in ("mm",):
                                g.wait_ge(s_nrm.h, (tc - 1) * 12)
                            s_x.inc(g.dma_start(xts[:, tc % 2, :, :],
                                                xT3[:, :, 256 * tc:256 * tc + 256]), tc)
                    blk.gpsimd(gx)

                    def sw(s):
                        s.wait_ge(s_csp.s[0], 16 * 16)
                        s.wait_ge(s_csp.s[1], 16 * 16)
                        k = 0
                        for tc in range(_NTCB):
                            for oh in range(6):
                                if k >= 2:
                                    s.wait_ge(s_mm.h, max(0, 2 * k - 2))
                                c0 = 256 * (tc * 0 + oh)
                                s_w.inc(s.dma_start(wbb[:, k % 2, :, :],
                                                    wq3[:, :, c0:c0 + 256]), k)
                                k += 1
                    blk.sync(sw)

                    def emit_transposes(t, tcb):
                        for r in range(12):
                            n = tcb * 12 + r
                            _, oh, _ = tile_info(n)
                            if oh == 5:
                                continue
                            for hh in range(2):
                                m = ntp_before[n] + hh
                                if m >= 2:
                                    t.wait_ge(s_ev.h, m - 1)
                                t.wait_ge(s_nrm.h, n + 1)
                                s_tp.inc(t.matmul(ptps[m % 2][:, :],
                                         qtm[:, n % 16, 128 * hh:128 * hh + 128],
                                         identB[:, :], is_transpose=True,
                                         start=True, stop=True))

                    def emit_evacs(sc, tcb):
                        for r in range(12):
                            n = tcb * 12 + r
                            tc2, oh, t128 = tile_info(n)
                            if oh == 5:
                                continue
                            gtok = tc2 * 2 + t128
                            for hh in range(2):
                                m = ntp_before[n] + hh
                                sc.wait_ge(s_tp.h, m + 1)
                                if oh < 4:
                                    head = oh * 2 + hh
                                    ei = sc.activation(qfm[:, head, 128 * gtok:128 * gtok + 128],
                                                       ptps[m % 2][:, :], AF.Copy,
                                                       scale=cq[:, :])
                                else:
                                    ei = sc.activation(kfm[:, hh, 128 * gtok:128 * gtok + 128],
                                                       ptps[m % 2][:, :], AF.Copy)
                                s_ev.inc(ei)

                    for tcb in range(_NTCB):
                        def t_body(t, tcb=tcb):
                            if tcb > 0 and _P1MODE in ("full", "notp_dbg"):
                                emit_transposes(t, tcb - 1)
                            s_x.wait(t, tcb)
                            for r in range(12):
                                n = tcb * 12 + r
                                oh, t128 = divmod(r, 2)
                                s_w.wait(t, tcb * 6 + oh)
                                if n >= 4:
                                    t.wait_ge(s_nrm.h, n - 3)
                                acc = paccs[n % 4]
                                for d in range(32):
                                    mi = t.matmul(acc[:, :],
                                                  xts[:, tcb % 2, d, 128 * t128:128 * t128 + 128],
                                                  wbb[:, (tcb * 6 + oh) % 2, d, :],
                                                  start=(d == 0), stop=(d == 31))
                                s_mm.inc(mi)
                        blk.tensor(t_body)

                        def sc_body(sc, tcb=tcb):
                            if tcb > 0 and _P1MODE == "full":
                                emit_evacs(sc, tcb - 1)
                            if _P1MODE == "mm":
                                for r in range(12):
                                    sc.wait_ge(s_mm.h, tcb * 12 + r + 1)
                                    s_sq.inc(sc.nop())
                                return
                            for r in range(12):
                                n = tcb * 12 + r
                                _, oh, _ = tile_info(n)
                                acc = paccs[n % 4]
                                sc.wait_ge(s_mm.h, n + 1)
                                if n >= 4:
                                    sc.wait_ge(s_nrm.h, n - 3)
                                if oh != 5:
                                    sl = 2 * (n % 4)
                                    if _P1MODE != "mmln":
                                        for hh in range(2):
                                            sc.activation(qsc[:, 128 * hh:128 * hh + 128],
                                                          acc[:, 128 * hh:128 * hh + 128],
                                                          AF.Square,
                                                          accum_out=ssb[:, sl + hh:sl + hh + 1])
                                    if _P1MODE != "mmsqonly":
                                        sc.drain()
                                        sc.activation(ssb[:, sl:sl + 2], ssb[:, sl:sl + 2],
                                                      AF.Ln, scale=1.0 / HD, bias=epss[:, 0:1])
                                        sc.drain()
                                        li = sc.activation(ssb[:, sl:sl + 2], ssb[:, sl:sl + 2],
                                                           AF.Exp, scale=-0.5)
                                        s_sq.inc(li)
                                    else:
                                        sc.drain()
                                        s_sq.inc(sc.nop())
                                else:
                                    sc.drain()
                                    s_sq.inc(sc.nop())
                        blk.scalar(sc_body)

                        def v_body(v, tcb=tcb):
                            if _P1MODE in ("mm", "mmsq"):
                                for r in range(12):
                                    v.wait_ge(s_sq.h, tcb * 12 + r + 1)
                                    s_nrm.inc(v.nop())
                                return
                            for r in range(12):
                                n = tcb * 12 + r
                                tc2, oh, t128 = tile_info(n)
                                acc = paccs[n % 4]
                                v.wait_ge(s_sq.h, n + 1)
                                gtok = tc2 * 2 + t128
                                if oh != 5:
                                    if n >= 16 and _P1MODE == "full":
                                        v.wait_ge(s_tp.h, ntp_before[n - 15])
                                    sl = 2 * (n % 4)
                                    for hh in range(2):
                                        vi = v.tensor_scalar_mul(qtm[:, n % 16, 128 * hh:128 * hh + 128],
                                                                 acc[:, 128 * hh:128 * hh + 128],
                                                                 ssb[:, sl + hh:sl + hh + 1])
                                else:
                                    for hh in range(2):
                                        vi = v.tensor_copy(vp[:, hh, gtok, 0:128],
                                                           acc[:, 128 * hh:128 * hh + 128])
                                s_nrm.inc(vi)
                        blk.vector(v_body)

                    if _P1MODE == "full":
                        blk.tensor(lambda t: emit_transposes(t, _NTCB - 1))
                        blk.scalar(lambda sc: emit_evacs(sc, _NTCB - 1))

                if dbg.keys() & {"qfm", "kfm", "v", "consts"}:
                    with nc.Block() as blk:
                        def dqkv(g):
                            if "qfm" in dbg:
                                q2 = dbg["qfm"].rearrange("(h p) t -> p h t", p=128)
                                s_dbg.inc(g.dma_start(q2, qfm[:, :, :]), 16)
                                n_dbg[0] += 1
                            if "kfm" in dbg:
                                k2 = dbg["kfm"].rearrange("(h p) t -> p h t", p=128)
                                s_dbg.inc(g.dma_start(k2, kfm[:, :, :]), 16)
                                n_dbg[0] += 1
                            if "consts" in dbg:
                                s_dbg.inc(g.dma_start(dbg["consts"][:, 0:4], smalls[:, :]), 16)
                                s_dbg.inc(g.dma_start(dbg["consts"][:, 4:5], cq[:, :], allow_slow_non_contiguous=True), 16)
                                s_dbg.inc(g.dma_start(dbg["consts"][:, 5:9], gsig[:, :]), 16)
                                s_dbg.inc(g.dma_start(dbg["consts"][:, 9:13], epss[:, :]), 16)
                            if "v" in dbg:
                                v2 = dbg["v"].rearrange("(n p) (g c) -> p n g c", p=128, g=KVPC)
                                for gg in range(KVPC):
                                    s_dbg.inc(g.dma_start(
                                        v2[:, :, gg, :], vp[:, gg, :, :]), 16)
                                n_dbg[0] += 2
                            g.wait_ge(s_dbg.h, s_dbg.val)
                        blk.gpsimd(dqkv)

            if "p2" in phases:
                with ExitStack() as aosx:
                    aouts = [aosx.enter_context(nc.sbuf_tensor(
                        f"aout{s_}", [128, 4, HPC, 128], BF16)) for s_ in range(NSUP)]
                    with ExitStack() as p2s:
                        e2 = p2s.enter_context
                        blk = e2(nc.Block())
                        ptb = e2(nc.sbuf_tensor("ptb", [128, 2, 512], BF16))
                        negmT = e2(nc.sbuf_tensor("negmT", [1, HPC, 512], BF16))
                        qsq = e2(nc.sbuf_tensor("qsq", [128, 8, 512], BF16))
                        t1row = e2(nc.sbuf_tensor("t1row", [1, 2, 512], F32))
                        t2row = e2(nc.sbuf_tensor("t2row", [1, 512], F32))
                        jrow = e2(nc.sbuf_tensor("jrow", [1, 512], F32))
                        kmsq = e2(nc.sbuf_tensor("kmsq", [1, 2], F32))
                        kqb = e2(nc.sbuf_tensor("kqb", [1, 16], F32))
                        onesc = e2(nc.sbuf_tensor("onesc", [128, 1], BF16))
                        rcp = e2(nc.sbuf_tensor("rcp", [128, 4], F32))
                        pst0 = e2(nc.psum_tensor("pst0", [128, 512], F32))
                        pst1 = e2(nc.psum_tensor("pst1", [128, 512], F32))
                        pss0 = e2(nc.psum_tensor("pss0", [128, 512], F32))
                        pss1 = e2(nc.psum_tensor("pss1", [128, 512], F32))
                        ppv0 = e2(nc.psum_tensor("ppv0", [128, 129], F32))
                        ppv1 = e2(nc.psum_tensor("ppv1", [128, 129], F32))
                        ppv2 = e2(nc.psum_tensor("ppv2", [128, 129], F32))
                        ppv3 = e2(nc.psum_tensor("ppv3", [128, 129], F32))
                        psts = [pst0, pst1]
                        psss = [pss0, pss1]
                        ppvs = [ppv0, ppv1, ppv2, ppv3]
                        s_i2 = Sem(ctx, nc, "s_i2")
                        s_qs = Sem(ctx, nc, "s_qs")
                        s_qm = Sem(ctx, nc, "s_qm")
                        s_kvr = Sem(ctx, nc, "s_kvr")
                        s_kv = Sem(ctx, nc, "s_kv")
                        s_sqr = Sem(ctx, nc, "s_sqr")
                        s_ne = Sem(ctx, nc, "s_ne")
                        s_st = Sem(ctx, nc, "s_st")
                        s_pt = Sem(ctx, nc, "s_pt")
                        s_pv = Sem(ctx, nc, "s_pv")
                        s_fin = Sem(ctx, nc, "s_fin")

                        nb = [0]
                        nfin = [0]
                        nb_end = {}
                        _acc = 0
                        for _s in range(NSUP):
                            for _h in range(HPC):
                                _acc += 4 * (_s + 1)
                                nb_end[(_s, _h)] = _acc

                        def g_i2(g):
                            g.memset(onesc[:, :], 1.0).then_inc(s_i2.h, 1)
                            s_i2.val += 1
                            s_i2.inc(g.iota(jrow[0:1, :], [[1, 512]], base=0,
                                            channel_multiplier=0,
                                            allow_small_or_imprecise_dtypes=True))
                        blk.gpsimd(g_i2)

                        def sc_km(sc):
                            for gk in range(KVPC):
                                for c in range(8):
                                    idx = gk * 8 + c
                                    if idx >= 8:
                                        sc.wait_ge(s_qm.h, idx - 7)
                                    s_qs.inc(sc.activation(
                                        qsq[:, idx % 8, :],
                                        kfm[:, gk, 512 * c:512 * c + 512],
                                        AF.Square))
                        blk.scalar(sc_km)

                        def t_km(t):
                            t.wait_ge(s_i2.h, 2)
                            for gk in range(KVPC):
                                for c in range(8):
                                    idx = gk * 8 + c
                                    t.wait_ge(s_qs.h, idx + 1)
                                    if idx >= 2:
                                        t.wait_ge(s_kvr.h, idx - 1)
                                    s_qm.inc(t.matmul(psts[idx % 2][0:1, :],
                                                      onesc[:, 0:1],
                                                      qsq[:, idx % 8, :],
                                                      start=True, stop=True))
                        blk.tensor(t_km)

                        def v_km(v):
                            for gk in range(KVPC):
                                for c in range(8):
                                    idx = gk * 8 + c
                                    v.wait_ge(s_qm.h, idx + 1)
                                    s_kvr.inc(v.tensor_reduce(
                                        kqb[0:1, idx:idx + 1],
                                        psts[idx % 2][0:1, :], AX.X, ALU.max))
                                v.drain()
                                s_kv.inc(v.tensor_reduce(
                                    kmsq[0:1, gk:gk + 1],
                                    kqb[0:1, 8 * gk:8 * gk + 8], AX.X, ALU.max))
                        blk.vector(v_km)

                        def emit_chain(s2, h):
                            sqidx = 8 * s2 + h
                            gk = h // 4

                            def tQ(t, s2=s2, h=h, sqidx=sqidx):
                                t.wait_ge(s_qs.h, 16 + sqidx + 1)
                                if sqidx >= 2:
                                    t.wait_ge(s_sqr.h, sqidx - 1)
                                s_qm.inc(t.matmul(psss[sqidx % 2][0:1, :],
                                                  onesc[:, 0:1], qsq[:, h, :],
                                                  start=True, stop=True))
                            blk.tensor(tQ)

                            def sQ(sc, s2=s2, h=h, sqidx=sqidx, gk=gk):
                                if s2 >= 1:
                                    sc.wait_ge(s_qm.h, 16 + 8 * (s2 - 1) + h + 1)
                                else:
                                    sc.wait_ge(s_qm.h, 8 + h + 1)
                                s_qs.inc(sc.activation(
                                    qsq[:, h, :],
                                    qfm[:, h, 512 * s2:512 * s2 + 512], AF.Square))
                                sc.wait_ge(s_qm.h, 16 + sqidx + 1)
                                sc.wait_ge(s_kv.h, gk + 1)
                                if sqidx >= 2:
                                    sc.wait_ge(s_ne.h, sqidx - 1)
                                si = sc.activation(t1row[0:1, sqidx % 2, :],
                                                   psss[sqidx % 2][0:1, :],
                                                   AF.Sqrt,
                                                   scale=kmsq[0:1, gk:gk + 1])
                                s_sqr.inc(si)
                            blk.scalar(sQ)

                            def vQ(v, s2=s2, h=h, sqidx=sqidx):
                                v.wait_ge(s_sqr.h, sqidx + 1)
                                v.wait_ge(s_i2.h, 2)
                                if s2 >= 1:
                                    v.wait_ge(s_st.h, nb_end[(s2 - 1, h)])
                                v.tensor_scalar(t2row[0:1, :], jrow[0:1, :],
                                                512.0 * s2, slp[0:1, h:h + 1],
                                                ALU.add, ALU.mult)
                                v.drain()
                                v.tensor_add(t2row[0:1, :], t2row[0:1, :],
                                             t1row[0:1, sqidx % 2, :])
                                v.drain()
                                vi = v.tensor_scalar(negmT[0:1, h, :],
                                                     t2row[0:1, :], -1.0, -8.0,
                                                     ALU.mult, ALU.add)
                                s_ne.inc(vi)
                            blk.vector(vQ)

                        for h in range(HPC):
                            emit_chain(0, h)

                        for s in range(NSUP):
                            nch = 4 * (s + 1)
                            for h in range(HPC):
                                gk = h // 4
                                nb0 = nb[0]

                                def tB(t, s=s, h=h, gk=gk, nb0=nb0, nch=nch,
                                       fin0=nfin[0]):
                                    t.wait_ge(s_ne.h, s * 8 + h + 1)
                                    if fin0 >= 1:
                                        t.wait_ge(s_fin.h, fin0)
                                    for c in range(nch):
                                        n = nb0 + c
                                        bank = psts[n % 2]
                                        if n >= 2:
                                            t.wait_ge(s_pt.h, n - 1)
                                        t.matmul(bank[:, :], onesb[0:1, :],
                                                 negmT[0:1, h, :],
                                                 start=True, stop=False)
                                        diag = (c >= nch - 4)
                                        mi = t.matmul(bank[:, :],
                                                      kfm[:, gk, 128 * c:128 * c + 128],
                                                      qfm[:, h, 512 * s:512 * s + 512],
                                                      start=False, stop=not diag)
                                        if diag:
                                            mi = t.matmul(bank[:, :], identB[:, :],
                                                          maskT[:, c - (nch - 4), :],
                                                          start=False, stop=True)
                                        s_st.inc(mi)
                                        if c >= 1:
                                            t.wait_ge(s_pt.h, nb0 + c)
                                            for i in range(4):
                                                pvi = t.matmul(
                                                    ppvs[i][:, :],
                                                    ptb[:, (nb0 + c - 1) % 2,
                                                        128 * i:128 * i + 128],
                                                    vp[:, gk, c - 1, :],
                                                    start=(c == 1), stop=False)
                                            s_pv.inc(pvi)
                                    t.wait_ge(s_pt.h, nb0 + nch)
                                    for i in range(4):
                                        pvi = t.matmul(ppvs[i][:, :],
                                                       ptb[:, (nb0 + nch - 1) % 2,
                                                           128 * i:128 * i + 128],
                                                       vp[:, gk, nch - 1, :],
                                                       start=(nch == 1), stop=True)
                                    s_pv.inc(pvi)
                                blk.tensor(tB)

                                def sB(sc, s=s, h=h, nb0=nb0, nch=nch):
                                    for c in range(nch):
                                        n = nb0 + c
                                        bank = psts[n % 2]
                                        sc.wait_ge(s_st.h, n + 1)
                                        if n >= 2:
                                            sc.wait_ge(s_pv.h, n - 1)
                                        s_pt.inc(sc.activation(
                                            ptb[:, n % 2, :], bank[:, :],
                                            AF.Exp, bias=strips[:, h, c:c + 1]))
                                blk.scalar(sB)

                                def vB(v, s=s, h=h, nb0=nb0, nch=nch):
                                    v.wait_ge(s_pv.h, nb0 + nch)
                                    for i in range(4):
                                        v.reciprocal(rcp[:, i:i + 1],
                                                     ppvs[i][:, 128:129])
                                    v.drain()
                                    for i in range(4):
                                        fi = v.tensor_scalar_mul(aouts[s][:, i, h, :],
                                                                 ppvs[i][:, 0:128],
                                                                 rcp[:, i:i + 1])
                                    s_fin.inc(fi)
                                blk.vector(vB)
                                nfin[0] += 1
                                nb[0] += nch
                                if s + 1 < NSUP:
                                    emit_chain(s + 1, h)

                        s_ci = Sem(ctx, nc, "s_ci")
                        s_cc = Sem(ctx, nc, "s_cc")

                        def gex(g):
                            cin4 = cin.rearrange("(n p) (h d) -> p n h d", p=128, h=HPC)
                            for s in range(NSUP):
                                g.wait_ge(s_fin.h, 8 * (s + 1))
                                s_ci.inc(g.dma_start(cin4[:, 4 * s:4 * s + 4, :, :],
                                                     aouts[s][:, :, :, :]), 16)
                                g.wait_ge(s_ci.h, 16 * (s + 1))
                                if s >= 1:
                                    g.wait_ge(s_cc.h, s)
                                s_cc.inc(g.collective_compute(
                                    "AllGather", ALU.bypass,
                                    replica_groups=[[0, 1, 2, 3], [4, 5, 6, 7]],
                                    ins=[cin[512 * s:512 * s + 512, :]],
                                    outs=[cout[s]]), 1)
                        blk.gpsimd(gex)

                    if "attn" in dbg:
                        with nc.Block() as blk:
                            def datt(g):
                                a2 = dbg["attn"].rearrange("(n p) f -> p n f", p=128)
                                for s_ in range(NSUP):
                                    s_dbg.inc(g.dma_start(
                                        a2[:, 4 * s_:4 * s_ + 4, :],
                                        aouts[s_].ap().rearrange("p n h d -> p n (h d)")), 16)
                                g.wait_ge(s_dbg.h, s_dbg.val)
                            blk.gpsimd(datt)

        if "p3" in phases:
            s_bf = ctx.enter_context(nc.sbuf_tensor("s_bf", [128, 8, DM], BF16))
            lnst = ctx.enter_context(nc.sbuf_tensor("lnst", [128, 8, 4], F32))

            with ExitStack() as p3s:
                e3 = p3s.enter_context
                afm = e3(nc.sbuf_tensor("afm", [128, 32, 1024], BF16))
                s_at = Sem(ctx, nc, "s_at")
                s_tq = Sem(ctx, nc, "s_tq")
                s_ae = Sem(ctx, nc, "s_ae")
                s_wl = PingSem(ctx, nc, "s_wl", 2)
                s_xr = PingSem(ctx, nc, "s_xr", 2)
                s_ym = Sem(ctx, nc, "s_ym")
                s_ye = Sem(ctx, nc, "s_ye")

                coutf = cout.rearrange("s r t f -> (s r t) f")

                with ExitStack() as p3g:
                    eg = p3g.enter_context
                    blkg = eg(nc.Block())
                    attm = eg(nc.sbuf_tensor("attm", [128, 8, 1024], BF16))
                    ptq0 = eg(nc.psum_tensor("ptq0", [128, 512], BF16))
                    ptq1 = eg(nc.psum_tensor("ptq1", [128, 512], BF16))
                    ptqs = [ptq0, ptq1]

                    def g_loads(g):
                        g.wait_ge(s_ld0.h, s_ld0.val)
                        g.wait_ge(s_cc.h, NSUP)
                        for tt in range(8):
                            if tt >= 2:
                                g.wait_ge(s_tq.h, 8 * (tt - 1))
                            for r in range(4):
                                s_at.inc(g.indirect_dma_start(
                                    attm[:, 4 * (tt % 2) + r, :], None, coutf,
                                    IndirectOffsetOnAxis(rsel[:, r, tt:tt + 1], 0)), 16)
                    blkg.gpsimd(g_loads)

                    def t_tr(t):
                        k = 0
                        for tt in range(8):
                            for r in range(4):
                                t.wait_ge(s_at.h, 16 * (4 * tt + r + 1))
                                for u in range(2):
                                    m = 2 * k + u
                                    if m >= 2:
                                        t.wait_ge(s_ae.h, m - 1)
                                    for q in range(4):
                                        fc = 4 * u + q
                                        ti = t.matmul(ptqs[m % 2][:, 128 * q:128 * q + 128],
                                                      attm[:, 4 * (tt % 2) + r,
                                                           128 * fc:128 * fc + 128],
                                                      identB[:, :], is_transpose=True,
                                                      start=(q == 0), stop=(q == 3))
                                    s_tq.inc(ti)
                                k += 1
                    blkg.tensor(t_tr)

                    def sc_ae(sc):
                        k = 0
                        for tt in range(8):
                            for r in range(4):
                                for u in range(2):
                                    m = 2 * k + u
                                    sc.wait_ge(s_tq.h, m + 1)
                                    fc0 = 8 * r + 4 * u
                                    s_ae.inc(sc.activation(
                                        afm[:, fc0:fc0 + 4, 128 * tt:128 * tt + 128],
                                        ptqs[m % 2][:, :].rearrange("p (a b) -> p a b", b=128),
                                        AF.Copy))
                                k += 1
                    blkg.scalar(sc_ae)

                blk = e3(nc.Block())
                wsb = e3(nc.sbuf_tensor("wsb", [128, 2, 32, 512], BF16))
                xrb = e3(nc.sbuf_tensor("xrb", [128, 2, 512], F32))
                py0 = e3(nc.psum_tensor("py0", [128, 512], F32))
                py1 = e3(nc.psum_tensor("py1", [128, 512], F32))
                py2 = e3(nc.psum_tensor("py2", [128, 512], F32))
                py3 = e3(nc.psum_tensor("py3", [128, 512], F32))
                pys = [py0, py1, py2, py3]

                wo3 = woT.rearrange("(c p) m -> p c m", p=128)

                def s_wxload(g):
                    for mh in range(8):
                        if mh >= 2:
                            g.wait_ge(s_ym.h, (mh - 1) * 8)
                        s_wl.inc(g.dma_start(wsb[:, mh % 2, :, :],
                                             wo3[:, :, 512 * mh:512 * mh + 512]), mh)
                        for tt in range(8):
                            k = mh * 8 + tt
                            if k >= 2:
                                g.wait_ge(s_ye.h, k - 1)
                            s_xr.inc(g.dma_start(
                                xrb[:, k % 2, :],
                                xrows[128 * tt:128 * tt + 128,
                                      512 * mh:512 * mh + 512]), k)
                blk.gpsimd(s_wxload)

                def t_wo(t):
                    for mh in range(8):
                        s_wl.wait(t, mh)
                        for tt in range(8):
                            k = mh * 8 + tt
                            acc = pys[k % 4]
                            t.wait_ge(s_ae.h, 8 * (tt + 1))
                            if k >= 4:
                                t.wait_ge(s_ye.h, k - 3)
                            for fc in range(32):
                                yi = t.matmul(acc[:, :], afm[:, fc, 128 * tt:128 * tt + 128],
                                              wsb[:, mh % 2, fc, :],
                                              start=(fc == 0), stop=(fc == 31))
                            s_ym.inc(yi)
                blk.tensor(t_wo)

                def v_wo(v):
                    for mh in range(8):
                        for tt in range(8):
                            k = mh * 8 + tt
                            acc = pys[k % 4]
                            v.wait_ge(s_ym.h, k + 1)
                            s_xr.wait(v, k)
                            s_ye.inc(v.scalar_tensor_tensor(
                                s_bf[:, tt, 512 * mh:512 * mh + 512], acc[:, :],
                                1.0, xrb[:, k % 2, :], ALU.mult, ALU.add))
                blk.vector(v_wo)

            if "s1" in dbg:
                with nc.Block() as blk:
                    def ds1(g):
                        d2 = dbg["s1"].rearrange("(n p) f -> p n f", p=128)
                        s_dbg.inc(g.dma_start(d2, s_bf[:, :, :]), 16)
                        g.wait_ge(s_dbg.h, s_dbg.val)
                    blk.gpsimd(ds1)

        if "p4" in phases:
            with ExitStack() as p4b:
                eb4 = p4b.enter_context
                blk = eb4(nc.Block())
                junk = eb4(nc.sbuf_tensor("junk", [128, DM], BF16))
                stat4 = eb4(nc.sbuf_tensor("stat4", [128, 32], F32))
                h1f = eb4(nc.sbuf_tensor("h1f", [128, 32, 128], BF16))
                zbuf = eb4(nc.sbuf_tensor("zbuf", [128, 8, 24], F32))
                z0 = eb4(nc.sbuf_tensor("z0", [128, 8, 16], F32))
                zp = eb4(nc.sbuf_tensor("zp", [128, 8, 16], F32))
                tsk = eb4(nc.sbuf_tensor("tsk", [128, 128], F32))
                esk = eb4(nc.sbuf_tensor("esk", [128, 128], F32))
                msk = eb4(nc.sbuf_tensor("msk", [128, 32], F32))
                ulse = eb4(nc.sbuf_tensor("ulse", [128, 32], F32))
                vlse = eb4(nc.sbuf_tensor("vlse", [128, 32], F32))
                hres = eb4(nc.sbuf_tensor("hres", [128, 8, 16], F32))
                hpp = eb4(nc.sbuf_tensor("hpp", [128, 8, 8], F32))
                invt = eb4(nc.sbuf_tensor("invt", [128, 8, 2], F32))
                xp = eb4(nc.sbuf_tensor("xp", [128, 1024], BF16))
                xpf = eb4(nc.sbuf_tensor("xpf", [128, 1024], F32))
                xpfm = eb4(nc.sbuf_tensor("xpfm", [128, 8, 128], BF16))
                wbsb = eb4(nc.sbuf_tensor("wbsb", [128, 8, 1024], BF16))
                ybf = eb4(nc.sbuf_tensor("ybf", [128, 8, 1024], BF16))
                s2 = eb4(nc.sbuf_tensor("s2", [128, 4, 1024], F32))
                outb = eb4(nc.sbuf_tensor("outb", [128, 2, DM], F32))
                ph0 = eb4(nc.psum_tensor("ph0", [128, 512], BF16))
                ph1 = eb4(nc.psum_tensor("ph1", [128, 512], BF16))
                plg = eb4(nc.psum_tensor("plg", [128, 24], F32))
                pyb0 = eb4(nc.psum_tensor("pyb0", [128, 512], F32))
                pyb1 = eb4(nc.psum_tensor("pyb1", [128, 512], F32))
                phs = [ph0, ph1]
                pybs = [pyb0, pyb1]
                s_sq1 = Sem(ctx, nc, "s_sq1")
                s_vst = Sem(ctx, nc, "s_vst")
                s_iv1 = Sem(ctx, nc, "s_iv1")
                s_h1 = Sem(ctx, nc, "s_h1")
                s_rm2 = Sem(ctx, nc, "s_rm2")
                s_ht = Sem(ctx, nc, "s_ht")
                s_he = Sem(ctx, nc, "s_he")
                s_lg = Sem(ctx, nc, "s_lg")
                s_zb = Sem(ctx, nc, "s_zb")
                s_sk = Sem(ctx, nc, "s_sk")
                s_xp = Sem(ctx, nc, "s_xp")
                s_xt = Sem(ctx, nc, "s_xt")
                s_xe = Sem(ctx, nc, "s_xe")
                s_yb = Sem(ctx, nc, "s_yb")
                s_yv = Sem(ctx, nc, "s_yv")
                s_s2 = Sem(ctx, nc, "s_s2")
                s_l2 = Sem(ctx, nc, "s_l2")
                s_ob = Sem(ctx, nc, "s_ob")
                s_out = PingSem(ctx, nc, "s_out", 2)
                s_wbl = Sem(ctx, nc, "s_wbl")

                def g_wb(g):
                    wb3 = wbranch.rearrange("(c p) m -> p c m", p=128)
                    s_wbl.inc(g.dma_start(wbsb[:, :, :], wb3), 16)
                blk.gpsimd(g_wb)

                def sc_ln1(sc):
                    for tt in range(8):
                        if tt:
                            sc.drain()
                        s_sq1.inc(sc.activation(junk[:, :], s_bf[:, tt, :], AF.Square,
                                                 accum_out=stat4[:, tt:tt + 1]))
                blk.scalar(sc_ln1)

                def v_ln1(v):
                    v.wait_ge(s_ld0.h, s_ld0.val)
                    for tt in range(8):
                        v.wait_ge(s_sq1.h, tt + 1)
                        v.reduce_sum(stat4[:, 8 + tt:8 + tt + 1], s_bf[:, tt, :], AX.X)
                        v.drain()
                        v.tensor_scalar_mul(lnst[:, tt, 0:1],
                                            stat4[:, 8 + tt:8 + tt + 1], 1.0 / DM)
                        v.tensor_scalar_mul(lnst[:, tt, 3:4], stat4[:, tt:tt + 1], 1.0 / DM)
                        v.drain()
                        v.tensor_mul(lnst[:, tt, 2:3], lnst[:, tt, 0:1], lnst[:, tt, 0:1])
                        v.drain()
                        s_vst.inc(v.tensor_sub(lnst[:, tt, 3:4], lnst[:, tt, 3:4],
                                                lnst[:, tt, 2:3]))
                blk.vector(v_ln1)

                def sc_inv1(sc):
                    for tt in range(8):
                        sc.wait_ge(s_vst.h, tt + 1)
                        sc.activation(lnst[:, tt, 1:2], lnst[:, tt, 3:4], AF.Ln,
                                      bias=epss[:, 1:2])
                        sc.drain()
                        s_iv1.inc(sc.activation(lnst[:, tt, 1:2], lnst[:, tt, 1:2],
                                                 AF.Exp, scale=-0.5))
                blk.scalar(sc_inv1)

                def v_h1(v):
                    for tt in range(8):
                        v.wait_ge(s_iv1.h, tt + 1)
                        s_h1.inc(v.tensor_scalar(s_bf[:, tt, :], s_bf[:, tt, :],
                                                  lnst[:, tt, 0:1], lnst[:, tt, 1:2],
                                                  ALU.subtract, ALU.mult))
                blk.vector(v_h1)

                if "h1" in dbg:
                    def dh1(g):
                        g.wait_ge(s_h1.h, 8)
                        d2 = dbg["h1"].rearrange("(n p) f -> p n f", p=128)
                        s_dbg.inc(g.dma_start(d2, s_bf[:, :, :]), 16)
                        g.wait_ge(s_dbg.h, s_dbg.val)
                    blk.gpsimd(dh1)

                def sc_rms(sc):
                    for tt in range(8):
                        sc.wait_ge(s_h1.h, tt + 1)
                        sc.drain()
                        sc.activation(junk[:, :], s_bf[:, tt, :], AF.Square,
                                      accum_out=invt[:, tt, 0:1])
                        sc.drain()
                        sc.activation(invt[:, tt, 1:2], invt[:, tt, 0:1], AF.Ln,
                                      scale=1.0 / DM, bias=epss[:, 2:3])
                        sc.drain()
                        s_rm2.inc(sc.activation(invt[:, tt, 1:2], invt[:, tt, 1:2],
                                                 AF.Exp, scale=-0.5))
                blk.scalar(sc_rms)

                for tt in range(8):
                    def t_h1t(t, tt=tt):
                        t.wait_ge(s_h1.h, tt + 1)
                        for u in range(8):
                            m = tt * 8 + u
                            if m >= 2:
                                t.wait_ge(s_he.h, m - 1)
                            for q in range(4):
                                fc = 4 * u + q
                                hi = t.matmul(phs[m % 2][:, 128 * q:128 * q + 128],
                                              s_bf[:, tt, 128 * fc:128 * fc + 128],
                                              identB[:, :], is_transpose=True,
                                              start=(q == 0), stop=(q == 3))
                            s_ht.inc(hi)
                    blk.tensor(t_h1t)

                    def sc_h1e(sc, tt=tt):
                        if tt >= 1:
                            sc.wait_ge(s_lg.h, tt)
                        for u in range(8):
                            m = tt * 8 + u
                            sc.wait_ge(s_ht.h, m + 1)
                            s_he.inc(sc.activation(
                                h1f[:, 4 * u:4 * u + 4, :],
                                phs[m % 2][:, :].rearrange("p (a b) -> p a b", b=128),
                                AF.Copy))
                    blk.scalar(sc_h1e)

                    def t_lg(t, tt=tt):
                        if tt == 0:
                            t.wait_ge(s_ldg.h, 16)
                        t.wait_ge(s_he.h, (tt + 1) * 8)
                        if tt >= 1:
                            t.wait_ge(s_zb.h, tt)
                        for fc in range(32):
                            lgi = t.matmul(plg[:, :], h1f[:, fc, :], wc_sb[:, fc, :],
                                           start=(fc == 0), stop=(fc == 31))
                        s_lg.inc(lgi)
                    blk.tensor(t_lg)

                    def v_z(v, tt=tt):
                        v.wait_ge(s_lg.h, tt + 1)
                        v.wait_ge(s_rm2.h, tt + 1)
                        v.tensor_scalar_mul(invt[:, tt, 0:1], invt[:, tt, 1:2], 10.0)
                        v.drain()
                        v.scalar_tensor_tensor(zbuf[:, tt, 0:16], plg[:, 0:16],
                                               invt[:, tt, 0:1], base10[:, :],
                                               ALU.mult, ALU.add)
                        s_zb.inc(v.scalar_tensor_tensor(
                            zbuf[:, tt, 16:24], plg[:, 16:24],
                            invt[:, tt, 1:2], base_s[:, 16:24],
                            ALU.mult, ALU.add))
                    blk.vector(v_z)

                def v_sm(v):
                    v.wait_ge(s_zb.h, 8)
                    pre = zbuf[:, :, 16:24].rearrange("p a (n f) -> p a n f", f=4)
                    v.tensor_reduce(msk[:, 0:16].rearrange("p (a n) -> p a n", n=2),
                                    pre, AX.X, ALU.max)
                    v.drain()
                    mb = msk[:, 0:16].rearrange("p (a n) -> p a n", n=2).unsqueeze(3) \
                        .broadcast_to([128, 8, 2, 4])
                    s_sk.inc(v.tensor_sub(
                        hpp.ap().rearrange("p a (n f) -> p a n f", f=4), pre, mb))
                blk.vector(v_sm)

                def sc_sm(sc):
                    sc.wait_ge(s_sk.h, 1)
                    s_sk.inc(sc.activation(hpp[:, :, :], hpp[:, :, :], AF.Exp))
                blk.scalar(sc_sm)

                def v_sm2(v):
                    v.wait_ge(s_sk.h, 2)
                    hp4 = hpp.ap().rearrange("p a (n f) -> p a n f", f=4)
                    v.tensor_reduce(msk[:, 0:16].rearrange("p (a n) -> p a n", n=2),
                                    hp4, AX.X, ALU.add)
                    v.drain()
                    v.reciprocal(msk[:, 0:16], msk[:, 0:16])
                    v.drain()
                    mb = msk[:, 0:16].rearrange("p (a n) -> p a n", n=2).unsqueeze(3) \
                        .broadcast_to([128, 8, 2, 4])
                    v.tensor_mul(hp4, hp4, mb)
                    zr = zbuf[:, :, 0:16].rearrange("p a (i j) -> p a i j", j=4)
                    v.drain()
                    v.tensor_reduce(msk[:, 0:32].rearrange("p (a i) -> p a i", i=4),
                                    zr, AX.X, ALU.max)
                    v.drain()
                    mb2 = msk[:, 0:32].rearrange("p (a i) -> p a i", i=4).unsqueeze(3) \
                        .broadcast_to([128, 8, 4, 4])
                    v.tensor_sub(z0.ap().rearrange("p a (i j) -> p a i j", j=4), zr, mb2)
                    v.drain()
                    s_sk.inc(v.tensor_copy(
                        zp.ap().rearrange("p a (j i) -> p a j i", i=4),
                        z0.ap().rearrange("p a (i j) -> p a i j", j=4)
                        .transpose([0, 1, 3, 2])))
                blk.vector(v_sm2)

                for tt in range(8):
                    def v_xp(v, tt=tt):
                        v.wait_ge(s_sk.h, 3)
                        if tt >= 1:
                            v.wait_ge(s_xt.h, 2 * tt)
                        hp = hpp[:, tt, 0:4]
                        v.tensor_scalar_mul(xpf[:, :], s_bf[:, tt, 0:1024], hp[:, 0:1])
                        for j in (1, 2):
                            v.drain()
                            v.scalar_tensor_tensor(xpf[:, :],
                                                   s_bf[:, tt, 1024 * j:1024 * j + 1024],
                                                   hp[:, j:j + 1], xpf[:, :],
                                                   ALU.mult, ALU.add)
                        v.drain()
                        s_xp.inc(v.scalar_tensor_tensor(
                            xp[:, :], s_bf[:, tt, 3072:4096],
                            hp[:, 3:4], xpf[:, :], ALU.mult, ALU.add))
                    blk.vector(v_xp)

                    def t_xpt(t, tt=tt):
                        t.wait_ge(s_xp.h, tt + 1)
                        for u in range(2):
                            m = 2 * tt + u
                            if m >= 2:
                                t.wait_ge(s_xe.h, m - 1)
                            for q in range(4):
                                fc = 4 * u + q
                                xi = t.matmul(phs[m % 2][:, 128 * q:128 * q + 128],
                                              xp[:, 128 * fc:128 * fc + 128],
                                              identB[:, :], is_transpose=True,
                                              start=(q == 0), stop=(q == 3))
                            s_xt.inc(xi)
                    blk.tensor(t_xpt)

                    def sc_xpe(sc, tt=tt):
                        if tt >= 1:
                            sc.wait_ge(s_yb.h, 2 * tt)
                        for u in range(2):
                            m = 2 * tt + u
                            sc.wait_ge(s_xt.h, m + 1)
                            s_xe.inc(sc.activation(
                                xpfm[:, 4 * u:4 * u + 4, :],
                                phs[m % 2][:, :].rearrange("p (a b) -> p a b", b=128),
                                AF.Copy))
                    blk.scalar(sc_xpe)

                    def t_y(t, tt=tt):
                        t.wait_ge(s_xe.h, 2 * (tt + 1))
                        if tt == 0:
                            t.wait_ge(s_wbl.h, 16)
                        for half in range(2):
                            k = 2 * tt + half
                            if k >= 2:
                                t.wait_ge(s_yv.h, k - 1)
                            for cc in range(8):
                                ybi = t.matmul(pybs[k % 2][:, :], xpfm[:, cc, :],
                                               wbsb[:, cc, 512 * half:512 * half + 512],
                                               start=(cc == 0), stop=(cc == 7))
                            s_yb.inc(ybi)
                    blk.tensor(t_y)

                    def v_y(v, tt=tt):
                        for half in range(2):
                            k = 2 * tt + half
                            v.wait_ge(s_yb.h, k + 1)
                            s_yv.inc(v.tensor_copy(
                                ybf[:, tt, 512 * half:512 * half + 512],
                                pybs[k % 2][:, :]))
                    blk.vector(v_y)

                z04 = z0.ap().rearrange("p a (i j) -> p a i j", j=4)
                zp4 = zp.ap().rearrange("p a (j i) -> p a j i", i=4)
                t4 = tsk.ap().rearrange("p (a i j) -> p a i j", i=4, j=4)
                e4 = esk.ap().rearrange("p (a i j) -> p a i j", i=4, j=4)
                m4 = msk.ap().rearrange("p (a i) -> p a i", i=4)
                ul4 = ulse.ap().rearrange("p (a i) -> p a i", i=4)
                vl4 = vlse.ap().rearrange("p (a j) -> p a j", j=4)
                skc = [3]

                def v_sink(v):
                    c = skc[0]
                    for it in range(50):
                        v.wait_ge(s_sk.h, c)
                        if it == 0:
                            v.tensor_copy(tsk[:, :], z0.ap().rearrange("p a f -> p (a f)"))
                        else:
                            vb = vl4.unsqueeze(2).broadcast_to([128, 8, 4, 4])
                            v.tensor_sub(t4, z04, vb)
                        v.drain()
                        v.tensor_reduce(m4, t4, AX.X, ALU.max)
                        v.drain()
                        mb = m4.unsqueeze(3).broadcast_to([128, 8, 4, 4])
                        s_sk.inc(v.tensor_sub(t4, t4, mb))
                        c += 2
                        v.wait_ge(s_sk.h, c)
                        s_sk.inc(v.tensor_reduce(ul4, e4, AX.X, ALU.add))
                        c += 2
                        v.wait_ge(s_sk.h, c)
                        v.tensor_add(ulse[:, :], ulse[:, :], msk[:, :])
                        v.drain()
                        ub = ul4.unsqueeze(2).broadcast_to([128, 8, 4, 4])
                        v.tensor_sub(t4, zp4, ub)
                        v.drain()
                        v.tensor_reduce(m4, t4, AX.X, ALU.max)
                        v.drain()
                        mb = m4.unsqueeze(3).broadcast_to([128, 8, 4, 4])
                        s_sk.inc(v.tensor_sub(t4, t4, mb))
                        c += 2
                        v.wait_ge(s_sk.h, c)
                        s_sk.inc(v.tensor_reduce(vl4, e4, AX.X, ALU.add))
                        c += 2
                        v.wait_ge(s_sk.h, c)
                        s_sk.inc(v.tensor_add(vlse[:, :], vlse[:, :], msk[:, :]))
                        c += 1
                    v.wait_ge(s_sk.h, c)
                    ub = ul4.unsqueeze(3).broadcast_to([128, 8, 4, 4])
                    vb = vl4.unsqueeze(2).broadcast_to([128, 8, 4, 4])
                    hr4 = hres.ap().rearrange("p a (i j) -> p a i j", j=4)
                    v.tensor_sub(hr4, z04, ub)
                    v.drain()
                    s_sk.inc(v.tensor_sub(hr4, hr4, vb))
                    c += 2
                    v.wait_ge(s_sk.h, c)
                    eb = eye_s[:, :].unsqueeze(1).broadcast_to([128, 8, 16])
                    v.tensor_scalar_mul(hres[:, :, :], hres[:, :, :], gsig[:, 0:1])
                    v.drain()
                    s_sk.inc(v.scalar_tensor_tensor(hres[:, :, :], eb, gsig[:, 1:2],
                                                    hres[:, :, :], ALU.mult, ALU.add))
                    c += 1
                    skc[0] = c
                blk.vector(v_sink)

                def s_sink(sc):
                    c = 3
                    for it in range(50):
                        c += 1
                        sc.wait_ge(s_sk.h, c)
                        s_sk.inc(sc.activation(esk[:, :], tsk[:, :], AF.Exp))
                        c += 1
                        c += 1
                        sc.wait_ge(s_sk.h, c)
                        s_sk.inc(sc.activation(ulse[:, :], ulse[:, :], AF.Ln))
                        c += 1
                        c += 1
                        sc.wait_ge(s_sk.h, c)
                        s_sk.inc(sc.activation(esk[:, :], tsk[:, :], AF.Exp))
                        c += 1
                        c += 1
                        sc.wait_ge(s_sk.h, c)
                        s_sk.inc(sc.activation(vlse[:, :], vlse[:, :], AF.Ln))
                        c += 1
                        c += 1
                    c += 1
                    sc.wait_ge(s_sk.h, c)
                    s_sk.inc(sc.activation(hres[:, :, :], hres[:, :, :], AF.Exp))
                    c += 1
                blk.scalar(s_sink)
                SK_DONE = s_sk.val

                if "zb" in dbg:
                    def dzb(g):
                        g.wait_ge(s_zb.h, 8)
                        d2 = dbg["zb"].rearrange("(n p) f -> p n f", p=128)
                        s_dbg.inc(g.dma_start(d2, zbuf[:, :, :]), 16)
                        g.wait_ge(s_dbg.h, s_dbg.val)
                    blk.gpsimd(dzb)
                if "hres" in dbg:
                    def dhr(g):
                        g.wait_ge(s_sk.h, SK_DONE)
                        d2 = dbg["hres"].rearrange("(n p) f -> p n f", p=128)
                        s_dbg.inc(g.dma_start(d2, hres[:, :, :]), 16)
                        g.wait_ge(s_dbg.h, s_dbg.val)
                    blk.gpsimd(dhr)

                for tt in range(8):
                    def v_mix(v, tt=tt):
                        if tt:
                            v.drain()
                        v.wait_ge(s_sk.h, SK_DONE)
                        v.wait_ge(s_yv.h, 2 * (tt + 1))
                        if tt >= 1:
                            v.wait_ge(s_l2.h, 3 * (tt - 1) + 1)
                        hr = hres[:, tt, :]
                        for i in range(4):
                            v.tensor_scalar_mul(s2[:, i, :], s_bf[:, tt, 0:1024],
                                                hr[:, 4 * i:4 * i + 1])
                        for j in (1, 2, 3):
                            for i in range(4):
                                v.scalar_tensor_tensor(
                                    s2[:, i, :], s_bf[:, tt, 1024 * j:1024 * j + 1024],
                                    hr[:, 4 * i + j:4 * i + j + 1],
                                    s2[:, i, :], ALU.mult, ALU.add)
                        v.drain()
                        for i in range(3):
                            v.scalar_tensor_tensor(s2[:, i, :], ybf[:, tt, :],
                                                   hpp[:, tt, 4 + i:4 + i + 1],
                                                   s2[:, i, :], ALU.mult, ALU.add)
                        v.drain()
                        s2i = v.scalar_tensor_tensor(s2[:, 3, :], ybf[:, tt, :],
                                                     hpp[:, tt, 7:8],
                                                     s2[:, 3, :], ALU.mult, ALU.add)
                        s_s2.inc(s2i)
                    blk.vector(v_mix)

                    def sc_ln2(sc, tt=tt):
                        sc.wait_ge(s_s2.h, tt + 1)
                        sc.drain()
                        s_l2.inc(sc.activation(
                            junk[:, :], s2.ap().rearrange("p a f -> p (a f)"),
                            AF.Square, accum_out=stat4[:, 16 + tt:16 + tt + 1]))
                    blk.scalar(sc_ln2)

                    def v_ln2(v, tt=tt):
                        v.wait_ge(s_l2.h, 3 * tt + 1)
                        s2f = s2.ap().rearrange("p a f -> p (a f)")
                        v.reduce_sum(stat4[:, 24 + tt:24 + tt + 1], s2f, AX.X)
                        v.drain()
                        v.tensor_scalar_mul(lnst[:, tt, 0:1],
                                            stat4[:, 24 + tt:24 + tt + 1], 1.0 / DM)
                        v.tensor_scalar_mul(lnst[:, tt, 3:4],
                                            stat4[:, 16 + tt:16 + tt + 1], 1.0 / DM)
                        v.drain()
                        v.tensor_mul(lnst[:, tt, 2:3], lnst[:, tt, 0:1], lnst[:, tt, 0:1])
                        v.drain()
                        s_l2.inc(v.tensor_sub(lnst[:, tt, 3:4], lnst[:, tt, 3:4],
                                              lnst[:, tt, 2:3]))
                    blk.vector(v_ln2)

                    def sc_inv2(sc, tt=tt):
                        sc.wait_ge(s_l2.h, 3 * tt + 2)
                        sc.activation(lnst[:, tt, 1:2], lnst[:, tt, 3:4], AF.Ln,
                                      bias=epss[:, 1:2])
                        sc.drain()
                        s_l2.inc(sc.activation(lnst[:, tt, 1:2], lnst[:, tt, 1:2],
                                                AF.Exp, scale=-0.5))
                    blk.scalar(sc_inv2)

                    def v_out(v, tt=tt):
                        v.wait_ge(s_l2.h, 3 * tt + 3)
                        if tt >= 2:
                            s_out.wait(v, tt - 2)
                        s_ob.inc(v.tensor_scalar(outb[:, tt % 2, :],
                                                  s2.ap().rearrange("p a f -> p (a f)"),
                                                  lnst[:, tt, 0:1], lnst[:, tt, 1:2],
                                                  ALU.subtract, ALU.mult))
                    blk.vector(v_out)

                    def s_st_out(s, tt=tt):
                        s.wait_ge(s_ob.h, tt + 1)
                        s_out.inc(s.dma_start(out[128 * tt:128 * tt + 128, :],
                                              outb[:, tt % 2, :]), tt)
                    blk.sync(s_st_out)

                def s_fin_wait(s):
                    s.wait_ge(s_out.s[0], 16 * 4)
                    s.wait_ge(s_out.s[1], 16 * 4)
                blk.sync(s_fin_wait)

    return nc


def _prep_inputs(L0_attn_x, L0_attn_Wq, L0_attn_Wk, L0_attn_Wv, L0_attn_Wo,
                 L0_attn_q_norm_w, L0_attn_k_norm_w, L0_ffn_W_res, L0_ffn_W_pre,
                 L0_ffn_W_post, L0_ffn_H_res_base, L0_ffn_H_pre_base,
                 L0_ffn_H_post_base, L0_ffn_gate, L0_ffn_W_branch):
    f32 = lambda a: np.ascontiguousarray(np.asarray(a), dtype=np.float32)
    x = f32(L0_attn_x)
    WqT = f32(L0_attn_Wq).T
    WkT = f32(L0_attn_Wk).T
    WvT = f32(L0_attn_Wv).T
    WoT = np.ascontiguousarray(f32(L0_attn_Wo).T)
    wcat = np.concatenate([f32(L0_ffn_W_res), f32(L0_ffn_W_pre), f32(L0_ffn_W_post)],
                          axis=0)
    wcatT = np.ascontiguousarray(wcat.T)
    basetile = np.ascontiguousarray(np.tile(np.concatenate(
        [f32(L0_ffn_H_res_base).ravel(), f32(L0_ffn_H_pre_base).ravel(),
         f32(L0_ffn_H_post_base).ravel()])[None, :], (128, 1)), dtype=np.float32)
    gatet = np.ascontiguousarray(np.tile(f32(L0_ffn_gate).reshape(1, 1), (128, 1)),
                                 dtype=np.float32)
    wbr = f32(L0_ffn_W_branch)
    eye = np.ascontiguousarray(np.tile(np.eye(4, dtype=np.float32).reshape(1, 16),
                                       (128, 1)), dtype=np.float32)
    slopes_all = _alibi_slopes(H)
    wqn = np.ascontiguousarray(f32(L0_attn_q_norm_w).reshape(HD, 1))
    wkn = np.ascontiguousarray(f32(L0_attn_k_norm_w).reshape(HD, 1))

    xTs = [np.ascontiguousarray(x[b].T) for b in range(B_)]
    in_maps = []
    for c in range(8):
        b, j = divmod(c, 4)
        ridx = (512 * (np.arange(TQ) // 128) + 128 * j
                + np.arange(TQ) % 128)
        wq_sl = WqT[:, 1024 * j:1024 * j + 1024]
        wk_sl = WkT[:, 256 * j:256 * j + 256]
        wv_sl = WvT[:, 256 * j:256 * j + 256]
        in_maps.append({
            "xT": xTs[b],
            "xrows": np.ascontiguousarray(x[b, ridx, :]),
            "wqkvT": np.ascontiguousarray(np.concatenate([wq_sl, wk_sl, wv_sl], axis=1)),
            "wqn": wqn, "wkn": wkn,
            "slopes": np.ascontiguousarray(
                np.tile(slopes_all[8 * j:8 * j + 8][None, :], (128, 1))),
            "woT": WoT,
            "wcatT": wcatT,
            "basetile": basetile,
            "gatet": gatet,
            "wbranch": wbr,
            "eye16": eye,
            "rowsel": np.ascontiguousarray(
                (512 * np.arange(4, dtype=np.uint32)[None, :, None]
                 + 128 * j + np.arange(128, dtype=np.uint32)[:, None, None]
                 + 2048 * (np.arange(32, dtype=np.uint32)[None, None, :] % 8))),
        })
    return in_maps


_CACHED = {}


def kernel(**inputs):
    in_maps = _prep_inputs(**inputs)
    if "full" not in _CACHED:
        _CACHED["full"] = build()
    nc = _CACHED["full"]
    res = run_bass_kernel_spmd(nc, in_maps, core_ids=list(range(8)))
    outp = np.empty((B_, S, DM), dtype=np.float32)
    for c in range(8):
        b, j = divmod(c, 4)
        ridx = (512 * (np.arange(TQ) // 128) + 128 * j
                + np.arange(TQ) % 128)
        outp[b, ridx, :] = res.results[c]["out"]
    return outp
